# version 17
# baseline (speedup 1.0000x reference)
import math
from contextlib import ExitStack

import numpy as np

import concourse.bass as bass
import concourse.mybir as mybir
from concourse.bass_utils import run_bass_kernel_spmd

F32 = mybir.dt.float32
BF16 = mybir.dt.bfloat16
FP16 = mybir.dt.float16
FP8 = mybir.dt.float8e4
I32 = mybir.dt.int32
AF = mybir.ActivationFunctionType
ALU = mybir.AluOpType
AX = mybir.AxisListType

S = 2048
D = 1024
NH = 16
TQ = 512
NQT = S // TQ
E = 8
H1 = 4096

ENGINES = ("sync", "gpsimd", "vector", "scalar", "tensor")


class Ctr:
    __slots__ = ("sem", "n", "name")

    def __init__(self, sem, name):
        self.sem = sem
        self.n = 0
        self.name = name


class Sched:
    def __init__(self):
        self.items = {e: [] for e in ENGINES}

    def op(self, eng, fn, waits=(), inc=None, amt=1):
        w = [(c, v) for (c, v) in waits if c is not None and v > 0]
        self.items[eng].append((tuple(w), fn, inc, amt))
        if inc is not None:
            inc.n += amt
            return inc.n
        return None

    def emit(self, block):
        for ename in ENGINES:
            items = self.items[ename]

            def mk(items=items):
                waited = {}

                def body(eng):
                    for waits, fn, inc, amt in items:
                        for c, v in waits:
                            if waited.get(c, 0) >= v:
                                continue
                            eng.wait_ge(c.sem, v)
                            waited[c] = v
                        inst = fn(eng)
                        if inc is not None:
                            inst.then_inc(inc.sem, amt)
                return body

            getattr(block, ename)(mk())


class Guard:

    def __init__(self, init=None):
        self.w = init
        self.readers = []

    def for_write(self):
        w = list(self.readers)
        if self.w is not None:
            w.append(self.w)
        return w

    def wrote(self, m):
        self.w = m
        self.readers = []

    def for_read(self):
        return [self.w] if self.w is not None else []

    def read(self, m):
        self.readers.append(m)


def build_program(stop_after=None, probe_attn=False):
    nc = bass.Bass("TRN2", target_bir_lowering=False, detect_race_conditions=False)
    dt = {}

    def din(name, shape):
        dt[name] = nc.dram_tensor(name, list(shape), F32, kind="ExternalInput").ap()
        return dt[name]

    x_in = din("x_t", [128, 8, S])
    din("wq0", [8, 128, 1024]); din("wk0", [8, 128, 1024]); din("wv0", [128, 8192])
    din("wo0t", [8, 64, 2048])
    din("bq0", [D]); din("bk0", [D]); din("bv0", [D]); din("bo0", [D])
    din("sq0", [1, 1]); din("sk0", [1, 1]); din("sv0", [1, 1]); din("so0", [1, 1])
    din("rw0", [128, 64])
    din("w1m", [128, 128, 1024])
    din("w2m", [E, 8, 128, 2048])
    din("wq1", [8, 128, 1024]); din("wk1", [8, 128, 1024]); din("wv1", [128, 8192])
    din("wo1t", [8, 64, 2048])
    din("w1f", [32, 128, 1024]); din("w2f", [8, 128, 4096])
    din("b1f", [H1]); din("b2f", [D])

    x_out = nc.dram_tensor("xout", [128, 8, S], F32, kind="ExternalOutput").ap()

    es = ExitStack()

    def sbuf(name, shape, dtype):
        return es.enter_context(nc.sbuf_tensor(name, list(shape), dtype))

    def psmk(name, shape):
        return es.enter_context(nc.psum_tensor(name, list(shape), F32))

    xn = sbuf("xn", [128, 8 * S], BF16)
    ar0 = sbuf("ar0", [128, 16384], BF16)
    ar1 = sbuf("ar1", [128, 16640], BF16)
    aat = sbuf("aat", [128, 16384], BF16)
    xw = sbuf("xw", [128, 4096], F32)
    uni = sbuf("uni", [128, 6144], BF16)
    wsa = sbuf("wsa", [128, 2048], BF16)
    wsb = sbuf("wsb", [128, 4096], BF16)
    scf = sbuf("scf", [128, 2048], F32)
    tbf = sbuf("tbf", [128, 1024], BF16)
    gts = sbuf("gts", [128, 512], F32)
    sbias = sbuf("sbias", [128, 64], F32)
    b1s = sbuf("b1s", [128, 32], F32)
    cbf = sbuf("cbf", [128, 256], BF16)
    cf = sbuf("cf", [128, 64], F32)
    onesf = sbuf("onesf", [128, 128], F32)
    ident = sbuf("ident", [128, 128], F32)
    pswp = sbuf("pswp", [128, 128], BF16)
    rwsb = sbuf("rwsb", [128, 64], F32)
    gcopy = sbuf("gcopy", [128, 2048], F32)
    itr = sbuf("itr", [128, 2048], F32)

    psA = psmk("psA", [128, 1024])
    psB = psmk("psB", [128, 1024])
    psC = psmk("psC", [128, 1024])
    psD = psmk("psD", [128, 512])
    psE = psmk("psE", [128, 512])

    aat_f32 = aat[:].bitcast(F32)
    uni_f32 = uni[:].bitcast(F32)
    scf_bf = scf[:].bitcast(BF16)
    scf_i32 = scf[:].bitcast(I32)
    scf_f8 = scf[:].bitcast(FP8)
    scf_f16 = scf[:].bitcast(FP16)

    CFC = dict(iota=0, theta=1, sign=2, eps=3, cbias=4, kmax=5, qmax=6, c2=7,
               sq=8, sk=9, sv=10, so=11, t0=12, t1=13, eps5=15, npi=16)
    SBC = dict(bq0=0, bk0=8, bv0=16, bo0=24, b2f=32, bqf=40)

    def mkctr(name):
        h = es.enter_context(nc.semaphore(name))
        return Ctr(h, name)

    sT = mkctr("sT"); sV = mkctr("sV"); sS = mkctr("sS"); sG = mkctr("sG")
    sLD = mkctr("sLD"); sXW = mkctr("sXW"); sST = mkctr("sST")
    sWA = [mkctr("sWA0"), mkctr("sWA1")]
    sWB = [mkctr("sWB0"), mkctr("sWB1")]
    sQW = mkctr("sQW")
    sFL = mkctr("sFL")

    sch = Sched()
    NONE = (sT, 0)

    def dma_s(out, in_, waits=(), ctr=None):
        c = ctr or sLD
        return (c, sch.op("sync", lambda e, o=out, i=in_: e.dma_start(out=o, in_=i),
                          waits=waits, inc=c, amt=16))

    def dma_g(out, in_, waits=(), ctr=None):
        c = ctr or sQW
        return (c, sch.op("gpsimd", lambda e, o=out, i=in_: e.dma_start(out=o, in_=i),
                          waits=waits, inc=c, amt=16))

    def top(fn, waits=()):
        return (sT, sch.op("tensor", fn, waits=waits, inc=sT))

    def vop(fn, waits=()):
        return (sV, sch.op("vector", fn, waits=waits, inc=sV))

    def sop(fn, waits=()):
        return (sS, sch.op("scalar", fn, waits=waits, inc=sS))

    def gop(fn, waits=()):
        return (sG, sch.op("gpsimd", fn, waits=waits, inc=sG))

    w_iota1 = gop(lambda e: e.iota(cf[:, 0:1], pattern=[[0, 1]], base=0,
                                   channel_multiplier=1,
                                   allow_small_or_imprecise_dtypes=True))
    w_iota2 = gop(lambda e: e.iota(scf[:, 0:512], pattern=[[-1, 512]], base=0,
                                   channel_multiplier=1,
                                   allow_small_or_imprecise_dtypes=True))
    w_itr = gop(lambda e: e.iota(itr[0:1, 0:2048], pattern=[[1, 2048]], base=0,
                                 channel_multiplier=0,
                                 allow_small_or_imprecise_dtypes=True),
                waits=[w_iota2])
    w_masks0 = NONE
    for off in range(4):
        w_masks0 = vop(lambda e, off=off: e.tensor_single_scalar(
            out=uni[:, 4096 + off * 512: 4096 + (off + 1) * 512],
            in_=scf[:, 0:512], scalar=-128.0 * off, op=ALU.is_le),
            waits=[w_iota2, w_masks0])

    vop(lambda e: e.memset(cbf[:, 0:1], 1.0))
    vop(lambda e: e.memset(cbf[:, 3:131], 1.0))
    vop(lambda e: e.memset(onesf[:], 1.0))
    vop(lambda e: e.memset(cf[:, CFC["eps"]:CFC["eps"] + 1], 1e-30))
    vop(lambda e: e.memset(cf[:, CFC["eps5"]:CFC["eps5"] + 1], 1e-5))
    vop(lambda e: e.memset(cf[:, CFC["npi"]:CFC["npi"] + 1], -math.pi))
    w_id = vop(lambda e: e.tensor_single_scalar(out=ident[:], in_=scf[:, 0:128],
                                                scalar=0.0, op=ALU.is_equal),
               waits=[w_iota2])
    P1 = uni_f32[:, 0:128]
    P2 = uni_f32[:, 128:256]
    selc = cf[:, CFC["t0"]:CFC["t0"] + 1]
    selc2 = cf[:, CFC["t1"]:CFC["t1"] + 1]
    wl = vop(lambda e: e.tensor_single_scalar(out=P1, in_=scf[:, 0:128], scalar=-32.0,
                                              op=ALU.is_equal), waits=[w_iota2])
    wl = vop(lambda e: e.tensor_single_scalar(out=P2, in_=scf[:, 0:128], scalar=32.0,
                                              op=ALU.is_equal), waits=[wl])
    wl = vop(lambda e: e.tensor_single_scalar(out=selc2, in_=cf[:, 0:1], scalar=64.0,
                                              op=ALU.is_ge), waits=[w_iota1, wl])
    wl = vop(lambda e: e.scalar_tensor_tensor(out=selc, in0=selc2, scalar=-64.0,
                                              in1=cf[:, 0:1], op0=ALU.mult,
                                              op1=ALU.add), waits=[wl])
    w_pm64 = wl = vop(lambda e: e.tensor_copy(out=cf[:, CFC["t1"] + 1:CFC["t1"] + 2],
                                              in_=selc), waits=[wl])
    wl = vop(lambda e: e.tensor_single_scalar(out=selc, in_=selc, scalar=32.0,
                                              op=ALU.is_lt), waits=[wl])
    wl = vop(lambda e: e.tensor_scalar(out=P1, in0=P1, scalar1=selc, scalar2=None,
                                       op0=ALU.mult), waits=[wl])
    wl = vop(lambda e: e.tensor_scalar(out=selc2, in0=selc, scalar1=-1.0, scalar2=1.0,
                                       op0=ALU.mult, op1=ALU.add), waits=[wl])
    wl = vop(lambda e: e.tensor_scalar(out=P2, in0=P2, scalar1=selc2, scalar2=None,
                                       op0=ALU.mult), waits=[wl])
    wl = vop(lambda e: e.tensor_tensor(out=P1, in0=P1, in1=P2, op=ALU.add), waits=[wl])
    w_pswap = wl = vop(lambda e: e.tensor_copy(out=pswp[:], in_=P1), waits=[wl])
    thc = cf[:, CFC["theta"]:CFC["theta"] + 1]
    sgc = cf[:, CFC["sign"]:CFC["sign"] + 1]
    pm64c = cf[:, CFC["t1"] + 1:CFC["t1"] + 2]
    wl = vop(lambda e: e.tensor_single_scalar(out=thc, in_=pm64c, scalar=32.0,
                                              op=ALU.is_ge), waits=[w_pm64, wl])
    wl = vop(lambda e: e.scalar_tensor_tensor(out=thc, in0=thc, scalar=-32.0,
                                              in1=pm64c, op0=ALU.mult, op1=ALU.add),
             waits=[wl])
    w_thv = wl = vop(lambda e: e.tensor_single_scalar(
        out=thc, in_=thc, scalar=-math.log(10000.0) / 32.0, op=ALU.mult), waits=[wl])
    w_thc = sop(lambda e: e.activation(out=thc, in_=thc, func=AF.Exp), waits=[w_thv])
    wl = vop(lambda e: e.tensor_single_scalar(out=sgc, in_=pm64c, scalar=32.0,
                                              op=ALU.is_ge),
             waits=[w_pm64, wl, w_thc])
    w_sgc = vop(lambda e: e.tensor_scalar(out=sgc, in0=sgc, scalar1=2.0, scalar2=-1.0,
                                          op0=ALU.mult, op1=ALU.add), waits=[wl])
    vop(lambda e: e.memset(cbf[0:64, 1:2], 1.0))
    vop(lambda e: e.memset(cbf[64:128, 1:2], 0.0))
    vop(lambda e: e.memset(cbf[0:64, 2:3], 0.0))
    w_cbf = vop(lambda e: e.memset(cbf[64:128, 2:3], 1.0))

    for nm, col in (("bq0", SBC["bq0"]), ("bk0", SBC["bk0"]), ("bv0", SBC["bv0"]),
                    ("bo0", SBC["bo0"]), ("b2f", SBC["b2f"])):
        bh = dt[nm].rearrange("(c p) -> c p", p=128)
        for c in range(8):
            dma_s(sbias[:, col + c:col + c + 1], bh[c:c + 1, :])
    b1h = dt["b1f"].rearrange("(c p) -> c p", p=128)
    for c in range(32):
        dma_s(b1s[:, c:c + 1], b1h[c:c + 1, :])
    for nm, key in (("sq0", "sq"), ("sk0", "sk"), ("sv0", "sv"), ("so0", "so")):
        dma_s(cf[0:1, CFC[key]:CFC[key] + 1], dt[nm])
    w_rw = dma_s(rwsb[:], dt["rw0"])
    w_params = (sLD, sLD.n)

    w_onesf = (sV, sV.n)
    prev_read = NONE
    w_scol = {}
    for key in ("sq", "sk", "sv", "so"):
        wt = top(lambda e, k=key: e.matmul(psD[:, 0:1], onesf[0:1, :],
                                           cf[0:1, CFC[k]:CFC[k] + 1],
                                           start=True, stop=True),
                 waits=[w_params, w_onesf, prev_read])
        wsc = sop(lambda e, k=key: e.copy(out=cf[:, CFC[k]:CFC[k] + 1],
                                          in_=psD[:, 0:1]), waits=[wt])
        w_scol[key] = wsc
        prev_read = wsc
    w_scol_done = prev_read

    w_b1 = vop(lambda e: e.tensor_copy(out=scf_f16[:, 2048:2056], in_=sbias[:, 32:40]),
               waits=[w_params])
    w_bqf = vop(lambda e: e.tensor_copy(out=sbias[:, 40:48], in_=scf_f16[:, 2048:2056]),
                waits=[w_b1])

    wa_free = [NONE, NONE]
    wb_free = [NONE, NONE]

    def emit_ln(layer, deps, psd_free, deps_tiles=None):
        xsrc = x_in if layer == 0 else x_out
        state = dict(last_xn=NONE, w_le=NONE, w_rmm_last=NONE,
                     w_mmm=NONE, w_vmm=NONE)
        arwin = ar0[:, 0:8192].bitcast(F32)
        bc_free = {0: psd_free, 1: psd_free}
        win_free = {0: NONE, 1: NONE}
        if deps_tiles is None:
            deps_tiles = [list(deps)] * 4
        p2 = {}

        def phase1(tt):
            t0 = tt * 512
            win = xw[:] if tt % 2 == 0 else arwin
            bc = psA if tt % 2 == 0 else psC
            wr = win.rearrange("p (dc t) -> p dc t", t=512)
            w_lo = dma_s(wr[:, 0:4, :], xsrc[:, 0:4, t0:t0 + 512],
                         waits=list(deps_tiles[tt]) + [win_free[tt % 2]],
                         ctr=sXW)
            w_hi = dma_s(wr[:, 4:8, :], xsrc[:, 4:8, t0:t0 + 512],
                         waits=list(deps_tiles[tt]) + [win_free[tt % 2]],
                         ctr=sLD)
            for dc in range(8):
                w_ld = w_lo if dc < 4 else w_hi
                state["w_mmm"] = top(lambda e, dc=dc, win=win: e.matmul(
                    psD[0:1, :], onesf[:, 0:1], win[:, dc * 512:(dc + 1) * 512],
                    start=(dc == 0), stop=(dc == 7)),
                    waits=[w_ld, psd_free, w_onesf])
                w_sq = sop(lambda e, dc=dc, win=win: e.activation(
                    out=tbf[:, 512:1024], in_=win[:, dc * 512:(dc + 1) * 512],
                    func=AF.Square), waits=[w_ld, state["w_vmm"]])
                state["w_vmm"] = top(lambda e, dc=dc: e.matmul(
                    psE[0:1, :], cbf[:, 0:1], tbf[:, 512:1024],
                    start=(dc == 0), stop=(dc == 7)),
                    waits=[w_sq, psd_free, w_cbf])
            m_row = scf[0:1, 0:512]
            v_row = scf[0:1, 512:1024]
            mm_row = scf[0:1, 1024:1536]
            w_m = vop(lambda e: e.tensor_single_scalar(out=m_row, in_=psD[0:1, :],
                                                       scalar=1.0 / D, op=ALU.mult),
                      waits=[state["w_mmm"]])
            w_v1 = vop(lambda e: e.tensor_single_scalar(out=v_row, in_=psE[0:1, :],
                                                        scalar=1.0 / D, op=ALU.mult),
                       waits=[state["w_vmm"]])
            w_v2 = vop(lambda e: e.tensor_tensor(out=mm_row, in0=m_row, in1=m_row,
                                                 op=ALU.mult), waits=[w_m])
            w_v3 = vop(lambda e: e.tensor_tensor(out=v_row, in0=v_row, in1=mm_row,
                                                 op=ALU.subtract), waits=[w_v1, w_v2])
            w_v4 = sop(lambda e: e.activation(out=v_row, in_=v_row, func=AF.Ln,
                                              bias=cf[0:1, CFC["eps5"]:CFC["eps5"] + 1]),
                       waits=[w_v3])
            w_v5 = sop(lambda e: e.activation(out=v_row, in_=v_row, func=AF.Exp,
                                              scale=-0.5), waits=[w_v4])
            w_bm = top(lambda e, bc=bc: e.matmul(
                bc[:, 0:512], onesf[0:1, :], m_row,
                start=True, stop=True),
                waits=[w_m, w_v5, psd_free, bc_free[tt % 2]])
            w_br = top(lambda e, bc=bc: e.matmul(
                bc[:, 512:1024], onesf[0:1, :], v_row,
                start=True, stop=True),
                waits=[w_v5, psd_free, bc_free[tt % 2]])
            p2[tt] = (t0, win, bc, w_bm, w_br)

        def phase2(tt):
            t0, win, bc, w_bm, w_br = p2[tt]
            w_xnf = []
            for dc in range(8):
                tmpf = scf[:, 1536:2048]
                w_t1 = vop(lambda e, dc=dc, win=win, bc=bc: e.tensor_tensor(
                    out=tmpf, in0=win[:, dc * 512:(dc + 1) * 512], in1=bc[:, 0:512],
                    op=ALU.subtract),
                    waits=[w_bm, w_xnf[-1] if w_xnf else NONE, state["w_le"]])
                w_t2 = vop(lambda e, dc=dc, t0=t0, bc=bc: e.tensor_tensor(
                    out=xn[:, dc * S + t0: dc * S + t0 + 512], in0=tmpf,
                    in1=bc[:, 512:1024], op=ALU.mult), waits=[w_br, w_t1])
                state["last_xn"] = w_t2
                if layer == 0:
                    w_t3 = vop(lambda e, dc=dc, bc=bc: e.tensor_tensor(
                        out=aat_f32[:, dc * 512:(dc + 1) * 512], in0=tmpf,
                        in1=bc[:, 512:1024], op=ALU.mult), waits=[w_t1])
                    w_xnf.append(w_t3)
                    state["last_xn"] = w_t3
                else:
                    w_xnf.append(w_t2)
            bc_free[tt % 2] = state["last_xn"]
            win_free[tt % 2] = state["last_xn"]
            if layer == 0:
                w_rmm = NONE
                for qs in range(4):
                    for dc in range(8):
                        w_rmm = top(lambda e, qs=qs, dc=dc: e.matmul(
                            psB[:, qs * 8:(qs + 1) * 8],
                            aat_f32[:, dc * 512 + qs * 128: dc * 512 + qs * 128 + 128],
                            rwsb[:, dc * 8:(dc + 1) * 8],
                            start=(dc == 0), stop=(dc == 7)),
                            waits=[w_xnf[dc], w_rw, state["w_le"], psd_free])
                state["w_le"] = sop(lambda e, tt=tt: e.copy(
                    out=gts[:, tt * 32:(tt + 1) * 32],
                    in_=psB[:, 0:32]), waits=[w_rmm])
                state["w_rmm_last"] = w_rmm

        for tt in range(4):
            phase1(tt)
            if tt > 0:
                phase2(tt - 1)
        phase2(3)
        return state["last_xn"], state["w_le"], state["w_rmm_last"]

    def emit_router(w_logits):
        L = gts[:, 0:128].rearrange("p (c e) -> p c e", e=8)
        m1 = gts[:, 128:144]; m2 = gts[:, 144:160]
        i1 = gts[:, 160:176]; i2 = gts[:, 176:192]
        tmp = gts[:, 192:208]; tmp2 = gts[:, 208:224]
        g1c = gts[:, 224:240]; t3 = gts[:, 240:256]
        wl = vop(lambda e: e.tensor_copy(out=m1, in_=L[:, :, 0]), waits=[w_logits])
        wl = vop(lambda e: e.memset(m2, -1e30), waits=[wl])
        wl = vop(lambda e: e.memset(i1, 0.0), waits=[wl])
        wl = vop(lambda e: e.memset(i2, 0.0), waits=[wl])
        for ei in range(1, 8):
            pe = L[:, :, ei]
            wl = vop(lambda e, pe=pe: e.tensor_tensor(out=tmp, in0=pe, in1=m1,
                                                      op=ALU.is_gt), waits=[wl])
            wl = vop(lambda e, pe=pe: e.tensor_tensor(out=tmp2, in0=pe, in1=m2,
                                                      op=ALU.is_gt), waits=[wl])
            wl = vop(lambda e, ei=ei: e.tensor_scalar(out=t3, in0=i2, scalar1=-1.0,
                                                      scalar2=float(ei), op0=ALU.mult,
                                                      op1=ALU.add), waits=[wl])
            wl = vop(lambda e: e.tensor_tensor(out=t3, in0=t3, in1=tmp2, op=ALU.mult),
                     waits=[wl])
            wl = vop(lambda e: e.tensor_tensor(out=i2, in0=i2, in1=t3, op=ALU.add),
                     waits=[wl])
            wl = vop(lambda e: e.tensor_tensor(out=t3, in0=i1, in1=i2,
                                               op=ALU.subtract), waits=[wl])
            wl = vop(lambda e: e.tensor_tensor(out=t3, in0=t3, in1=tmp, op=ALU.mult),
                     waits=[wl])
            wl = vop(lambda e: e.tensor_tensor(out=i2, in0=i2, in1=t3, op=ALU.add),
                     waits=[wl])
            wl = vop(lambda e, pe=pe: e.tensor_tensor(out=t3, in0=m1, in1=pe,
                                                      op=ALU.min), waits=[wl])
            wl = vop(lambda e: e.tensor_tensor(out=m2, in0=m2, in1=t3, op=ALU.max),
                     waits=[wl])
            wl = vop(lambda e, ei=ei: e.tensor_scalar(out=t3, in0=i1, scalar1=-1.0,
                                                      scalar2=float(ei), op0=ALU.mult,
                                                      op1=ALU.add), waits=[wl])
            wl = vop(lambda e: e.tensor_tensor(out=t3, in0=t3, in1=tmp, op=ALU.mult),
                     waits=[wl])
            wl = vop(lambda e: e.tensor_tensor(out=i1, in0=i1, in1=t3, op=ALU.add),
                     waits=[wl])
            wl = vop(lambda e, pe=pe: e.tensor_tensor(out=m1, in0=m1, in1=pe,
                                                      op=ALU.max), waits=[wl])
        wl = vop(lambda e: e.tensor_tensor(out=tmp, in0=m2, in1=m1, op=ALU.subtract),
                 waits=[wl])
        w_e = sop(lambda e: e.activation(out=tmp, in_=tmp, func=AF.Exp), waits=[wl])
        wl = vop(lambda e: e.tensor_scalar(out=tmp, in0=tmp, scalar1=1.0, scalar2=None,
                                           op0=ALU.add), waits=[w_e, wl])
        wl = vop(lambda e: e.reciprocal(out=g1c, in_=tmp), waits=[wl])
        for ei in range(8):
            ge = gts[:, 256 + ei * 16: 256 + (ei + 1) * 16]
            wl = vop(lambda e, ei=ei: e.tensor_single_scalar(out=tmp2, in_=i1,
                                                             scalar=float(ei),
                                                             op=ALU.is_equal),
                     waits=[wl])
            wl = vop(lambda e, ge=ge: e.tensor_tensor(out=ge, in0=g1c, in1=tmp2,
                                                      op=ALU.mult), waits=[wl])
            wl = vop(lambda e, ei=ei: e.tensor_single_scalar(out=tmp2, in_=i2,
                                                             scalar=float(ei),
                                                             op=ALU.is_equal),
                     waits=[wl])
            wl = vop(lambda e: e.tensor_scalar(out=t3, in0=g1c, scalar1=-1.0,
                                               scalar2=1.0, op0=ALU.mult,
                                               op1=ALU.add), waits=[wl])
            wl = vop(lambda e: e.tensor_tensor(out=tmp2, in0=tmp2, in1=t3,
                                               op=ALU.mult), waits=[wl])
            wl = vop(lambda e, ge=ge: e.tensor_tensor(out=ge, in0=ge, in1=tmp2,
                                                      op=ALU.add), waits=[wl])
        return wl

    def wa_load(slot, src_ap, extra=()):
        out = wsa[:, slot * 1024:(slot + 1) * 1024]
        return dma_g(out, src_ap, waits=[wa_free[slot]] + list(extra), ctr=sWA[slot])

    def emit_kv(layer, w_xn, psfree):
        sfx = str(layer)
        krope_last = NONE
        prev_ev = [psfree, psfree]
        kev = {}
        for oc in range(8):
            slot = oc % 2
            w_w = wa_load(slot, dt["wk" + sfx][oc])
            for tt in range(4):
                half = (oc * 4 + tt) % 2
                pr = psB[:, half * 512:(half + 1) * 512]
                w_mm = NONE
                for dc in range(8):
                    w_mm = top(lambda e, dc=dc, tt=tt, pr=pr, slot=slot: e.matmul(
                        pr, wsa[:, slot * 1024 + dc * 128: slot * 1024 + (dc + 1) * 128],
                        xn[:, dc * S + tt * 512: dc * S + tt * 512 + 512],
                        start=(dc == 0), stop=(dc == 7)),
                        waits=[w_w, w_xn, prev_ev[half]])
                if layer == 0:
                    ev = vop(lambda e, oc=oc, tt=tt, pr=pr: e.tensor_scalar(
                        out=ar0[:, oc * S + tt * 512: oc * S + tt * 512 + 512],
                        in0=pr, scalar1=cf[:, CFC["sk"]:CFC["sk"] + 1],
                        scalar2=sbias[:, SBC["bk0"] + oc:SBC["bk0"] + oc + 1],
                        op0=ALU.mult, op1=ALU.add),
                        waits=[w_mm, w_scol_done, w_params])
                else:
                    ev = sop(lambda e, oc=oc, tt=tt, pr=pr: e.copy(
                        out=ar0[:, oc * S + tt * 512: oc * S + tt * 512 + 512],
                        in_=pr), waits=[w_mm])
                prev_ev[half] = ev
                kev[(oc, tt)] = ev
            wa_free[slot] = w_mm
        klast = kev[(7, 3)]
        if layer == 1:
            for oc in range(8):
                for tt in range(4):
                    ksl = ar0[:, oc * S + tt * 512: oc * S + tt * 512 + 512]
                    w_sw = top(lambda e, ksl=ksl: e.matmul(
                        psC[:, 0:512], pswp[:], ksl, start=True, stop=True),
                        waits=[kev[(oc, tt)], w_pswap, krope_last, psfree])
                    w_r1 = vop(lambda e, ksl=ksl, tt=tt: e.tensor_tensor(
                        out=tbf[:, 0:512], in0=ksl,
                        in1=uni[:, tt * 512:(tt + 1) * 512], op=ALU.mult),
                        waits=[w_cc_g[0], kev[(oc, tt)], krope_last])
                    w_r2 = vop(lambda e, tt=tt: e.tensor_tensor(
                        out=tbf[:, 512:1024], in0=psC[:, 0:512],
                        in1=uni[:, 2048 + tt * 512: 2048 + (tt + 1) * 512],
                        op=ALU.mult), waits=[w_ss_g[0], w_sw])
                    krope_last = vop(lambda e, ksl=ksl: e.tensor_tensor(
                        out=ksl, in0=tbf[:, 0:512], in1=tbf[:, 512:1024],
                        op=ALU.add), waits=[w_r1, w_r2])
            klast = krope_last
        w_k0 = vop(lambda e: e.memset(cf[0:1, CFC["kmax"]:CFC["kmax"] + 1], 0.0))
        wprev = w_k0
        rd_g = [NONE, NONE]
        sq_g = [NONE, NONE]
        i = 0
        for oc in range(8):
            for tt in range(4):
                th = i % 2
                w_sq = sop(lambda e, oc=oc, tt=tt, th=th: e.activation(
                    out=tbf[:, th * 512:(th + 1) * 512],
                    in_=ar0[:, oc * S + tt * 512: oc * S + tt * 512 + 512],
                    func=AF.Square), waits=[klast, sq_g[th]])
                for par in range(2):
                    bank = (i * 2 + par) % 2
                    pr = psD[0:1, :] if bank == 0 else psE[0:1, :]
                    w_mm = top(lambda e, par=par, th=th, pr=pr: e.matmul(
                        pr, cbf[par * 64:(par + 1) * 64, 1 + par:2 + par],
                        tbf[par * 64:(par + 1) * 64, th * 512:(th + 1) * 512],
                        start=True, stop=True), waits=[w_sq, rd_g[bank], psfree])
                    sq_g[th] = w_mm
                    w_rm = vop(lambda e, pr=pr: e.reduce_max(
                        out=cf[0:1, CFC["t0"]:CFC["t0"] + 1], in_=pr,
                        axis=AX.X), waits=[w_mm, wprev])
                    rd_g[bank] = w_rm
                    wprev = vop(lambda e: e.tensor_tensor(
                        out=cf[0:1, CFC["kmax"]:CFC["kmax"] + 1],
                        in0=cf[0:1, CFC["kmax"]:CFC["kmax"] + 1],
                        in1=cf[0:1, CFC["t0"]:CFC["t0"] + 1], op=ALU.max),
                        waits=[w_rm])
                i += 1
        w_kmax = wprev
        w_o65 = vop(lambda e: e.memset(ar1[:, 64:16640:65], 1.0), waits=[psfree])
        w_wv = dma_g(aat[:, 0:8192], dt["wv" + sfx], waits=[psfree])
        w_bv = [NONE, NONE]
        if layer == 0:
            w_bvr = dma_s(scf[0:1, 0:1024],
                          dt["bv0"].rearrange("(o d) -> o d", o=1), waits=[w_xn])
            prev = NONE
            for ot in range(2):
                wt = top(lambda e, ot=ot: e.matmul(
                    psC[:, 0:512], onesf[0:1, :], scf[0:1, ot * 512:(ot + 1) * 512],
                    start=True, stop=True), waits=[w_bvr, prev, psfree])
                wc = sop(lambda e, ot=ot: e.copy(
                    out=scf[:, 1024 + ot * 512: 1024 + (ot + 1) * 512],
                    in_=psC[:, 0:512]), waits=[wt])
                w_bv[ot] = wc
                prev = wc
        prev_ev = [psfree, psfree]
        vev = NONE
        for ot in range(2):
            for tc in range(16):
                half = (ot * 16 + tc) % 2
                pr = psA[:, half * 512:(half + 1) * 512]
                w_mm = NONE
                for dc in range(8):
                    w_mm = top(lambda e, dc=dc, tc=tc, ot=ot, pr=pr: e.matmul(
                        pr, xn[:, dc * S + tc * 128: dc * S + (tc + 1) * 128],
                        aat[:, dc * 1024 + ot * 512: dc * 1024 + ot * 512 + 512],
                        start=(dc == 0), stop=(dc == 7)),
                        waits=[w_wv, w_xn, prev_ev[half]])
                vout = ar1[:, tc * 1040: (tc + 1) * 1040].rearrange(
                    "p (h c) -> p h c", c=65)[:, ot * 8:(ot + 1) * 8, 0:64]
                vin = pr.rearrange("p (h c) -> p h c", c=64)
                if layer == 0:
                    vev = vop(lambda e, vout=vout, vin=vin, ot=ot: e.scalar_tensor_tensor(
                        out=vout, in0=vin, scalar=cf[:, CFC["sv"]:CFC["sv"] + 1],
                        in1=scf[:, 1024 + ot * 512: 1024 + (ot + 1) * 512].rearrange(
                            "p (h c) -> p h c", c=64),
                        op0=ALU.mult, op1=ALU.add),
                        waits=[w_mm, w_scol_done, w_bv[ot], w_o65])
                else:
                    vev = vop(lambda e, vout=vout, vin=vin: e.tensor_copy(
                        out=vout, in_=vin), waits=[w_mm, w_o65])
                prev_ev[half] = vev
        return klast, w_kmax, vev

    w_cc_g = [NONE]
    w_ss_g = [NONE]

    def emit_rope_tables(deps):
        wlast = NONE
        bank = [psA[:, 0:512], psA[:, 512:1024], psD[:, 0:512], psE[:, 0:512]]
        bguard = [gA, gA, gD, gE]
        w_bs = []
        for tt in range(4):
            w_b = top(lambda e, tt=tt: e.matmul(
                bank[tt], onesf[0:1, :], itr[0:1, tt * 512:(tt + 1) * 512],
                start=True, stop=True),
                waits=[w_itr] + list(deps) + bguard[tt].for_write())
            bguard[tt].wrote(w_b)
            w_bs.append(w_b)
        for tt in range(4):
            w_b = w_bs[tt]
            w_a = vop(lambda e, tt=tt: e.tensor_scalar(
                out=scf[:, 512:1024], in0=bank[tt],
                scalar1=thc, scalar2=None,
                op0=ALU.mult), waits=[w_b, w_thc, wlast])
            bguard[tt].read(w_a)
            w_s_prev = NONE
            for which in range(2):
                shift = 0.25 if which == 1 else 0.0
                dst = uni[:, (2048 if which == 0 else 0) + tt * 512:][:, 0:512]
                w_n = vop(lambda e, shift=shift: e.tensor_scalar(
                    out=scf[:, 1024:1536], in0=scf[:, 512:1024],
                    scalar1=1.0 / (2 * math.pi), scalar2=shift,
                    op0=ALU.mult, op1=ALU.add), waits=[w_a, wlast, w_s_prev])
                w_k = vop(lambda e: e.tensor_copy(out=scf_i32[:, 1536:2048],
                                                  in_=scf[:, 1024:1536]),
                          waits=[w_n])
                w_kf = vop(lambda e: e.tensor_copy(out=gts[:, 0:512],
                                                   in_=scf_i32[:, 1536:2048]),
                           waits=[w_k])
                w_f = vop(lambda e: e.tensor_tensor(out=scf[:, 1024:1536],
                                                    in0=scf[:, 1024:1536],
                                                    in1=gts[:, 0:512],
                                                    op=ALU.subtract), waits=[w_kf])
                w_w1 = vop(lambda e: e.tensor_single_scalar(
                    out=gts[:, 0:512], in_=scf[:, 1024:1536], scalar=0.5,
                    op=ALU.is_ge), waits=[w_f])
                w_w2 = vop(lambda e: e.tensor_tensor(
                    out=scf[:, 1024:1536], in0=scf[:, 1024:1536],
                    in1=gts[:, 0:512], op=ALU.subtract), waits=[w_w1])
                w_s = sop(lambda e, dst=dst: e.activation(
                    out=dst, in_=scf[:, 1024:1536], func=AF.Sin,
                    scale=2 * math.pi), waits=[w_w2])
                w_s_prev = w_s
                if which == 0:
                    wlast = vop(lambda e, dst=dst: e.tensor_scalar(
                        out=dst, in0=dst, scalar1=sgc, scalar2=None, op0=ALU.mult),
                        waits=[w_s, w_sgc])
                    w_ss_g[0] = wlast
                else:
                    wlast = w_s
                    w_cc_g[0] = wlast
        return w_cc_g[0], w_ss_g[0]

    def emit_attention(layer, w_kdone, w_kmax, w_vdone, w_xn, w_masks, qrope_init,
                       filler=None, probe_attn=probe_attn):
        sfx = str(layer)
        xsrc = x_in if layer == 0 else x_out
        ctx0 = 0
        q0 = 8192
        p0 = 12288
        w_pv_slot = [NONE, NONE, NONE, NONE]
        qrope_last = qrope_init
        w_xw_store = {}
        w_attn_done = NONE

        def pump(n=1):
            if filler is not None:
                for _ in range(n):
                    if not filler():
                        break

        for qt in range(NQT):
            qev = {}
            for oc in range(8):
                slot = oc % 2
                w_w = wa_load(slot, dt["wq" + sfx][oc])
                w_mm = NONE
                for dc in range(8):
                    w_mm = top(lambda e, dc=dc, qt=qt, slot=slot: e.matmul(
                        psE[:, 0:512],
                        wsa[:, slot * 1024 + dc * 128: slot * 1024 + (dc + 1) * 128],
                        xn[:, dc * S + qt * 512: dc * S + qt * 512 + 512],
                        start=(dc == 0), stop=(dc == 7)),
                        waits=[w_w, w_xn] + (gE.for_write() if dc == 0 else []))
                wa_free[slot] = w_mm
                gE.wrote(w_mm)
                if layer == 0:
                    ev = vop(lambda e, oc=oc: e.tensor_scalar(
                        out=aat[:, q0 + oc * 512: q0 + (oc + 1) * 512],
                        in0=psE[:, 0:512], scalar1=cf[:, CFC["sq"]:CFC["sq"] + 1],
                        scalar2=sbias[:, SBC["bq0"] + oc:SBC["bq0"] + oc + 1],
                        op0=ALU.mult, op1=ALU.add),
                        waits=[w_mm, w_scol_done, w_params])
                else:
                    ev = sop(lambda e, oc=oc: e.copy(
                        out=aat[:, q0 + oc * 512: q0 + (oc + 1) * 512],
                        in_=psE[:, 0:512]), waits=[w_mm])
                gE.read(ev)
                qev[oc] = ev
                pump(1)
            if layer == 1:
                for oc in range(8):
                    qsl = aat[:, q0 + oc * 512: q0 + (oc + 1) * 512]
                    w_sw = top(lambda e, qsl=qsl: e.matmul(
                        psC[:, 0:512], pswp[:], qsl, start=True, stop=True),
                        waits=[qev[oc], w_pswap, qrope_last] + gCB.for_write())
                    gCB.wrote(w_sw)
                    w_r1 = vop(lambda e, qsl=qsl, qt=qt: e.tensor_tensor(
                        out=tbf[:, 0:512], in0=qsl,
                        in1=uni[:, qt * 512:(qt + 1) * 512], op=ALU.mult),
                        waits=[w_cc_g[0], qev[oc], qrope_last])
                    w_r2 = vop(lambda e, qt=qt: e.tensor_tensor(
                        out=tbf[:, 512:1024], in0=psC[:, 0:512],
                        in1=uni[:, 2048 + qt * 512: 2048 + (qt + 1) * 512],
                        op=ALU.mult), waits=[w_ss_g[0], w_sw])
                    gCB.read(w_r2)
                    qrope_last = vop(lambda e, qsl=qsl: e.tensor_tensor(
                        out=qsl, in0=tbf[:, 0:512], in1=tbf[:, 512:1024],
                        op=ALU.add), waits=[w_r1, w_r2])
                    qev[oc] = qrope_last
            qall = qev[7] if layer == 0 else qrope_last
            w_qm = vop(lambda e: e.memset(cf[0:1, CFC["qmax"]:CFC["qmax"] + 1], 0.0),
                       waits=[qall])
            wprev = w_qm
            rd_g = {0: NONE, 1: NONE}
            sq_g = [NONE, NONE]
            for oc in range(8):
                th = oc % 2
                w_sq = sop(lambda e, oc=oc, th=th: e.activation(
                    out=tbf[:, th * 512:(th + 1) * 512],
                    in_=aat[:, q0 + oc * 512: q0 + (oc + 1) * 512],
                    func=AF.Square), waits=[qall, sq_g[th]])
                for par in range(2):
                    bank = par
                    pr = psD[0:1, :] if bank == 0 else psE[0:1, :]
                    w_mm = top(lambda e, par=par, th=th, pr=pr: e.matmul(
                        pr, cbf[par * 64:(par + 1) * 64, 1 + par:2 + par],
                        tbf[par * 64:(par + 1) * 64, th * 512:(th + 1) * 512],
                        start=True, stop=True),
                        waits=[w_sq, rd_g[bank]] +
                              (gD.for_write() if (oc == 0 and par == 0) else []) +
                              (gE.for_write() if (oc == 0 and par == 1) else []))
                    sq_g[th] = w_mm
                    w_rm = vop(lambda e, pr=pr: e.reduce_max(
                        out=cf[0:1, CFC["t0"]:CFC["t0"] + 1], in_=pr,
                        axis=AX.X), waits=[w_mm, wprev])
                    rd_g[bank] = w_rm
                    wprev = vop(lambda e: e.tensor_tensor(
                        out=cf[0:1, CFC["qmax"]:CFC["qmax"] + 1],
                        in0=cf[0:1, CFC["qmax"]:CFC["qmax"] + 1],
                        in1=cf[0:1, CFC["t0"]:CFC["t0"] + 1], op=ALU.max),
                        waits=[w_rm])
                pump(1)
            w_c1 = vop(lambda e: e.tensor_tensor(
                out=cf[0:1, CFC["c2"]:CFC["c2"] + 1],
                in0=cf[0:1, CFC["qmax"]:CFC["qmax"] + 1],
                in1=cf[0:1, CFC["kmax"]:CFC["kmax"] + 1], op=ALU.mult),
                waits=[wprev, w_kmax])
            w_c2 = sop(lambda e: e.activation(out=cf[0:1, CFC["c2"]:CFC["c2"] + 1],
                                              in_=cf[0:1, CFC["c2"]:CFC["c2"] + 1],
                                              func=AF.Ln,
                                              bias=cf[0:1, CFC["eps"]:CFC["eps"] + 1]),
                       waits=[w_c1])
            w_c3 = sop(lambda e: e.activation(out=cf[0:1, CFC["c2"]:CFC["c2"] + 1],
                                              in_=cf[0:1, CFC["c2"]:CFC["c2"] + 1],
                                              func=AF.Exp, scale=0.5), waits=[w_c2])
            w_c4a = vop(lambda e: e.tensor_scalar(
                out=cf[0:1, CFC["c2"]:CFC["c2"] + 1],
                in0=cf[0:1, CFC["c2"]:CFC["c2"] + 1], scalar1=0.125, scalar2=-40.0,
                op0=ALU.mult, op1=ALU.add), waits=[w_c3])
            w_c4b = vop(lambda e: e.tensor_scalar_max(
                out=cf[0:1, CFC["c2"]:CFC["c2"] + 1],
                in0=cf[0:1, CFC["c2"]:CFC["c2"] + 1], scalar1=0.0), waits=[w_c4a])
            w_c4 = vop(lambda e: e.tensor_single_scalar(
                out=cf[0:1, CFC["c2"]:CFC["c2"] + 1],
                in_=cf[0:1, CFC["c2"]:CFC["c2"] + 1], scalar=-1.0, op=ALU.mult),
                waits=[w_c4b])
            w_c5 = top(lambda e: e.matmul(psD[:, 0:1], onesf[0:1, :],
                                          cf[0:1, CFC["c2"]:CFC["c2"] + 1],
                                          start=True, stop=True),
                       waits=[w_c4, rd_g[0]] + gD.for_write())
            gD.wrote(w_c5)
            w_cb = sop(lambda e: e.copy(out=cf[:, CFC["cbias"]:CFC["cbias"] + 1],
                                        in_=psD[:, 0:1]), waits=[w_c5])
            gD.read(w_cb)
            nkc = 16 if layer == 0 else (qt + 1) * 4
            nbat = nkc // 2
            pending_den = [None]
            psCB = psC[:, 0:512]
            psDEN = psC[:, 512:1024]

            def emit_den(h, ctxps, gX):
                w_pv = gX.w
                w_d1 = sop(lambda e, ctxps=ctxps: e.activation(
                    out=scf[64:65, 0:512], in_=ctxps[64:65, :], func=AF.Ln,
                    bias=cf[64:65, CFC["eps"]:CFC["eps"] + 1]), waits=[w_pv])
                w_d2 = sop(lambda e: e.activation(
                    out=scf[64:65, 0:512], in_=scf[64:65, 0:512], func=AF.Exp,
                    scale=-1.0), waits=[w_d1])
                w_d3 = top(lambda e: e.matmul(psDEN, onesf[64:65, :],
                                              scf[64:65, 0:512],
                                              start=True, stop=True),
                           waits=[w_d2] + gC1.for_write())
                gC1.wrote(w_d3)
                w_d4 = sop(lambda e: e.copy(out=scf[:, 512:1024], in_=psDEN),
                           waits=[w_d3])
                gC1.read(w_d4)
                w_ce = vop(lambda e, h=h, ctxps=ctxps: e.tensor_tensor(
                    out=aat[0:64, ctx0 + h * 512: ctx0 + (h + 1) * 512],
                    in0=ctxps[0:64, :], in1=scf[0:64, 512:1024], op=ALU.mult),
                    waits=[w_d4, w_pv])
                gX.read(w_d1)
                gX.read(w_ce)
                gC1.read(w_ce)
                return w_ce

            w_ctx_last = NONE
            pending_pv = [None]

            def flush_pv():
                if pending_pv[0] is not None:
                    pending_pv[0]()
                    pending_pv[0] = None

            for hp in range(NH // 2):
                hA = 2 * hp
                hB = 2 * hp + 1
                oc = hp
                for b in range(nbat):
                    w_smm = {}
                    for par, g in ((0, gA), (1, gB)):
                        rows = slice(par * 64, (par + 1) * 64)
                        w_m = NONE
                        for j in range(2):
                            kc = b * 2 + j
                            w_m = top(lambda e, kc=kc, j=j, oc=oc, rows=rows,
                                      par=par: e.matmul(
                                (psA if par == 0 else psB)[:, j * 512:(j + 1) * 512],
                                ar0[rows, oc * S + kc * 128: oc * S + (kc + 1) * 128],
                                aat[rows, q0 + oc * 512: q0 + (oc + 1) * 512],
                                start=True, stop=True),
                                waits=[w_kdone, qall] +
                                      (g.for_write() if j == 0 else []))
                        g.wrote(w_m)
                        w_smm[par] = w_m
                    flush_pv()
                    pump(1)
                    w_ready = {}
                    for par, g in ((0, gA), (1, gB)):
                        slot = par * 2 + (b % 2)
                        pbank = psA if par == 0 else psB
                        w_exp = sop(lambda e, pbank=pbank, slot=slot: e.activation(
                            out=aat[:, p0 + slot * 1024: p0 + slot * 1024 + 1024],
                            in_=pbank[:, 0:1024], func=AF.Exp,
                            bias=cf[:, CFC["cbias"]:CFC["cbias"] + 1], scale=0.125),
                            waits=[w_smm[par], w_cb, w_pv_slot[slot]])
                        g.read(w_exp)
                        w_r = w_exp
                        if layer == 1:
                            for j in range(2):
                                kc = b * 2 + j
                                off = kc - (nkc - 4)
                                if off >= 0:
                                    w_r = vop(lambda e, slot=slot, j=j, off=off:
                                              e.tensor_tensor(
                                        out=aat[:, p0 + slot * 1024 + j * 512:
                                                p0 + slot * 1024 + (j + 1) * 512],
                                        in0=aat[:, p0 + slot * 1024 + j * 512:
                                                p0 + slot * 1024 + (j + 1) * 512],
                                        in1=uni[:, 4096 + off * 512:
                                                4096 + (off + 1) * 512],
                                        op=ALU.mult), waits=[w_exp, w_masks])
                        w_ready[par] = w_r
                    if b == 0 and pending_den[0] is not None:
                        w_ctx_last = pending_den[0]()
                        pending_den[0] = None

                    def emit_pv(b=b, w_ready=w_ready, nkc=nkc, hA=hA, hB=hB):
                        for par, gX, ctxps, h in ((0, gD, psD[0:65, :], hA),
                                                  (1, gCB, psCB[0:65, :]
                                                   if False else psC[0:65, 0:512],
                                                   hB)):
                            slot = par * 2 + (b % 2)
                            w_pv = NONE
                            for j in range(2):
                                kc = b * 2 + j
                                w_pv = top(lambda e, kc=kc, j=j, h=h, slot=slot,
                                           nkc=nkc, ctxps=ctxps: e.matmul(
                                    ctxps,
                                    ar1[:, kc * 1040 + h * 65:
                                        kc * 1040 + (h + 1) * 65],
                                    aat[:, p0 + slot * 1024 + j * 512:
                                        p0 + slot * 1024 + (j + 1) * 512],
                                    start=(kc == 0), stop=(kc == nkc - 1)),
                                    waits=[w_ready[par], w_vdone] +
                                          (gX.for_write() if kc == 0 else []))
                            w_pv_slot[slot] = w_pv
                            if b == nbat - 1:
                                gX.wrote(w_pv)

                    pending_pv[0] = emit_pv
                flush_pv()

                def den_pair(hA=hA, hB=hB):
                    emit_den(hA, psD[0:65, :], gD)
                    return emit_den(hB, psC[0:65, 0:512], gCB)

                pending_den[0] = den_pair
                pump(1)
            if pending_den[0] is not None:
                w_ctx_last = pending_den[0]()
                pending_den[0] = None
            for oc in range(8):
                slot = oc % 2
                wbsl = wsb[0:64, slot * 2048:(slot + 1) * 2048]
                w_w = dma_g(wbsl, dt["wo" + sfx + "t"][oc],
                            waits=[wb_free[slot]], ctr=sWB[slot])
                w_mm = NONE
                for h in range(NH):
                    w_mm = top(lambda e, h=h, slot=slot: e.matmul(
                        psE[:, 0:512],
                        wsb[0:64, slot * 2048 + h * 128: slot * 2048 + (h + 1) * 128],
                        aat[0:64, ctx0 + h * 512: ctx0 + (h + 1) * 512],
                        start=(h == 0), stop=(h == 15)),
                        waits=[w_w, w_ctx_last] +
                              (gE.for_write() if h == 0 else []))
                wb_free[slot] = w_mm
                gE.wrote(w_mm)
                xslot = oc % 4
                xsl = xw[:, xslot * 512:(xslot + 1) * 512]
                w_xl = dma_s(xsl, xsrc[:, oc, qt * 512:(qt + 1) * 512],
                             waits=[w_xn, w_xw_store.get((qt - 1, oc), NONE)],
                             ctr=sXW)
                if layer == 0:
                    w_a1 = vop(lambda e, xsl=xsl, oc=oc: e.tensor_scalar(
                        out=xsl, in0=xsl,
                        scalar1=sbias[:, SBC["bo0"] + oc:SBC["bo0"] + oc + 1],
                        scalar2=None, op0=ALU.add), waits=[w_xl, w_params])
                    w_a2 = vop(lambda e, xsl=xsl: e.scalar_tensor_tensor(
                        out=xsl, in0=psE[:, 0:512],
                        scalar=cf[:, CFC["so"]:CFC["so"] + 1], in1=xsl,
                        op0=ALU.mult, op1=ALU.add),
                        waits=[w_mm, w_a1, w_scol_done])
                else:
                    w_a2 = vop(lambda e, xsl=xsl: e.tensor_tensor(
                        out=xsl, in0=xsl, in1=psE[:, 0:512], op=ALU.add),
                        waits=[w_xl, w_mm])
                gE.read(w_a2)
                w_st = dma_s(x_out[:, oc, qt * 512:(qt + 1) * 512], xsl,
                             waits=[w_a2], ctr=sST)
                w_xw_store[(qt, oc)] = w_st
                w_attn_done = w_st
                pump(1)
        return w_attn_done

    def make_moe_filler(w_gates, w_xn, w_ln_done):
        acc = aat_f32
        w1t = dt["w1m"]
        w2t = dt["w2m"]
        state = dict(done=NONE, x07=NONE)
        w_h_unit = {}
        w_attn_ref = [NONE]
        state_g = dict(gtr=NONE, fl=NONE)
        w_gelu_read = [NONE, NONE]
        fst = dict(state_g=state_g, w_gelu_read=w_gelu_read)

        def units():
            w_gelu_read[0] = w_attn_ref[0]
            w_gelu_read[1] = w_attn_ref[0]
            prev_psC = [w_attn_ref[0]]
            w_xupd = {}
            w_accev = {}
            w_gate_done = {}
            seq = [(tt, ei) for tt in range(2) for ei in range(E)]

            def gates_A(tt, ei):
                ge = gts[:, 256 + ei * 16: 256 + (ei + 1) * 16]
                w_tr = top(lambda e, ge=ge: e.transpose(
                    psD[0:16, 0:128], ge, ident[:]),
                    waits=[w_gates, w_id, state_g["gtr"], w_attn_ref[0]] +
                          gD.for_write())
                gD.wrote(w_tr)
                w_gc = sop(lambda e: e.copy(out=scf[0:16, 1536:1664],
                                            in_=psD[0:16, 0:128]),
                           waits=[w_tr])
                gD.read(w_gc)
                state_g["fl"] = dma_g(scf_bf[0:1, 0:2048],
                                      scf[0:16, 1536:1664],
                                      waits=[w_gc], ctr=sFL)

            def gates_B(tt, ei):
                gcol = (ei % 2) * 1024
                w_fl = state_g["fl"]
                w_gb = NONE
                for hf in range(2):
                    pr = psD if hf == 0 else psE
                    gX = gD if hf == 0 else gE
                    wt = top(lambda e, hf=hf, tt=tt, pr=pr: e.matmul(
                        pr[:, 0:512], cbf[0:1, 3:131],
                        scf_bf[0:1, tt * 1024 + hf * 512:
                               tt * 1024 + (hf + 1) * 512],
                        start=True, stop=True),
                        waits=[w_fl, state_g["gtr"]] + gX.for_write())
                    gX.wrote(wt)
                    w_gb = sop(lambda e, hf=hf, pr=pr, gcol=gcol: e.copy(
                        out=uni[:, gcol + hf * 512: gcol + (hf + 1) * 512],
                        in_=pr[:, 0:512]),
                        waits=[wt, w_pswap, w_gelu_read[0], w_gelu_read[1]])
                    gX.read(w_gb)
                state_g["gtr"] = w_gb
                w_gate_done[(tt, ei)] = w_gb

            def x_update(tt, dc):
                xslot = dc % 2
                xsl = gcopy[:, xslot * 1024: (xslot + 1) * 1024]
                w_xl = dma_s(xsl, x_out[:, dc, tt * 1024:(tt + 1) * 1024],
                             waits=[w_attn_ref[0],
                                    w_xupd.get((tt, dc - 2), NONE)],
                             ctr=sXW)
                w_ad = vop(lambda e, xsl=xsl, dc=dc: e.tensor_tensor(
                    out=xsl, in0=xsl, in1=acc[:, dc * 1024:(dc + 1) * 1024],
                    op=ALU.add), waits=[w_xl, w_accev[dc]])
                w_stx = dma_s(x_out[:, dc, tt * 1024:(tt + 1) * 1024], xsl,
                              waits=[w_ad], ctr=sST)
                w_xupd[(tt, dc)] = w_stx
                state["done"] = w_stx
                if (tt, dc) == (0, 7):
                    state["x07"] = w_stx

            gates_A(*seq[0])
            gates_B(*seq[0])
            for k, (tt, ei) in enumerate(seq):
                gcol = (ei % 2) * 1024
                yield ("start", k)
                for hc in range(16):
                    slot = hc % 2
                    w_w = wa_load(slot, w1t[ei * 16 + hc])
                    pbank = psA if slot == 0 else psB
                    g = gA if slot == 0 else gB
                    harena = ar0 if hc % 2 == 0 else ar1
                    hcol = (hc // 2) * 2048
                    w_mm = NONE
                    for dc in range(8):
                        for half in range(2):
                            w_mm = top(lambda e, dc=dc, half=half, tt=tt,
                                       slot=slot, pbank=pbank: e.matmul(
                                pbank[:, half * 512:(half + 1) * 512],
                                wsa[:, slot * 1024 + dc * 128:
                                    slot * 1024 + (dc + 1) * 128],
                                xn[:, dc * S + tt * 1024 + half * 512:
                                   dc * S + tt * 1024 + half * 512 + 512],
                                start=(dc == 0), stop=(dc == 7)),
                                waits=[w_w, w_xn] +
                                      (g.for_write()
                                       if (dc == 0 and half == 0) else []))
                    wa_free[slot] = w_mm
                    g.wrote(w_mm)
                    w_gl = sop(lambda e, pbank=pbank: e.activation(
                        out=uni_f32[:, 1024:2048], in_=pbank[:, 0:1024],
                        func=AF.Gelu),
                        waits=[w_mm, w_gate_done[(tt, ei)], w_gelu_read[0],
                               w_gelu_read[1]])
                    g.read(w_gl)
                    w_hm = vop(lambda e, harena=harena, hcol=hcol, gcol=gcol:
                               e.tensor_tensor(
                        out=harena[:, hcol:hcol + 1024],
                        in0=uni_f32[:, 1024:2048],
                        in1=uni[:, gcol:gcol + 1024], op=ALU.mult),
                        waits=[w_gl])
                    w_gelu_read[slot] = w_hm
                    w_h_unit[(ei, hc)] = w_hm
                    if hc % 2 == 1:
                        yield ("w1", k, hc)
                w2_loads = {}
                for dc in range(8):
                    slot = dc % 2
                    pr = psC if dc % 2 == 0 else psB
                    if dc == 0:
                        w2_loads[0] = dma_g(
                            wsb[:, 0:2048], w2t[ei, 0],
                            waits=[wb_free[0]], ctr=sWB[0])
                    if dc + 1 < 8:
                        w2_loads[dc + 1] = dma_g(
                            wsb[:, ((dc + 1) % 2) * 2048:
                                ((dc + 1) % 2 + 1) * 2048],
                            w2t[ei, dc + 1], waits=[wb_free[(dc + 1) % 2]],
                            ctr=sWB[(dc + 1) % 2])
                    w_w = w2_loads[dc]
                    w_mm = NONE
                    if dc % 2 == 0:
                        bank_w = [prev_psC[0]]
                    else:
                        bank_w = gB.for_write()
                    for hc in range(16):
                        harena = ar0 if hc % 2 == 0 else ar1
                        hcol = (hc // 2) * 2048
                        for half in range(2):
                            w_mm = top(lambda e, hc=hc, half=half, slot=slot,
                                       harena=harena, hcol=hcol, pr=pr:
                                       e.matmul(
                                pr[:, half * 512:(half + 1) * 512],
                                wsb[:, slot * 2048 + hc * 128:
                                    slot * 2048 + (hc + 1) * 128],
                                harena[:, hcol + half * 512:
                                       hcol + half * 512 + 512],
                                start=(hc == 0), stop=(hc == 15)),
                                waits=[w_w, w_h_unit[(ei, hc)]] +
                                      (bank_w if hc == 0 and half == 0
                                       else []))
                    wb_free[slot] = w_mm
                    if dc % 2 == 1:
                        gB.wrote(w_mm)
                    if ei == 0:
                        w_accev[dc] = vop(lambda e, dc=dc, pr=pr:
                                          e.tensor_copy(
                            out=acc[:, dc * 1024:(dc + 1) * 1024],
                            in_=pr[:, 0:1024]),
                            waits=[w_mm, w_gates, w_attn_ref[0],
                                   w_xupd.get((tt - 1, dc), NONE)])
                    else:
                        w_accev[dc] = vop(lambda e, dc=dc, pr=pr:
                                          e.tensor_tensor(
                            out=acc[:, dc * 1024:(dc + 1) * 1024],
                            in0=acc[:, dc * 1024:(dc + 1) * 1024],
                            in1=pr[:, 0:1024], op=ALU.add), waits=[w_mm])
                    if dc % 2 == 0:
                        prev_psC[0] = w_accev[dc]
                    else:
                        gB.read(w_accev[dc])
                    if ei == E - 1:
                        x_update(tt, dc)
                    if dc == 3 and k + 1 < len(seq):
                        gates_A(*seq[k + 1])
                    if dc == 6 and k + 1 < len(seq):
                        gates_B(*seq[k + 1])
                    yield ("w2", k, dc)

        gen = units()

        def pump_fn():
            try:
                next(gen)
                return True
            except StopIteration:
                return False

        def pump_until(label):
            for lab in gen:
                if lab == label:
                    return

        def finish_fn():
            while pump_fn():
                pass
            return state["done"]

        return pump_fn, pump_until, finish_fn, w_attn_ref, fst, state

    def emit_ffn1(w_xn, w_attn_done):
        w1t = dt["w1f"]
        w2t = dt["w2f"]
        w_done = NONE
        w_h8_read = [w_attn_done, w_attn_done]
        w_h8s = [w_attn_done, w_attn_done]
        w_fupd = {}
        psC_rd = [w_attn_done, w_attn_done]
        w_fadd_l = NONE
        for tt in range(2):
            w_h = {}
            for hc in range(32):
                slot = hc % 2
                w_w = wa_load(slot, w1t[hc])
                pbank = psA if slot == 0 else psB
                g = gA if slot == 0 else gB
                w_mm = NONE
                for dc in range(8):
                    for half in range(2):
                        w_mm = top(lambda e, dc=dc, half=half, tt=tt, slot=slot,
                                   pbank=pbank: e.matmul(
                            pbank[:, half * 512:(half + 1) * 512],
                            wsa[:, slot * 1024 + dc * 128:
                                slot * 1024 + (dc + 1) * 128],
                            xn[:, dc * S + tt * 1024 + half * 512:
                               dc * S + tt * 1024 + half * 512 + 512],
                            start=(dc == 0), stop=(dc == 7)),
                            waits=[w_w, w_xn] +
                                  (g.for_write() if (dc == 0 and half == 0)
                                   else []))
                wa_free[slot] = w_mm
                g.wrote(w_mm)
                w_rl = sop(lambda e, pbank=pbank, hc=hc, slot=slot: e.activation(
                    out=scf_f8[:, slot * 1024:(slot + 1) * 1024],
                    in_=pbank[:, 0:1024], func=AF.Relu,
                    bias=b1s[:, hc:hc + 1]),
                    waits=[w_mm, w_params, w_h8s[slot]])
                g.read(w_rl)
                harena = ar0 if hc < 16 else ar1
                hcol = (hc % 16) * 1024
                w_cv = vop(lambda e, harena=harena, hcol=hcol, slot=slot:
                           e.tensor_copy(
                    out=harena[:, hcol:hcol + 1024],
                    in_=scf_f8[:, slot * 1024:(slot + 1) * 1024]),
                    waits=[w_rl])
                w_h8s[slot] = w_cv
                w_h[hc] = w_cv
            for dc in range(8):
                w_q = {}
                for hhalf in range(2):
                    slot = hhalf
                    wbsl = wsb[:, slot * 2048:(slot + 1) * 2048]
                    w_w = dma_g(wbsl, w2t[dc, :, hhalf * 2048:(hhalf + 1) * 2048],
                                waits=[wb_free[slot]], ctr=sWB[slot])
                    for half in range(2):
                        w_mm = NONE
                        for hc16 in range(16):
                            hc = hhalf * 16 + hc16
                            harena = ar0 if hc < 16 else ar1
                            hcol = (hc % 16) * 1024
                            w_mm = top(lambda e, hc16=hc16, half=half, slot=slot,
                                       harena=harena, hcol=hcol, hc=hc,
                                       hhalf=hhalf: e.matmul(
                                psC[:, half * 512:(half + 1) * 512],
                                wsb[:, slot * 2048 + hc16 * 128:
                                    slot * 2048 + (hc16 + 1) * 128],
                                harena[:, hcol + half * 512:
                                       hcol + half * 512 + 512],
                                start=(hc == 0), stop=(hc == 31)),
                                waits=[w_w, w_h[hc]] +
                                      ([psC_rd[half]] if hc16 == 0 and
                                       hhalf == 0 else []))
                        if hhalf == 1:
                            w_q[half] = vop(lambda e, dc=dc, half=half:
                                            e.tensor_scalar(
                                out=scf_f16[:, 2048 + half * 512:
                                            2048 + (half + 1) * 512],
                                in0=psC[:, half * 512:(half + 1) * 512],
                                scalar1=sbias[:, SBC["bqf"] + dc:
                                              SBC["bqf"] + dc + 1],
                                scalar2=None, op0=ALU.add),
                                waits=[w_mm, w_bqf, w_fadd_l])
                            psC_rd[half] = w_q[half]
                    wb_free[slot] = w_mm
                xslot = dc % 2
                xsl = xw[:, 2048 + xslot * 1024: 2048 + (xslot + 1) * 1024]
                w_xl = dma_s(xsl, x_out[:, dc, tt * 1024:(tt + 1) * 1024],
                             waits=[w_attn_done, w_fupd.get((tt, dc - 2), NONE),
                                    w_fupd.get((tt - 1, dc + 6), NONE)],
                             ctr=sXW)
                w_ad = vop(lambda e, xsl=xsl: e.tensor_tensor(
                    out=xsl, in0=xsl, in1=scf_f16[:, 2048:3072], op=ALU.add),
                    waits=[w_xl, w_q[0], w_q[1]])
                w_fadd_l = w_ad
                w_stx = dma_s(x_out[:, dc, tt * 1024:(tt + 1) * 1024], xsl,
                              waits=[w_ad], ctr=sST)
                w_fupd[(tt, dc)] = w_stx
                w_done = w_stx
        return w_done

    w_gelu_read2 = [NONE]
    w_h8_read2 = [NONE]
    w_fadd = {}
    gA = Guard(); gB = Guard(); gC0 = Guard(); gC1 = Guard()
    gD = Guard(); gE = Guard(); gCB = Guard()

    def dbg_dump(waits):
        d0 = nc.dram_tensor("dbg0", [128, 16384], BF16, kind="ExternalOutput").ap()
        d1 = nc.dram_tensor("dbg1", [128, 16640], BF16, kind="ExternalOutput").ap()
        d2 = nc.dram_tensor("dbg2", [128, 14336], BF16, kind="ExternalOutput").ap()
        d3 = nc.dram_tensor("dbg3", [128, 16384], BF16, kind="ExternalOutput").ap()
        d4 = nc.dram_tensor("dbg4", [128, 2048], F32, kind="ExternalOutput").ap()
        d5 = nc.dram_tensor("dbg5", [128, 7168], F32, kind="ExternalOutput").ap()
        d6 = nc.dram_tensor("dbg6", [128, 6144], BF16, kind="ExternalOutput").ap()
        d7 = nc.dram_tensor("dbg7", [128, 512], F32, kind="ExternalOutput").ap()
        d8 = nc.dram_tensor("dbg8", [128, 2048], F32, kind="ExternalOutput").ap()
        dma_s(d0, ar0[:], waits=waits, ctr=sST)
        dma_s(d1, ar1[:], waits=waits, ctr=sST)
        dma_s(d2, aat[:, 0:14336], waits=waits, ctr=sST)
        dma_s(d3, xn[:], waits=waits, ctr=sST)
        dma_s(d4, scf[:], waits=waits, ctr=sST)
        dma_s(d5, aat_f32[:, 0:7168], waits=waits, ctr=sST)
        dma_s(d6, uni[:], waits=waits, ctr=sST)
        dma_s(d7, gts[:], waits=waits, ctr=sST)
        dma_s(d8, gcopy[:], waits=waits, ctr=sST)

    stopped = False
    w_xn0, w_logits, _ = emit_ln(0, [w_params], w_scol_done)
    w_gates = emit_router(w_logits)
    if stop_after == "ln0":
        dbg_dump([w_xn0, w_gates]); stopped = True
    if not stopped:
        w_kd0, w_km0, w_vd0 = emit_kv(0, w_xn0, w_logits)
        gA.read(w_vd0); gB.read(w_vd0); gC0.read(w_vd0); gC1.read(w_vd0); gCB.read(w_vd0)
        gD.read(w_km0); gE.read(w_km0)
        if stop_after == "kv0":
            dbg_dump([w_kd0, w_km0, w_vd0, w_gates]); stopped = True
    if not stopped:
        moe_pump, moe_until, moe_finish, w_attn_ref, fst, moe_state = \
            make_moe_filler(w_gates, w_xn0, w_xn0)
        w_at0 = emit_attention(0, w_kd0, w_km0, w_vd0, w_xn0, NONE, NONE,
                               filler=None)
        if stop_after == "attn0":
            dbg_dump([w_at0]); stopped = True
    if not stopped:
        w_attn_ref[0] = w_at0
        moe_until(("w1", 15, 15))
        w_cc, w_ss = emit_rope_tables([fst["w_gelu_read"][0],
                                       fst["w_gelu_read"][1],
                                       fst["state_g"]["gtr"]])
        w_moe = moe_finish()
        if stop_after == "moe0":
            dbg_dump([w_moe]); stopped = True
    if not stopped:
        w_masks = w_masks0
        w_xn1, _, _ = emit_ln(1, [w_moe], w_moe)
        w_kd1, w_km1, w_vd1 = emit_kv(1, w_xn1, w_xn1)
        gA.read(w_vd1); gB.read(w_vd1); gC0.read(w_kd1); gC1.read(w_vd1); gCB.read(w_kd1)
        gD.read(w_km1); gE.read(w_km1)
        if stop_after == "kv1":
            dbg_dump([w_kd1, w_km1, w_vd1, w_masks]); stopped = True
    if not stopped:
        w_at1 = emit_attention(1, w_kd1, w_km1, w_vd1, w_xn1, w_masks, w_kd1,
                               filler=None)
        if stop_after == "attn1":
            dbg_dump([w_at1]); stopped = True
    if not stopped:
        w_ffn = emit_ffn1(w_xn1, w_at1)

    sST_final = sST.n
    sch.op("sync", lambda e: e.wait_ge(sST.sem, sST_final), waits=())

    with es:
        with nc.Block() as block:
            sch.emit(block)
    return nc


def _host_layout(inputs):
    def cp(a):
        return np.ascontiguousarray(np.asarray(a, dtype=np.float32))

    def oc_tiles(wT):
        return np.ascontiguousarray(
            wT.reshape(8, 128, 8, 128).transpose(2, 1, 0, 3).reshape(8, 128, 1024))

    perm = np.concatenate([np.concatenate([h * 64 + np.arange(0, 64, 2),
                                           h * 64 + np.arange(1, 64, 2)])
                           for h in range(16)])
    m = {}
    m["wq0"] = oc_tiles(cp(inputs["L0_attn_w_q"]).T)
    m["wk0"] = oc_tiles(cp(inputs["L0_attn_w_k"]).T)
    m["wv0"] = np.ascontiguousarray(
        cp(inputs["L0_attn_w_v"]).T.reshape(8, 128, 1024).transpose(1, 0, 2)
        .reshape(128, 8192))
    m["wo0t"] = np.ascontiguousarray(
        cp(inputs["L0_attn_w_o"]).T.reshape(16, 64, 8, 128).transpose(2, 1, 0, 3)
        .reshape(8, 64, 2048))
    for nm, key in (("bq0", "L0_attn_b_q"), ("bk0", "L0_attn_b_k"),
                    ("bv0", "L0_attn_b_v"), ("bo0", "L0_attn_b_o")):
        m[nm] = cp(inputs[key])
    for nm, key in (("sq0", "L0_attn_s_q"), ("sk0", "L0_attn_s_k"),
                    ("sv0", "L0_attn_s_v"), ("so0", "L0_attn_s_o")):
        m[nm] = cp(inputs[key]).reshape(1, 1)
    m["rw0"] = np.ascontiguousarray(
        cp(inputs["L0_ffn_router_w"]).reshape(8, 128, 8).transpose(1, 0, 2)
        .reshape(128, 64))
    w1 = cp(inputs["L0_ffn_w1"])
    m["w1m"] = np.ascontiguousarray(
        w1.reshape(8, 128, 128, 128).transpose(2, 1, 0, 3).reshape(128, 128, 1024))
    w2 = cp(inputs["L0_ffn_w2"])
    m["w2m"] = np.ascontiguousarray(
        w2.reshape(8, 16, 128, 8, 128).transpose(0, 3, 2, 1, 4)
        .reshape(8, 8, 128, 2048))
    m["wq1"] = oc_tiles(cp(inputs["L1_attn_wq"]).T[:, perm])
    m["wk1"] = oc_tiles(cp(inputs["L1_attn_wk"]).T[:, perm])
    m["wv1"] = np.ascontiguousarray(
        cp(inputs["L1_attn_wv"]).T.reshape(8, 128, 1024).transpose(1, 0, 2)
        .reshape(128, 8192))
    m["wo1t"] = np.ascontiguousarray(
        cp(inputs["L1_attn_wo"]).T.reshape(16, 64, 8, 128).transpose(2, 1, 0, 3)
        .reshape(8, 64, 2048))
    w1f = cp(inputs["L1_ffn_w1"]).T
    m["w1f"] = np.ascontiguousarray(
        w1f.reshape(8, 128, 32, 128).transpose(2, 1, 0, 3).reshape(32, 128, 1024))
    w2f = cp(inputs["L1_ffn_w2"]).T
    m["w2f"] = np.ascontiguousarray(
        w2f.reshape(32, 128, 8, 128).transpose(2, 1, 0, 3).reshape(8, 128, 4096))
    m["b1f"] = cp(inputs["L1_ffn_b1"])
    m["b2f"] = cp(inputs["L1_ffn_b2"])
    return m


def kernel(**inputs):
    nc = build_program()
    shared = _host_layout(inputs)
    x = np.asarray(inputs["L0_attn_x"], dtype=np.float32)
    in_maps = []
    for b in range(8):
        mm = dict(shared)
        mm["x_t"] = np.ascontiguousarray(
            x[b].T.reshape(8, 128, S).transpose(1, 0, 2))
        in_maps.append(mm)
    res = run_bass_kernel_spmd(nc, in_maps, core_ids=list(range(8)))
    out = np.stack([np.asarray(r["xout"]).transpose(1, 0, 2).reshape(D, S).T
                    for r in res.results], axis=0)
    return out


# version 18
# speedup vs baseline: 1.0745x; 1.0745x over previous
import math
from contextlib import ExitStack

import numpy as np

import concourse.bass as bass
import concourse.mybir as mybir
from concourse.bass_utils import run_bass_kernel_spmd

F32 = mybir.dt.float32
BF16 = mybir.dt.bfloat16
FP16 = mybir.dt.float16
FP8 = mybir.dt.float8e4
I32 = mybir.dt.int32
AF = mybir.ActivationFunctionType
ALU = mybir.AluOpType
AX = mybir.AxisListType

S = 2048
D = 1024
NH = 16
TQ = 512
NQT = S // TQ
E = 8
H1 = 4096

ENGINES = ("sync", "gpsimd", "vector", "scalar", "tensor")


class Ctr:
    __slots__ = ("sem", "n", "name")

    def __init__(self, sem, name):
        self.sem = sem
        self.n = 0
        self.name = name


class Sched:
    def __init__(self):
        self.items = {e: [] for e in ENGINES}

    def op(self, eng, fn, waits=(), inc=None, amt=1):
        w = [(c, v) for (c, v) in waits if c is not None and v > 0]
        self.items[eng].append((tuple(w), fn, inc, amt))
        if inc is not None:
            inc.n += amt
            return inc.n
        return None

    def emit(self, block):
        for ename in ENGINES:
            items = self.items[ename]

            def mk(items=items):
                waited = {}

                def body(eng):
                    for waits, fn, inc, amt in items:
                        for c, v in waits:
                            if waited.get(c, 0) >= v:
                                continue
                            eng.wait_ge(c.sem, v)
                            waited[c] = v
                        inst = fn(eng)
                        if inc is not None:
                            inst.then_inc(inc.sem, amt)
                return body

            getattr(block, ename)(mk())


class Guard:

    def __init__(self, init=None):
        self.w = init
        self.readers = []

    def for_write(self):
        w = list(self.readers)
        if self.w is not None:
            w.append(self.w)
        return w

    def wrote(self, m):
        self.w = m
        self.readers = []

    def for_read(self):
        return [self.w] if self.w is not None else []

    def read(self, m):
        self.readers.append(m)


def build_program(stop_after=None, probe_attn=False):
    nc = bass.Bass("TRN2", target_bir_lowering=False, detect_race_conditions=False)
    dt = {}

    def din(name, shape):
        dt[name] = nc.dram_tensor(name, list(shape), F32, kind="ExternalInput").ap()
        return dt[name]

    x_in = din("x_t", [128, 8, S])
    din("wq0", [8, 128, 1024]); din("wk0", [8, 128, 1024]); din("wv0", [128, 8192])
    din("wo0t", [8, 64, 2048])
    din("bq0", [D]); din("bk0", [D]); din("bv0", [D]); din("bo0", [D])
    din("sq0", [1, 1]); din("sk0", [1, 1]); din("sv0", [1, 1]); din("so0", [1, 1])
    din("rw0", [128, 64])
    din("w1m", [128, 128, 1024])
    din("w2m", [E, 8, 128, 2048])
    din("wq1", [8, 128, 1024]); din("wk1", [8, 128, 1024]); din("wv1", [128, 8192])
    din("wo1t", [8, 64, 2048])
    din("w1f", [32, 128, 1024]); din("w2f", [8, 128, 4096])
    din("b1f", [H1]); din("b2f", [D])

    x_out = nc.dram_tensor("xout", [128, 8, S], F32, kind="ExternalOutput").ap()

    es = ExitStack()

    def sbuf(name, shape, dtype):
        return es.enter_context(nc.sbuf_tensor(name, list(shape), dtype))

    def psmk(name, shape):
        return es.enter_context(nc.psum_tensor(name, list(shape), F32))

    xn = sbuf("xn", [128, 8 * S], BF16)
    ar0 = sbuf("ar0", [128, 16384], BF16)
    ar1 = sbuf("ar1", [128, 16640], BF16)
    aat = sbuf("aat", [128, 16384], BF16)
    xw = sbuf("xw", [128, 4096], F32)
    uni = sbuf("uni", [128, 6144], BF16)
    wsa = sbuf("wsa", [128, 3072], BF16)
    wsb = sbuf("wsb", [128, 4096], BF16)
    scf = sbuf("scf", [128, 2048], F32)
    tbf = sbuf("tbf", [128, 1024], BF16)
    gts = sbuf("gts", [128, 512], F32)
    sbias = sbuf("sbias", [128, 64], F32)
    b1s = sbuf("b1s", [128, 32], F32)
    cbf = sbuf("cbf", [128, 256], BF16)
    cf = sbuf("cf", [128, 64], F32)
    onesf = sbuf("onesf", [128, 128], F32)
    ident = sbuf("ident", [128, 128], F32)
    pswp = sbuf("pswp", [128, 128], BF16)
    rwsb = sbuf("rwsb", [128, 64], F32)
    gcopy = sbuf("gcopy", [128, 2048], F32)
    itr = sbuf("itr", [128, 2048], F32)

    psA = psmk("psA", [128, 1024])
    psB = psmk("psB", [128, 1024])
    psC = psmk("psC", [128, 1024])
    psD = psmk("psD", [128, 512])
    psE = psmk("psE", [128, 512])

    aat_f32 = aat[:].bitcast(F32)
    uni_f32 = uni[:].bitcast(F32)
    scf_bf = scf[:].bitcast(BF16)
    scf_i32 = scf[:].bitcast(I32)
    scf_f8 = scf[:].bitcast(FP8)
    scf_f16 = scf[:].bitcast(FP16)

    CFC = dict(iota=0, theta=1, sign=2, eps=3, cbias=4, kmax=5, qmax=6, c2=7,
               sq=8, sk=9, sv=10, so=11, t0=12, t1=13, eps5=15, npi=16)
    SBC = dict(bq0=0, bk0=8, bv0=16, bo0=24, b2f=32, bqf=40)

    def mkctr(name):
        h = es.enter_context(nc.semaphore(name))
        return Ctr(h, name)

    sT = mkctr("sT"); sV = mkctr("sV"); sS = mkctr("sS"); sG = mkctr("sG")
    sLD = mkctr("sLD"); sXW = mkctr("sXW"); sST = mkctr("sST")
    sWA = [mkctr("sWA0"), mkctr("sWA1"), mkctr("sWA2")]
    sWB = [mkctr("sWB0"), mkctr("sWB1")]
    sQW = mkctr("sQW")
    sFL = mkctr("sFL")

    sch = Sched()
    NONE = (sT, 0)

    def dma_s(out, in_, waits=(), ctr=None):
        c = ctr or sLD
        return (c, sch.op("sync", lambda e, o=out, i=in_: e.dma_start(out=o, in_=i),
                          waits=waits, inc=c, amt=16))

    def dma_g(out, in_, waits=(), ctr=None):
        c = ctr or sQW
        return (c, sch.op("gpsimd", lambda e, o=out, i=in_: e.dma_start(out=o, in_=i),
                          waits=waits, inc=c, amt=16))

    def top(fn, waits=()):
        return (sT, sch.op("tensor", fn, waits=waits, inc=sT))

    def vop(fn, waits=()):
        return (sV, sch.op("vector", fn, waits=waits, inc=sV))

    def sop(fn, waits=()):
        return (sS, sch.op("scalar", fn, waits=waits, inc=sS))

    def gop(fn, waits=()):
        return (sG, sch.op("gpsimd", fn, waits=waits, inc=sG))

    w_iota1 = gop(lambda e: e.iota(cf[:, 0:1], pattern=[[0, 1]], base=0,
                                   channel_multiplier=1,
                                   allow_small_or_imprecise_dtypes=True))
    w_iota2 = gop(lambda e: e.iota(scf[:, 0:512], pattern=[[-1, 512]], base=0,
                                   channel_multiplier=1,
                                   allow_small_or_imprecise_dtypes=True))
    w_itr = gop(lambda e: e.iota(itr[0:1, 0:2048], pattern=[[1, 2048]], base=0,
                                 channel_multiplier=0,
                                 allow_small_or_imprecise_dtypes=True),
                waits=[w_iota2])
    w_masks0 = NONE
    for off in range(4):
        w_masks0 = vop(lambda e, off=off: e.tensor_single_scalar(
            out=uni[:, 4096 + off * 512: 4096 + (off + 1) * 512],
            in_=scf[:, 0:512], scalar=-128.0 * off, op=ALU.is_le),
            waits=[w_iota2, w_masks0])

    vop(lambda e: e.memset(cbf[:, 0:1], 1.0))
    vop(lambda e: e.memset(cbf[:, 3:131], 1.0))
    vop(lambda e: e.memset(onesf[:], 1.0))
    vop(lambda e: e.memset(cf[:, CFC["eps"]:CFC["eps"] + 1], 1e-30))
    vop(lambda e: e.memset(cf[:, CFC["eps5"]:CFC["eps5"] + 1], 1e-5))
    vop(lambda e: e.memset(cf[:, CFC["npi"]:CFC["npi"] + 1], -math.pi))
    w_id = vop(lambda e: e.tensor_single_scalar(out=ident[:], in_=scf[:, 0:128],
                                                scalar=0.0, op=ALU.is_equal),
               waits=[w_iota2])
    P1 = uni_f32[:, 0:128]
    P2 = uni_f32[:, 128:256]
    selc = cf[:, CFC["t0"]:CFC["t0"] + 1]
    selc2 = cf[:, CFC["t1"]:CFC["t1"] + 1]
    wl = vop(lambda e: e.tensor_single_scalar(out=P1, in_=scf[:, 0:128], scalar=-32.0,
                                              op=ALU.is_equal), waits=[w_iota2])
    wl = vop(lambda e: e.tensor_single_scalar(out=P2, in_=scf[:, 0:128], scalar=32.0,
                                              op=ALU.is_equal), waits=[wl])
    wl = vop(lambda e: e.tensor_single_scalar(out=selc2, in_=cf[:, 0:1], scalar=64.0,
                                              op=ALU.is_ge), waits=[w_iota1, wl])
    wl = vop(lambda e: e.scalar_tensor_tensor(out=selc, in0=selc2, scalar=-64.0,
                                              in1=cf[:, 0:1], op0=ALU.mult,
                                              op1=ALU.add), waits=[wl])
    w_pm64 = wl = vop(lambda e: e.tensor_copy(out=cf[:, CFC["t1"] + 1:CFC["t1"] + 2],
                                              in_=selc), waits=[wl])
    wl = vop(lambda e: e.tensor_single_scalar(out=selc, in_=selc, scalar=32.0,
                                              op=ALU.is_lt), waits=[wl])
    wl = vop(lambda e: e.tensor_scalar(out=P1, in0=P1, scalar1=selc, scalar2=None,
                                       op0=ALU.mult), waits=[wl])
    wl = vop(lambda e: e.tensor_scalar(out=selc2, in0=selc, scalar1=-1.0, scalar2=1.0,
                                       op0=ALU.mult, op1=ALU.add), waits=[wl])
    wl = vop(lambda e: e.tensor_scalar(out=P2, in0=P2, scalar1=selc2, scalar2=None,
                                       op0=ALU.mult), waits=[wl])
    wl = vop(lambda e: e.tensor_tensor(out=P1, in0=P1, in1=P2, op=ALU.add), waits=[wl])
    w_pswap = wl = vop(lambda e: e.tensor_copy(out=pswp[:], in_=P1), waits=[wl])
    thc = cf[:, CFC["theta"]:CFC["theta"] + 1]
    sgc = cf[:, CFC["sign"]:CFC["sign"] + 1]
    pm64c = cf[:, CFC["t1"] + 1:CFC["t1"] + 2]
    wl = vop(lambda e: e.tensor_single_scalar(out=thc, in_=pm64c, scalar=32.0,
                                              op=ALU.is_ge), waits=[w_pm64, wl])
    wl = vop(lambda e: e.scalar_tensor_tensor(out=thc, in0=thc, scalar=-32.0,
                                              in1=pm64c, op0=ALU.mult, op1=ALU.add),
             waits=[wl])
    w_thv = wl = vop(lambda e: e.tensor_single_scalar(
        out=thc, in_=thc, scalar=-math.log(10000.0) / 32.0, op=ALU.mult), waits=[wl])
    w_thc = sop(lambda e: e.activation(out=thc, in_=thc, func=AF.Exp), waits=[w_thv])
    wl = vop(lambda e: e.tensor_single_scalar(out=sgc, in_=pm64c, scalar=32.0,
                                              op=ALU.is_ge),
             waits=[w_pm64, wl, w_thc])
    w_sgc = vop(lambda e: e.tensor_scalar(out=sgc, in0=sgc, scalar1=2.0, scalar2=-1.0,
                                          op0=ALU.mult, op1=ALU.add), waits=[wl])
    vop(lambda e: e.memset(cbf[0:64, 1:2], 1.0))
    vop(lambda e: e.memset(cbf[64:128, 1:2], 0.0))
    vop(lambda e: e.memset(cbf[0:64, 2:3], 0.0))
    w_cbf = vop(lambda e: e.memset(cbf[64:128, 2:3], 1.0))

    for nm, col in (("bq0", SBC["bq0"]), ("bk0", SBC["bk0"]), ("bv0", SBC["bv0"]),
                    ("bo0", SBC["bo0"]), ("b2f", SBC["b2f"])):
        bh = dt[nm].rearrange("(c p) -> c p", p=128)
        for c in range(8):
            dma_s(sbias[:, col + c:col + c + 1], bh[c:c + 1, :])
    b1h = dt["b1f"].rearrange("(c p) -> c p", p=128)
    for c in range(32):
        dma_s(b1s[:, c:c + 1], b1h[c:c + 1, :])
    for nm, key in (("sq0", "sq"), ("sk0", "sk"), ("sv0", "sv"), ("so0", "so")):
        dma_s(cf[0:1, CFC[key]:CFC[key] + 1], dt[nm])
    w_rw = dma_s(rwsb[:], dt["rw0"])
    w_params = (sLD, sLD.n)

    w_onesf = (sV, sV.n)
    prev_read = NONE
    w_scol = {}
    for key in ("sq", "sk", "sv", "so"):
        wt = top(lambda e, k=key: e.matmul(psD[:, 0:1], onesf[0:1, :],
                                           cf[0:1, CFC[k]:CFC[k] + 1],
                                           start=True, stop=True),
                 waits=[w_params, w_onesf, prev_read])
        wsc = sop(lambda e, k=key: e.copy(out=cf[:, CFC[k]:CFC[k] + 1],
                                          in_=psD[:, 0:1]), waits=[wt])
        w_scol[key] = wsc
        prev_read = wsc
    w_scol_done = prev_read

    w_b1 = vop(lambda e: e.tensor_copy(out=scf_f16[:, 2048:2056], in_=sbias[:, 32:40]),
               waits=[w_params])
    w_bqf = vop(lambda e: e.tensor_copy(out=sbias[:, 40:48], in_=scf_f16[:, 2048:2056]),
                waits=[w_b1])

    wa_free = [NONE, NONE, NONE]
    wb_free = [NONE, NONE]

    def emit_ln(layer, deps, psd_free, deps_tiles=None):
        xsrc = x_in if layer == 0 else x_out
        state = dict(last_xn=NONE, w_le=NONE, w_rmm_last=NONE,
                     w_mmm=NONE, w_vmm=NONE)
        arwin = ar0[:, 0:8192].bitcast(F32)
        bc_free = {0: psd_free, 1: psd_free}
        win_free = {0: NONE, 1: NONE}
        if deps_tiles is None:
            deps_tiles = [list(deps)] * 4
        p2 = {}

        def phase1(tt):
            t0 = tt * 512
            win = xw[:] if tt % 2 == 0 else arwin
            bc = psA if tt % 2 == 0 else psC
            wr = win.rearrange("p (dc t) -> p dc t", t=512)
            w_lo = dma_s(wr[:, 0:4, :], xsrc[:, 0:4, t0:t0 + 512],
                         waits=list(deps_tiles[tt]) + [win_free[tt % 2]],
                         ctr=sXW)
            w_hi = dma_s(wr[:, 4:8, :], xsrc[:, 4:8, t0:t0 + 512],
                         waits=list(deps_tiles[tt]) + [win_free[tt % 2]],
                         ctr=sLD)
            for dc in range(8):
                w_ld = w_lo if dc < 4 else w_hi
                state["w_mmm"] = top(lambda e, dc=dc, win=win: e.matmul(
                    psD[0:1, :], onesf[:, 0:1], win[:, dc * 512:(dc + 1) * 512],
                    start=(dc == 0), stop=(dc == 7)),
                    waits=[w_ld, psd_free, w_onesf])
                w_sq = sop(lambda e, dc=dc, win=win: e.activation(
                    out=tbf[:, 512:1024], in_=win[:, dc * 512:(dc + 1) * 512],
                    func=AF.Square), waits=[w_ld, state["w_vmm"]])
                state["w_vmm"] = top(lambda e, dc=dc: e.matmul(
                    psE[0:1, :], cbf[:, 0:1], tbf[:, 512:1024],
                    start=(dc == 0), stop=(dc == 7)),
                    waits=[w_sq, psd_free, w_cbf])
            m_row = scf[0:1, 0:512]
            v_row = scf[0:1, 512:1024]
            mm_row = scf[0:1, 1024:1536]
            w_m = vop(lambda e: e.tensor_single_scalar(out=m_row, in_=psD[0:1, :],
                                                       scalar=1.0 / D, op=ALU.mult),
                      waits=[state["w_mmm"]])
            w_v1 = vop(lambda e: e.tensor_single_scalar(out=v_row, in_=psE[0:1, :],
                                                        scalar=1.0 / D, op=ALU.mult),
                       waits=[state["w_vmm"]])
            w_v2 = vop(lambda e: e.tensor_tensor(out=mm_row, in0=m_row, in1=m_row,
                                                 op=ALU.mult), waits=[w_m])
            w_v3 = vop(lambda e: e.tensor_tensor(out=v_row, in0=v_row, in1=mm_row,
                                                 op=ALU.subtract), waits=[w_v1, w_v2])
            w_v4 = sop(lambda e: e.activation(out=v_row, in_=v_row, func=AF.Ln,
                                              bias=cf[0:1, CFC["eps5"]:CFC["eps5"] + 1]),
                       waits=[w_v3])
            w_v5 = sop(lambda e: e.activation(out=v_row, in_=v_row, func=AF.Exp,
                                              scale=-0.5), waits=[w_v4])
            w_bm = top(lambda e, bc=bc: e.matmul(
                bc[:, 0:512], onesf[0:1, :], m_row,
                start=True, stop=True),
                waits=[w_m, w_v5, psd_free, bc_free[tt % 2]])
            w_br = top(lambda e, bc=bc: e.matmul(
                bc[:, 512:1024], onesf[0:1, :], v_row,
                start=True, stop=True),
                waits=[w_v5, psd_free, bc_free[tt % 2]])
            p2[tt] = (t0, win, bc, w_bm, w_br)

        def phase2(tt):
            t0, win, bc, w_bm, w_br = p2[tt]
            w_xnf = []
            for dc in range(8):
                tmpf = scf[:, 1536:2048]
                w_t1 = vop(lambda e, dc=dc, win=win, bc=bc: e.tensor_tensor(
                    out=tmpf, in0=win[:, dc * 512:(dc + 1) * 512], in1=bc[:, 0:512],
                    op=ALU.subtract),
                    waits=[w_bm, w_xnf[-1] if w_xnf else NONE, state["w_le"]])
                w_t2 = vop(lambda e, dc=dc, t0=t0, bc=bc: e.tensor_tensor(
                    out=xn[:, dc * S + t0: dc * S + t0 + 512], in0=tmpf,
                    in1=bc[:, 512:1024], op=ALU.mult), waits=[w_br, w_t1])
                state["last_xn"] = w_t2
                if layer == 0:
                    w_t3 = vop(lambda e, dc=dc, bc=bc: e.tensor_tensor(
                        out=aat_f32[:, dc * 512:(dc + 1) * 512], in0=tmpf,
                        in1=bc[:, 512:1024], op=ALU.mult), waits=[w_t1])
                    w_xnf.append(w_t3)
                    state["last_xn"] = w_t3
                else:
                    w_xnf.append(w_t2)
            bc_free[tt % 2] = state["last_xn"]
            win_free[tt % 2] = state["last_xn"]
            if layer == 0:
                w_rmm = NONE
                for qs in range(4):
                    for dc in range(8):
                        w_rmm = top(lambda e, qs=qs, dc=dc: e.matmul(
                            psB[:, qs * 8:(qs + 1) * 8],
                            aat_f32[:, dc * 512 + qs * 128: dc * 512 + qs * 128 + 128],
                            rwsb[:, dc * 8:(dc + 1) * 8],
                            start=(dc == 0), stop=(dc == 7)),
                            waits=[w_xnf[dc], w_rw, state["w_le"], psd_free])
                state["w_le"] = sop(lambda e, tt=tt: e.copy(
                    out=gts[:, tt * 32:(tt + 1) * 32],
                    in_=psB[:, 0:32]), waits=[w_rmm])
                state["w_rmm_last"] = w_rmm

        for tt in range(4):
            phase1(tt)
            if tt > 0:
                phase2(tt - 1)
        phase2(3)
        return state["last_xn"], state["w_le"], state["w_rmm_last"]

    def emit_router(w_logits):
        L = gts[:, 0:128].rearrange("p (c e) -> p c e", e=8)
        m1 = gts[:, 128:144]; m2 = gts[:, 144:160]
        i1 = gts[:, 160:176]; i2 = gts[:, 176:192]
        tmp = gts[:, 192:208]; tmp2 = gts[:, 208:224]
        g1c = gts[:, 224:240]; t3 = gts[:, 240:256]
        wl = vop(lambda e: e.tensor_copy(out=m1, in_=L[:, :, 0]), waits=[w_logits])
        wl = vop(lambda e: e.memset(m2, -1e30), waits=[wl])
        wl = vop(lambda e: e.memset(i1, 0.0), waits=[wl])
        wl = vop(lambda e: e.memset(i2, 0.0), waits=[wl])
        for ei in range(1, 8):
            pe = L[:, :, ei]
            wl = vop(lambda e, pe=pe: e.tensor_tensor(out=tmp, in0=pe, in1=m1,
                                                      op=ALU.is_gt), waits=[wl])
            wl = vop(lambda e, pe=pe: e.tensor_tensor(out=tmp2, in0=pe, in1=m2,
                                                      op=ALU.is_gt), waits=[wl])
            wl = vop(lambda e, ei=ei: e.tensor_scalar(out=t3, in0=i2, scalar1=-1.0,
                                                      scalar2=float(ei), op0=ALU.mult,
                                                      op1=ALU.add), waits=[wl])
            wl = vop(lambda e: e.tensor_tensor(out=t3, in0=t3, in1=tmp2, op=ALU.mult),
                     waits=[wl])
            wl = vop(lambda e: e.tensor_tensor(out=i2, in0=i2, in1=t3, op=ALU.add),
                     waits=[wl])
            wl = vop(lambda e: e.tensor_tensor(out=t3, in0=i1, in1=i2,
                                               op=ALU.subtract), waits=[wl])
            wl = vop(lambda e: e.tensor_tensor(out=t3, in0=t3, in1=tmp, op=ALU.mult),
                     waits=[wl])
            wl = vop(lambda e: e.tensor_tensor(out=i2, in0=i2, in1=t3, op=ALU.add),
                     waits=[wl])
            wl = vop(lambda e, pe=pe: e.tensor_tensor(out=t3, in0=m1, in1=pe,
                                                      op=ALU.min), waits=[wl])
            wl = vop(lambda e: e.tensor_tensor(out=m2, in0=m2, in1=t3, op=ALU.max),
                     waits=[wl])
            wl = vop(lambda e, ei=ei: e.tensor_scalar(out=t3, in0=i1, scalar1=-1.0,
                                                      scalar2=float(ei), op0=ALU.mult,
                                                      op1=ALU.add), waits=[wl])
            wl = vop(lambda e: e.tensor_tensor(out=t3, in0=t3, in1=tmp, op=ALU.mult),
                     waits=[wl])
            wl = vop(lambda e: e.tensor_tensor(out=i1, in0=i1, in1=t3, op=ALU.add),
                     waits=[wl])
            wl = vop(lambda e, pe=pe: e.tensor_tensor(out=m1, in0=m1, in1=pe,
                                                      op=ALU.max), waits=[wl])
        wl = vop(lambda e: e.tensor_tensor(out=tmp, in0=m2, in1=m1, op=ALU.subtract),
                 waits=[wl])
        w_e = sop(lambda e: e.activation(out=tmp, in_=tmp, func=AF.Exp), waits=[wl])
        wl = vop(lambda e: e.tensor_scalar(out=tmp, in0=tmp, scalar1=1.0, scalar2=None,
                                           op0=ALU.add), waits=[w_e, wl])
        wl = vop(lambda e: e.reciprocal(out=g1c, in_=tmp), waits=[wl])
        for ei in range(8):
            ge = gts[:, 256 + ei * 16: 256 + (ei + 1) * 16]
            wl = vop(lambda e, ei=ei: e.tensor_single_scalar(out=tmp2, in_=i1,
                                                             scalar=float(ei),
                                                             op=ALU.is_equal),
                     waits=[wl])
            wl = vop(lambda e, ge=ge: e.tensor_tensor(out=ge, in0=g1c, in1=tmp2,
                                                      op=ALU.mult), waits=[wl])
            wl = vop(lambda e, ei=ei: e.tensor_single_scalar(out=tmp2, in_=i2,
                                                             scalar=float(ei),
                                                             op=ALU.is_equal),
                     waits=[wl])
            wl = vop(lambda e: e.tensor_scalar(out=t3, in0=g1c, scalar1=-1.0,
                                               scalar2=1.0, op0=ALU.mult,
                                               op1=ALU.add), waits=[wl])
            wl = vop(lambda e: e.tensor_tensor(out=tmp2, in0=tmp2, in1=t3,
                                               op=ALU.mult), waits=[wl])
            wl = vop(lambda e, ge=ge: e.tensor_tensor(out=ge, in0=ge, in1=tmp2,
                                                      op=ALU.add), waits=[wl])
        return wl

    wa_rot = [0]

    def wa_load(src_ap, extra=()):
        slot = wa_rot[0]
        wa_rot[0] = (slot + 1) % 3
        out = wsa[:, slot * 1024:(slot + 1) * 1024]
        return slot, dma_g(out, src_ap, waits=[wa_free[slot]] + list(extra),
                           ctr=sWA[slot])

    def emit_kv(layer, w_xn, psfree):
        sfx = str(layer)
        krope_last = NONE
        prev_ev = [psfree, psfree]
        kev = {}
        for oc in range(8):
            wslot, w_w = wa_load(dt["wk" + sfx][oc])
            for tt in range(4):
                half = (oc * 4 + tt) % 2
                pr = psB[:, half * 512:(half + 1) * 512]
                w_mm = NONE
                for dc in range(8):
                    w_mm = top(lambda e, dc=dc, tt=tt, pr=pr, slot=wslot: e.matmul(
                        pr, wsa[:, slot * 1024 + dc * 128: slot * 1024 + (dc + 1) * 128],
                        xn[:, dc * S + tt * 512: dc * S + tt * 512 + 512],
                        start=(dc == 0), stop=(dc == 7)),
                        waits=[w_w, w_xn, prev_ev[half]])
                if layer == 0:
                    ev = vop(lambda e, oc=oc, tt=tt, pr=pr: e.tensor_scalar(
                        out=ar0[:, oc * S + tt * 512: oc * S + tt * 512 + 512],
                        in0=pr, scalar1=cf[:, CFC["sk"]:CFC["sk"] + 1],
                        scalar2=sbias[:, SBC["bk0"] + oc:SBC["bk0"] + oc + 1],
                        op0=ALU.mult, op1=ALU.add),
                        waits=[w_mm, w_scol_done, w_params])
                else:
                    ev = sop(lambda e, oc=oc, tt=tt, pr=pr: e.copy(
                        out=ar0[:, oc * S + tt * 512: oc * S + tt * 512 + 512],
                        in_=pr), waits=[w_mm])
                prev_ev[half] = ev
                kev[(oc, tt)] = ev
            wa_free[wslot] = w_mm
        klast = kev[(7, 3)]
        if layer == 1:
            for oc in range(8):
                for tt in range(4):
                    ksl = ar0[:, oc * S + tt * 512: oc * S + tt * 512 + 512]
                    w_sw = top(lambda e, ksl=ksl: e.matmul(
                        psC[:, 0:512], pswp[:], ksl, start=True, stop=True),
                        waits=[kev[(oc, tt)], w_pswap, krope_last, psfree])
                    w_r1 = vop(lambda e, ksl=ksl, tt=tt: e.tensor_tensor(
                        out=tbf[:, 0:512], in0=ksl,
                        in1=uni[:, tt * 512:(tt + 1) * 512], op=ALU.mult),
                        waits=[w_cc_g[0], kev[(oc, tt)], krope_last])
                    w_r2 = vop(lambda e, tt=tt: e.tensor_tensor(
                        out=tbf[:, 512:1024], in0=psC[:, 0:512],
                        in1=uni[:, 2048 + tt * 512: 2048 + (tt + 1) * 512],
                        op=ALU.mult), waits=[w_ss_g[0], w_sw])
                    krope_last = vop(lambda e, ksl=ksl: e.tensor_tensor(
                        out=ksl, in0=tbf[:, 0:512], in1=tbf[:, 512:1024],
                        op=ALU.add), waits=[w_r1, w_r2])
            klast = krope_last
        w_k0 = vop(lambda e: e.memset(cf[0:1, CFC["kmax"]:CFC["kmax"] + 1], 0.0))
        wprev = w_k0
        rd_g = [NONE, NONE]
        sq_g = [NONE, NONE]
        i = 0
        for oc in range(8):
            for tt in range(4):
                th = i % 2
                w_sq = sop(lambda e, oc=oc, tt=tt, th=th: e.activation(
                    out=tbf[:, th * 512:(th + 1) * 512],
                    in_=ar0[:, oc * S + tt * 512: oc * S + tt * 512 + 512],
                    func=AF.Square), waits=[klast, sq_g[th]])
                for par in range(2):
                    bank = (i * 2 + par) % 2
                    pr = psD[0:1, :] if bank == 0 else psE[0:1, :]
                    w_mm = top(lambda e, par=par, th=th, pr=pr: e.matmul(
                        pr, cbf[par * 64:(par + 1) * 64, 1 + par:2 + par],
                        tbf[par * 64:(par + 1) * 64, th * 512:(th + 1) * 512],
                        start=True, stop=True), waits=[w_sq, rd_g[bank], psfree])
                    sq_g[th] = w_mm
                    w_rm = vop(lambda e, pr=pr: e.reduce_max(
                        out=cf[0:1, CFC["t0"]:CFC["t0"] + 1], in_=pr,
                        axis=AX.X), waits=[w_mm, wprev])
                    rd_g[bank] = w_rm
                    wprev = vop(lambda e: e.tensor_tensor(
                        out=cf[0:1, CFC["kmax"]:CFC["kmax"] + 1],
                        in0=cf[0:1, CFC["kmax"]:CFC["kmax"] + 1],
                        in1=cf[0:1, CFC["t0"]:CFC["t0"] + 1], op=ALU.max),
                        waits=[w_rm])
                i += 1
        w_kmax = wprev
        w_o65 = vop(lambda e: e.memset(ar1[:, 64:16640:65], 1.0), waits=[psfree])
        w_wv = dma_g(aat[:, 0:8192], dt["wv" + sfx], waits=[psfree])
        w_bv = [NONE, NONE]
        if layer == 0:
            w_bvr = dma_s(scf[0:1, 0:1024],
                          dt["bv0"].rearrange("(o d) -> o d", o=1), waits=[w_xn])
            prev = NONE
            for ot in range(2):
                wt = top(lambda e, ot=ot: e.matmul(
                    psC[:, 0:512], onesf[0:1, :], scf[0:1, ot * 512:(ot + 1) * 512],
                    start=True, stop=True), waits=[w_bvr, prev, psfree])
                wc = sop(lambda e, ot=ot: e.copy(
                    out=scf[:, 1024 + ot * 512: 1024 + (ot + 1) * 512],
                    in_=psC[:, 0:512]), waits=[wt])
                w_bv[ot] = wc
                prev = wc
        prev_ev = [psfree, psfree]
        vev = NONE
        for ot in range(2):
            for tc in range(16):
                half = (ot * 16 + tc) % 2
                pr = psA[:, half * 512:(half + 1) * 512]
                w_mm = NONE
                for dc in range(8):
                    w_mm = top(lambda e, dc=dc, tc=tc, ot=ot, pr=pr: e.matmul(
                        pr, xn[:, dc * S + tc * 128: dc * S + (tc + 1) * 128],
                        aat[:, dc * 1024 + ot * 512: dc * 1024 + ot * 512 + 512],
                        start=(dc == 0), stop=(dc == 7)),
                        waits=[w_wv, w_xn, prev_ev[half]])
                vout = ar1[:, tc * 1040: (tc + 1) * 1040].rearrange(
                    "p (h c) -> p h c", c=65)[:, ot * 8:(ot + 1) * 8, 0:64]
                vin = pr.rearrange("p (h c) -> p h c", c=64)
                if layer == 0:
                    vev = vop(lambda e, vout=vout, vin=vin, ot=ot: e.scalar_tensor_tensor(
                        out=vout, in0=vin, scalar=cf[:, CFC["sv"]:CFC["sv"] + 1],
                        in1=scf[:, 1024 + ot * 512: 1024 + (ot + 1) * 512].rearrange(
                            "p (h c) -> p h c", c=64),
                        op0=ALU.mult, op1=ALU.add),
                        waits=[w_mm, w_scol_done, w_bv[ot], w_o65])
                else:
                    vev = vop(lambda e, vout=vout, vin=vin: e.tensor_copy(
                        out=vout, in_=vin), waits=[w_mm, w_o65])
                prev_ev[half] = vev
        return klast, w_kmax, vev

    w_cc_g = [NONE]
    w_ss_g = [NONE]

    def emit_rope_tables(deps):
        wlast = NONE
        bank = [psA[:, 0:512], psA[:, 512:1024], psD[:, 0:512], psE[:, 0:512]]
        bguard = [gA, gA, gD, gE]
        w_bs = []
        for tt in range(4):
            w_b = top(lambda e, tt=tt: e.matmul(
                bank[tt], onesf[0:1, :], itr[0:1, tt * 512:(tt + 1) * 512],
                start=True, stop=True),
                waits=[w_itr] + list(deps) + bguard[tt].for_write())
            bguard[tt].wrote(w_b)
            w_bs.append(w_b)
        for tt in range(4):
            w_b = w_bs[tt]
            w_a = vop(lambda e, tt=tt: e.tensor_scalar(
                out=scf[:, 512:1024], in0=bank[tt],
                scalar1=thc, scalar2=None,
                op0=ALU.mult), waits=[w_b, w_thc, wlast])
            bguard[tt].read(w_a)
            w_s_prev = NONE
            for which in range(2):
                shift = 0.25 if which == 1 else 0.0
                dst = uni[:, (2048 if which == 0 else 0) + tt * 512:][:, 0:512]
                w_n = vop(lambda e, shift=shift: e.tensor_scalar(
                    out=scf[:, 1024:1536], in0=scf[:, 512:1024],
                    scalar1=1.0 / (2 * math.pi), scalar2=shift,
                    op0=ALU.mult, op1=ALU.add), waits=[w_a, wlast, w_s_prev])
                w_k = vop(lambda e: e.tensor_copy(out=scf_i32[:, 1536:2048],
                                                  in_=scf[:, 1024:1536]),
                          waits=[w_n])
                w_kf = vop(lambda e: e.tensor_copy(out=gts[:, 0:512],
                                                   in_=scf_i32[:, 1536:2048]),
                           waits=[w_k])
                w_f = vop(lambda e: e.tensor_tensor(out=scf[:, 1024:1536],
                                                    in0=scf[:, 1024:1536],
                                                    in1=gts[:, 0:512],
                                                    op=ALU.subtract), waits=[w_kf])
                w_w1 = vop(lambda e: e.tensor_single_scalar(
                    out=gts[:, 0:512], in_=scf[:, 1024:1536], scalar=0.5,
                    op=ALU.is_ge), waits=[w_f])
                w_w2 = vop(lambda e: e.tensor_tensor(
                    out=scf[:, 1024:1536], in0=scf[:, 1024:1536],
                    in1=gts[:, 0:512], op=ALU.subtract), waits=[w_w1])
                w_s = sop(lambda e, dst=dst: e.activation(
                    out=dst, in_=scf[:, 1024:1536], func=AF.Sin,
                    scale=2 * math.pi), waits=[w_w2])
                w_s_prev = w_s
                if which == 0:
                    wlast = vop(lambda e, dst=dst: e.tensor_scalar(
                        out=dst, in0=dst, scalar1=sgc, scalar2=None, op0=ALU.mult),
                        waits=[w_s, w_sgc])
                    w_ss_g[0] = wlast
                else:
                    wlast = w_s
                    w_cc_g[0] = wlast
        return w_cc_g[0], w_ss_g[0]

    def emit_attention(layer, w_kdone, w_kmax, w_vdone, w_xn, w_masks, qrope_init,
                       filler=None, probe_attn=probe_attn):
        sfx = str(layer)
        xsrc = x_in if layer == 0 else x_out
        ctx0 = 0
        q0 = 8192
        p0 = 12288
        w_pv_slot = [NONE, NONE, NONE, NONE]
        qrope_last = qrope_init
        w_xw_store = {}
        w_attn_done = NONE

        def pump(n=1):
            if filler is not None:
                for _ in range(n):
                    if not filler():
                        break

        for qt in range(NQT):
            qev = {}
            for oc in range(8):
                wslot, w_w = wa_load(dt["wq" + sfx][oc])
                w_mm = NONE
                for dc in range(8):
                    w_mm = top(lambda e, dc=dc, qt=qt, slot=wslot: e.matmul(
                        psE[:, 0:512],
                        wsa[:, slot * 1024 + dc * 128: slot * 1024 + (dc + 1) * 128],
                        xn[:, dc * S + qt * 512: dc * S + qt * 512 + 512],
                        start=(dc == 0), stop=(dc == 7)),
                        waits=[w_w, w_xn] + (gE.for_write() if dc == 0 else []))
                wa_free[wslot] = w_mm
                gE.wrote(w_mm)
                if layer == 0:
                    ev = vop(lambda e, oc=oc: e.tensor_scalar(
                        out=aat[:, q0 + oc * 512: q0 + (oc + 1) * 512],
                        in0=psE[:, 0:512], scalar1=cf[:, CFC["sq"]:CFC["sq"] + 1],
                        scalar2=sbias[:, SBC["bq0"] + oc:SBC["bq0"] + oc + 1],
                        op0=ALU.mult, op1=ALU.add),
                        waits=[w_mm, w_scol_done, w_params])
                else:
                    ev = sop(lambda e, oc=oc: e.copy(
                        out=aat[:, q0 + oc * 512: q0 + (oc + 1) * 512],
                        in_=psE[:, 0:512]), waits=[w_mm])
                gE.read(ev)
                qev[oc] = ev
                pump(1)
            if layer == 1:
                for oc in range(8):
                    qsl = aat[:, q0 + oc * 512: q0 + (oc + 1) * 512]
                    w_sw = top(lambda e, qsl=qsl: e.matmul(
                        psC[:, 0:512], pswp[:], qsl, start=True, stop=True),
                        waits=[qev[oc], w_pswap, qrope_last] + gCB.for_write())
                    gCB.wrote(w_sw)
                    w_r1 = vop(lambda e, qsl=qsl, qt=qt: e.tensor_tensor(
                        out=tbf[:, 0:512], in0=qsl,
                        in1=uni[:, qt * 512:(qt + 1) * 512], op=ALU.mult),
                        waits=[w_cc_g[0], qev[oc], qrope_last])
                    w_r2 = vop(lambda e, qt=qt: e.tensor_tensor(
                        out=tbf[:, 512:1024], in0=psC[:, 0:512],
                        in1=uni[:, 2048 + qt * 512: 2048 + (qt + 1) * 512],
                        op=ALU.mult), waits=[w_ss_g[0], w_sw])
                    gCB.read(w_r2)
                    qrope_last = vop(lambda e, qsl=qsl: e.tensor_tensor(
                        out=qsl, in0=tbf[:, 0:512], in1=tbf[:, 512:1024],
                        op=ALU.add), waits=[w_r1, w_r2])
                    qev[oc] = qrope_last
            qall = qev[7] if layer == 0 else qrope_last
            w_qm = vop(lambda e: e.memset(cf[0:1, CFC["qmax"]:CFC["qmax"] + 1], 0.0),
                       waits=[qall])
            wprev = w_qm
            rd_g = {0: NONE, 1: NONE}
            sq_g = [NONE, NONE]
            for oc in range(8):
                th = oc % 2
                w_sq = sop(lambda e, oc=oc, th=th: e.activation(
                    out=tbf[:, th * 512:(th + 1) * 512],
                    in_=aat[:, q0 + oc * 512: q0 + (oc + 1) * 512],
                    func=AF.Square), waits=[qall, sq_g[th]])
                for par in range(2):
                    bank = par
                    pr = psD[0:1, :] if bank == 0 else psE[0:1, :]
                    w_mm = top(lambda e, par=par, th=th, pr=pr: e.matmul(
                        pr, cbf[par * 64:(par + 1) * 64, 1 + par:2 + par],
                        tbf[par * 64:(par + 1) * 64, th * 512:(th + 1) * 512],
                        start=True, stop=True),
                        waits=[w_sq, rd_g[bank]] +
                              (gD.for_write() if (oc == 0 and par == 0) else []) +
                              (gE.for_write() if (oc == 0 and par == 1) else []))
                    sq_g[th] = w_mm
                    w_rm = vop(lambda e, pr=pr: e.reduce_max(
                        out=cf[0:1, CFC["t0"]:CFC["t0"] + 1], in_=pr,
                        axis=AX.X), waits=[w_mm, wprev])
                    rd_g[bank] = w_rm
                    wprev = vop(lambda e: e.tensor_tensor(
                        out=cf[0:1, CFC["qmax"]:CFC["qmax"] + 1],
                        in0=cf[0:1, CFC["qmax"]:CFC["qmax"] + 1],
                        in1=cf[0:1, CFC["t0"]:CFC["t0"] + 1], op=ALU.max),
                        waits=[w_rm])
                pump(1)
            w_c1 = vop(lambda e: e.tensor_tensor(
                out=cf[0:1, CFC["c2"]:CFC["c2"] + 1],
                in0=cf[0:1, CFC["qmax"]:CFC["qmax"] + 1],
                in1=cf[0:1, CFC["kmax"]:CFC["kmax"] + 1], op=ALU.mult),
                waits=[wprev, w_kmax])
            w_c2 = sop(lambda e: e.activation(out=cf[0:1, CFC["c2"]:CFC["c2"] + 1],
                                              in_=cf[0:1, CFC["c2"]:CFC["c2"] + 1],
                                              func=AF.Ln,
                                              bias=cf[0:1, CFC["eps"]:CFC["eps"] + 1]),
                       waits=[w_c1])
            w_c3 = sop(lambda e: e.activation(out=cf[0:1, CFC["c2"]:CFC["c2"] + 1],
                                              in_=cf[0:1, CFC["c2"]:CFC["c2"] + 1],
                                              func=AF.Exp, scale=0.5), waits=[w_c2])
            w_c4a = vop(lambda e: e.tensor_scalar(
                out=cf[0:1, CFC["c2"]:CFC["c2"] + 1],
                in0=cf[0:1, CFC["c2"]:CFC["c2"] + 1], scalar1=0.125, scalar2=-40.0,
                op0=ALU.mult, op1=ALU.add), waits=[w_c3])
            w_c4b = vop(lambda e: e.tensor_scalar_max(
                out=cf[0:1, CFC["c2"]:CFC["c2"] + 1],
                in0=cf[0:1, CFC["c2"]:CFC["c2"] + 1], scalar1=0.0), waits=[w_c4a])
            w_c4 = vop(lambda e: e.tensor_single_scalar(
                out=cf[0:1, CFC["c2"]:CFC["c2"] + 1],
                in_=cf[0:1, CFC["c2"]:CFC["c2"] + 1], scalar=-1.0, op=ALU.mult),
                waits=[w_c4b])
            w_c5 = top(lambda e: e.matmul(psD[:, 0:1], onesf[0:1, :],
                                          cf[0:1, CFC["c2"]:CFC["c2"] + 1],
                                          start=True, stop=True),
                       waits=[w_c4, rd_g[0]] + gD.for_write())
            gD.wrote(w_c5)
            w_cb = sop(lambda e: e.copy(out=cf[:, CFC["cbias"]:CFC["cbias"] + 1],
                                        in_=psD[:, 0:1]), waits=[w_c5])
            gD.read(w_cb)
            nkc = 16 if layer == 0 else (qt + 1) * 4
            nbat = nkc // 2
            pending_den = [None]
            psCB = psC[:, 0:512]
            psDEN = psC[:, 512:1024]

            def emit_den(h, ctxps, gX):
                w_pv = gX.w
                w_d1 = sop(lambda e, ctxps=ctxps: e.activation(
                    out=scf[64:65, 0:512], in_=ctxps[64:65, :], func=AF.Ln,
                    bias=cf[64:65, CFC["eps"]:CFC["eps"] + 1]), waits=[w_pv])
                w_d2 = sop(lambda e: e.activation(
                    out=scf[64:65, 0:512], in_=scf[64:65, 0:512], func=AF.Exp,
                    scale=-1.0), waits=[w_d1])
                w_d3 = top(lambda e: e.matmul(psDEN, onesf[64:65, :],
                                              scf[64:65, 0:512],
                                              start=True, stop=True),
                           waits=[w_d2] + gC1.for_write())
                gC1.wrote(w_d3)
                w_d4 = sop(lambda e: e.copy(out=scf[:, 512:1024], in_=psDEN),
                           waits=[w_d3])
                gC1.read(w_d4)
                w_ce = vop(lambda e, h=h, ctxps=ctxps: e.tensor_tensor(
                    out=aat[0:64, ctx0 + h * 512: ctx0 + (h + 1) * 512],
                    in0=ctxps[0:64, :], in1=scf[0:64, 512:1024], op=ALU.mult),
                    waits=[w_d4, w_pv])
                gX.read(w_d1)
                gX.read(w_ce)
                gC1.read(w_ce)
                return w_ce

            w_ctx_last = NONE
            pending_pv = [None]

            def flush_pv():
                if pending_pv[0] is not None:
                    pending_pv[0]()
                    pending_pv[0] = None

            for hp in range(NH // 2):
                hA = 2 * hp
                hB = 2 * hp + 1
                oc = hp
                for b in range(nbat):
                    w_smm = {}
                    for par, g in ((0, gA), (1, gB)):
                        rows = slice(par * 64, (par + 1) * 64)
                        w_m = NONE
                        for j in range(2):
                            kc = b * 2 + j
                            w_m = top(lambda e, kc=kc, j=j, oc=oc, rows=rows,
                                      par=par: e.matmul(
                                (psA if par == 0 else psB)[:, j * 512:(j + 1) * 512],
                                ar0[rows, oc * S + kc * 128: oc * S + (kc + 1) * 128],
                                aat[rows, q0 + oc * 512: q0 + (oc + 1) * 512],
                                start=True, stop=True),
                                waits=[w_kdone, qall] +
                                      (g.for_write() if j == 0 else []))
                        g.wrote(w_m)
                        w_smm[par] = w_m
                    flush_pv()
                    if b == 0 and pending_den[0] is not None:
                        w_ctx_last = pending_den[0]()
                        pending_den[0] = None
                    pump(1)
                    w_ready = {}
                    for par, g in ((0, gA), (1, gB)):
                        slot = par * 2 + (b % 2)
                        pbank = psA if par == 0 else psB
                        w_exp = sop(lambda e, pbank=pbank, slot=slot: e.activation(
                            out=aat[:, p0 + slot * 1024: p0 + slot * 1024 + 1024],
                            in_=pbank[:, 0:1024], func=AF.Exp,
                            bias=cf[:, CFC["cbias"]:CFC["cbias"] + 1], scale=0.125),
                            waits=[w_smm[par], w_cb, w_pv_slot[slot]])
                        g.read(w_exp)
                        w_r = w_exp
                        if layer == 1:
                            for j in range(2):
                                kc = b * 2 + j
                                off = kc - (nkc - 4)
                                if off >= 0:
                                    w_r = vop(lambda e, slot=slot, j=j, off=off:
                                              e.tensor_tensor(
                                        out=aat[:, p0 + slot * 1024 + j * 512:
                                                p0 + slot * 1024 + (j + 1) * 512],
                                        in0=aat[:, p0 + slot * 1024 + j * 512:
                                                p0 + slot * 1024 + (j + 1) * 512],
                                        in1=uni[:, 4096 + off * 512:
                                                4096 + (off + 1) * 512],
                                        op=ALU.mult), waits=[w_exp, w_masks])
                        w_ready[par] = w_r

                    def emit_pv(b=b, w_ready=w_ready, nkc=nkc, hA=hA, hB=hB):
                        for par, gX, ctxps, h in ((0, gD, psD[0:65, :], hA),
                                                  (1, gCB, psCB[0:65, :]
                                                   if False else psC[0:65, 0:512],
                                                   hB)):
                            slot = par * 2 + (b % 2)
                            w_pv = NONE
                            for j in range(2):
                                kc = b * 2 + j
                                w_pv = top(lambda e, kc=kc, j=j, h=h, slot=slot,
                                           nkc=nkc, ctxps=ctxps: e.matmul(
                                    ctxps,
                                    ar1[:, kc * 1040 + h * 65:
                                        kc * 1040 + (h + 1) * 65],
                                    aat[:, p0 + slot * 1024 + j * 512:
                                        p0 + slot * 1024 + (j + 1) * 512],
                                    start=(kc == 0), stop=(kc == nkc - 1)),
                                    waits=[w_ready[par], w_vdone] +
                                          (gX.for_write() if kc == 0 else []))
                            w_pv_slot[slot] = w_pv
                            if b == nbat - 1:
                                gX.wrote(w_pv)

                    pending_pv[0] = emit_pv
                flush_pv()

                def den_pair(hA=hA, hB=hB):
                    emit_den(hA, psD[0:65, :], gD)
                    return emit_den(hB, psC[0:65, 0:512], gCB)

                pending_den[0] = den_pair
                pump(1)
            if pending_den[0] is not None:
                w_ctx_last = pending_den[0]()
                pending_den[0] = None
            for oc in range(8):
                slot = oc % 2
                wbsl = wsb[0:64, slot * 2048:(slot + 1) * 2048]
                w_w = dma_g(wbsl, dt["wo" + sfx + "t"][oc],
                            waits=[wb_free[slot]], ctr=sWB[slot])
                w_mm = NONE
                for h in range(NH):
                    w_mm = top(lambda e, h=h, slot=slot: e.matmul(
                        psE[:, 0:512],
                        wsb[0:64, slot * 2048 + h * 128: slot * 2048 + (h + 1) * 128],
                        aat[0:64, ctx0 + h * 512: ctx0 + (h + 1) * 512],
                        start=(h == 0), stop=(h == 15)),
                        waits=[w_w, w_ctx_last] +
                              (gE.for_write() if h == 0 else []))
                wb_free[slot] = w_mm
                gE.wrote(w_mm)
                xslot = oc % 4
                xsl = xw[:, xslot * 512:(xslot + 1) * 512]
                w_xl = dma_s(xsl, xsrc[:, oc, qt * 512:(qt + 1) * 512],
                             waits=[w_xn, w_xw_store.get((qt - 1, oc), NONE)],
                             ctr=sXW)
                if layer == 0:
                    w_a1 = vop(lambda e, xsl=xsl, oc=oc: e.tensor_scalar(
                        out=xsl, in0=xsl,
                        scalar1=sbias[:, SBC["bo0"] + oc:SBC["bo0"] + oc + 1],
                        scalar2=None, op0=ALU.add), waits=[w_xl, w_params])
                    w_a2 = vop(lambda e, xsl=xsl: e.scalar_tensor_tensor(
                        out=xsl, in0=psE[:, 0:512],
                        scalar=cf[:, CFC["so"]:CFC["so"] + 1], in1=xsl,
                        op0=ALU.mult, op1=ALU.add),
                        waits=[w_mm, w_a1, w_scol_done])
                else:
                    w_a2 = vop(lambda e, xsl=xsl: e.tensor_tensor(
                        out=xsl, in0=xsl, in1=psE[:, 0:512], op=ALU.add),
                        waits=[w_xl, w_mm])
                gE.read(w_a2)
                w_st = dma_s(x_out[:, oc, qt * 512:(qt + 1) * 512], xsl,
                             waits=[w_a2], ctr=sST)
                w_xw_store[(qt, oc)] = w_st
                w_attn_done = w_st
                pump(1)
        return w_attn_done

    def make_moe_filler(w_gates, w_xn, w_ln_done):
        acc = aat_f32
        w1t = dt["w1m"]
        w2t = dt["w2m"]
        state = dict(done=NONE, x07=NONE)
        w_h_unit = {}
        w_attn_ref = [NONE]
        state_g = dict(gtr=NONE, fl=NONE)
        w_gelu_read = [NONE, NONE]
        fst = dict(state_g=state_g, w_gelu_read=w_gelu_read)

        def units():
            w_gelu_read[0] = w_attn_ref[0]
            w_gelu_read[1] = w_attn_ref[0]
            prev_psC = [w_attn_ref[0]]
            w_xupd = {}
            w_accev = {}
            w_gate_done = {}
            seq = [(tt, ei) for tt in range(2) for ei in range(E)]

            def gates_A(tt, ei):
                ge = gts[:, 256 + ei * 16: 256 + (ei + 1) * 16]
                w_tr = top(lambda e, ge=ge: e.transpose(
                    psD[0:16, 0:128], ge, ident[:]),
                    waits=[w_gates, w_id, state_g["gtr"], w_attn_ref[0]] +
                          gD.for_write())
                gD.wrote(w_tr)
                w_gc = sop(lambda e: e.copy(out=scf[0:16, 1536:1664],
                                            in_=psD[0:16, 0:128]),
                           waits=[w_tr])
                gD.read(w_gc)
                state_g["fl"] = dma_g(scf_bf[0:1, 0:2048],
                                      scf[0:16, 1536:1664],
                                      waits=[w_gc], ctr=sFL)

            def gates_B(tt, ei):
                gcol = (ei % 2) * 1024
                w_fl = state_g["fl"]
                w_gb = NONE
                for hf in range(2):
                    pr = psD if hf == 0 else psE
                    gX = gD if hf == 0 else gE
                    wt = top(lambda e, hf=hf, tt=tt, pr=pr: e.matmul(
                        pr[:, 0:512], cbf[0:1, 3:131],
                        scf_bf[0:1, tt * 1024 + hf * 512:
                               tt * 1024 + (hf + 1) * 512],
                        start=True, stop=True),
                        waits=[w_fl, state_g["gtr"]] + gX.for_write())
                    gX.wrote(wt)
                    w_gb = sop(lambda e, hf=hf, pr=pr, gcol=gcol: e.copy(
                        out=uni[:, gcol + hf * 512: gcol + (hf + 1) * 512],
                        in_=pr[:, 0:512]),
                        waits=[wt, w_pswap, w_gelu_read[0], w_gelu_read[1]])
                    gX.read(w_gb)
                state_g["gtr"] = w_gb
                w_gate_done[(tt, ei)] = w_gb

            def x_update(tt, dc):
                xslot = dc % 2
                xsl = gcopy[:, xslot * 1024: (xslot + 1) * 1024]
                w_xl = dma_s(xsl, x_out[:, dc, tt * 1024:(tt + 1) * 1024],
                             waits=[w_attn_ref[0],
                                    w_xupd.get((tt, dc - 2), NONE)],
                             ctr=sXW)
                w_ad = vop(lambda e, xsl=xsl, dc=dc: e.tensor_tensor(
                    out=xsl, in0=xsl, in1=acc[:, dc * 1024:(dc + 1) * 1024],
                    op=ALU.add), waits=[w_xl, w_accev[dc]])
                w_stx = dma_s(x_out[:, dc, tt * 1024:(tt + 1) * 1024], xsl,
                              waits=[w_ad], ctr=sST)
                w_xupd[(tt, dc)] = w_stx
                state["done"] = w_stx
                if (tt, dc) == (0, 7):
                    state["x07"] = w_stx

            gates_A(*seq[0])
            gates_B(*seq[0])
            for k, (tt, ei) in enumerate(seq):
                gcol = (ei % 2) * 1024
                yield ("start", k)
                for hc in range(16):
                    slot = hc % 2
                    wslot, w_w = wa_load(w1t[ei * 16 + hc])
                    pbank = psA if slot == 0 else psB
                    g = gA if slot == 0 else gB
                    harena = ar0 if hc % 2 == 0 else ar1
                    hcol = (hc // 2) * 2048
                    w_mm = NONE
                    for dc in range(8):
                        for half in range(2):
                            w_mm = top(lambda e, dc=dc, half=half, tt=tt,
                                       slot=wslot, pbank=pbank: e.matmul(
                                pbank[:, half * 512:(half + 1) * 512],
                                wsa[:, slot * 1024 + dc * 128:
                                    slot * 1024 + (dc + 1) * 128],
                                xn[:, dc * S + tt * 1024 + half * 512:
                                   dc * S + tt * 1024 + half * 512 + 512],
                                start=(dc == 0), stop=(dc == 7)),
                                waits=[w_w, w_xn] +
                                      (g.for_write()
                                       if (dc == 0 and half == 0) else []))
                    wa_free[wslot] = w_mm
                    g.wrote(w_mm)
                    w_gl = sop(lambda e, pbank=pbank: e.activation(
                        out=uni_f32[:, 1024:2048], in_=pbank[:, 0:1024],
                        func=AF.Gelu),
                        waits=[w_mm, w_gate_done[(tt, ei)], w_gelu_read[0],
                               w_gelu_read[1]])
                    g.read(w_gl)
                    w_hm = vop(lambda e, harena=harena, hcol=hcol, gcol=gcol:
                               e.tensor_tensor(
                        out=harena[:, hcol:hcol + 1024],
                        in0=uni_f32[:, 1024:2048],
                        in1=uni[:, gcol:gcol + 1024], op=ALU.mult),
                        waits=[w_gl])
                    w_gelu_read[slot] = w_hm
                    w_h_unit[(ei, hc)] = w_hm
                    if hc % 2 == 1:
                        yield ("w1", k, hc)
                w2_loads = {}
                for dc in range(8):
                    slot = dc % 2
                    pr = psC if dc % 2 == 0 else psB
                    if dc == 0:
                        w2_loads[0] = dma_g(
                            wsb[:, 0:2048], w2t[ei, 0],
                            waits=[wb_free[0]], ctr=sWB[0])
                    if dc + 1 < 8:
                        w2_loads[dc + 1] = dma_g(
                            wsb[:, ((dc + 1) % 2) * 2048:
                                ((dc + 1) % 2 + 1) * 2048],
                            w2t[ei, dc + 1], waits=[wb_free[(dc + 1) % 2]],
                            ctr=sWB[(dc + 1) % 2])
                    w_w = w2_loads[dc]
                    w_mm = NONE
                    if dc % 2 == 0:
                        bank_w = [prev_psC[0]]
                    else:
                        bank_w = gB.for_write()
                    for hc in range(16):
                        harena = ar0 if hc % 2 == 0 else ar1
                        hcol = (hc // 2) * 2048
                        for half in range(2):
                            w_mm = top(lambda e, hc=hc, half=half, slot=slot,
                                       harena=harena, hcol=hcol, pr=pr:
                                       e.matmul(
                                pr[:, half * 512:(half + 1) * 512],
                                wsb[:, slot * 2048 + hc * 128:
                                    slot * 2048 + (hc + 1) * 128],
                                harena[:, hcol + half * 512:
                                       hcol + half * 512 + 512],
                                start=(hc == 0), stop=(hc == 15)),
                                waits=[w_w, w_h_unit[(ei, hc)]] +
                                      (bank_w if hc == 0 and half == 0
                                       else []))
                    wb_free[slot] = w_mm
                    if dc % 2 == 1:
                        gB.wrote(w_mm)
                    if ei == 0:
                        w_accev[dc] = vop(lambda e, dc=dc, pr=pr:
                                          e.tensor_copy(
                            out=acc[:, dc * 1024:(dc + 1) * 1024],
                            in_=pr[:, 0:1024]),
                            waits=[w_mm, w_gates, w_attn_ref[0],
                                   w_xupd.get((tt - 1, dc), NONE)])
                    else:
                        w_accev[dc] = vop(lambda e, dc=dc, pr=pr:
                                          e.tensor_tensor(
                            out=acc[:, dc * 1024:(dc + 1) * 1024],
                            in0=acc[:, dc * 1024:(dc + 1) * 1024],
                            in1=pr[:, 0:1024], op=ALU.add), waits=[w_mm])
                    if dc % 2 == 0:
                        prev_psC[0] = w_accev[dc]
                    else:
                        gB.read(w_accev[dc])
                    if ei == E - 1:
                        x_update(tt, dc)
                    if dc == 3 and k + 1 < len(seq):
                        gates_A(*seq[k + 1])
                    if dc == 6 and k + 1 < len(seq):
                        gates_B(*seq[k + 1])
                    yield ("w2", k, dc)

        gen = units()

        def pump_fn():
            try:
                next(gen)
                return True
            except StopIteration:
                return False

        def pump_until(label):
            for lab in gen:
                if lab == label:
                    return

        def finish_fn():
            while pump_fn():
                pass
            return state["done"]

        return pump_fn, pump_until, finish_fn, w_attn_ref, fst, state

    def emit_ffn1(w_xn, w_attn_done):
        w1t = dt["w1f"]
        w2t = dt["w2f"]
        w_done = NONE
        w_h8_read = [w_attn_done, w_attn_done]
        w_h8s = [w_attn_done, w_attn_done]
        w_fupd = {}
        psC_rd = [w_attn_done, w_attn_done]
        w_fadd_l = NONE
        for tt in range(2):
            w_h = {}
            for hc in range(32):
                slot = hc % 2
                wslot, w_w = wa_load(w1t[hc])
                pbank = psA if slot == 0 else psB
                g = gA if slot == 0 else gB
                w_mm = NONE
                for dc in range(8):
                    for half in range(2):
                        w_mm = top(lambda e, dc=dc, half=half, tt=tt, slot=wslot,
                                   pbank=pbank: e.matmul(
                            pbank[:, half * 512:(half + 1) * 512],
                            wsa[:, slot * 1024 + dc * 128:
                                slot * 1024 + (dc + 1) * 128],
                            xn[:, dc * S + tt * 1024 + half * 512:
                               dc * S + tt * 1024 + half * 512 + 512],
                            start=(dc == 0), stop=(dc == 7)),
                            waits=[w_w, w_xn] +
                                  (g.for_write() if (dc == 0 and half == 0)
                                   else []))
                wa_free[wslot] = w_mm
                g.wrote(w_mm)
                w_rl = sop(lambda e, pbank=pbank, hc=hc, slot=slot: e.activation(
                    out=scf_f8[:, slot * 1024:(slot + 1) * 1024],
                    in_=pbank[:, 0:1024], func=AF.Relu,
                    bias=b1s[:, hc:hc + 1]),
                    waits=[w_mm, w_params, w_h8s[slot]])
                g.read(w_rl)
                harena = ar0 if hc < 16 else ar1
                hcol = (hc % 16) * 1024
                w_cv = vop(lambda e, harena=harena, hcol=hcol, slot=slot:
                           e.tensor_copy(
                    out=harena[:, hcol:hcol + 1024],
                    in_=scf_f8[:, slot * 1024:(slot + 1) * 1024]),
                    waits=[w_rl])
                w_h8s[slot] = w_cv
                w_h[hc] = w_cv
            for dc in range(8):
                w_q = {}
                for hhalf in range(2):
                    slot = hhalf
                    wbsl = wsb[:, slot * 2048:(slot + 1) * 2048]
                    w_w = dma_g(wbsl, w2t[dc, :, hhalf * 2048:(hhalf + 1) * 2048],
                                waits=[wb_free[slot]], ctr=sWB[slot])
                    for half in range(2):
                        w_mm = NONE
                        for hc16 in range(16):
                            hc = hhalf * 16 + hc16
                            harena = ar0 if hc < 16 else ar1
                            hcol = (hc % 16) * 1024
                            w_mm = top(lambda e, hc16=hc16, half=half, slot=slot,
                                       harena=harena, hcol=hcol, hc=hc,
                                       hhalf=hhalf: e.matmul(
                                psC[:, half * 512:(half + 1) * 512],
                                wsb[:, slot * 2048 + hc16 * 128:
                                    slot * 2048 + (hc16 + 1) * 128],
                                harena[:, hcol + half * 512:
                                       hcol + half * 512 + 512],
                                start=(hc == 0), stop=(hc == 31)),
                                waits=[w_w, w_h[hc]] +
                                      ([psC_rd[half]] if hc16 == 0 and
                                       hhalf == 0 else []))
                        if hhalf == 1:
                            w_q[half] = vop(lambda e, dc=dc, half=half:
                                            e.tensor_scalar(
                                out=scf_f16[:, 2048 + half * 512:
                                            2048 + (half + 1) * 512],
                                in0=psC[:, half * 512:(half + 1) * 512],
                                scalar1=sbias[:, SBC["bqf"] + dc:
                                              SBC["bqf"] + dc + 1],
                                scalar2=None, op0=ALU.add),
                                waits=[w_mm, w_bqf, w_fadd_l])
                            psC_rd[half] = w_q[half]
                    wb_free[slot] = w_mm
                xslot = dc % 2
                xsl = xw[:, 2048 + xslot * 1024: 2048 + (xslot + 1) * 1024]
                w_xl = dma_s(xsl, x_out[:, dc, tt * 1024:(tt + 1) * 1024],
                             waits=[w_attn_done, w_fupd.get((tt, dc - 2), NONE),
                                    w_fupd.get((tt - 1, dc + 6), NONE)],
                             ctr=sXW)
                w_ad = vop(lambda e, xsl=xsl: e.tensor_tensor(
                    out=xsl, in0=xsl, in1=scf_f16[:, 2048:3072], op=ALU.add),
                    waits=[w_xl, w_q[0], w_q[1]])
                w_fadd_l = w_ad
                w_stx = dma_s(x_out[:, dc, tt * 1024:(tt + 1) * 1024], xsl,
                              waits=[w_ad], ctr=sST)
                w_fupd[(tt, dc)] = w_stx
                w_done = w_stx
        return w_done

    w_gelu_read2 = [NONE]
    w_h8_read2 = [NONE]
    w_fadd = {}
    gA = Guard(); gB = Guard(); gC0 = Guard(); gC1 = Guard()
    gD = Guard(); gE = Guard(); gCB = Guard()

    def dbg_dump(waits):
        d0 = nc.dram_tensor("dbg0", [128, 16384], BF16, kind="ExternalOutput").ap()
        d1 = nc.dram_tensor("dbg1", [128, 16640], BF16, kind="ExternalOutput").ap()
        d2 = nc.dram_tensor("dbg2", [128, 14336], BF16, kind="ExternalOutput").ap()
        d3 = nc.dram_tensor("dbg3", [128, 16384], BF16, kind="ExternalOutput").ap()
        d4 = nc.dram_tensor("dbg4", [128, 2048], F32, kind="ExternalOutput").ap()
        d5 = nc.dram_tensor("dbg5", [128, 7168], F32, kind="ExternalOutput").ap()
        d6 = nc.dram_tensor("dbg6", [128, 6144], BF16, kind="ExternalOutput").ap()
        d7 = nc.dram_tensor("dbg7", [128, 512], F32, kind="ExternalOutput").ap()
        d8 = nc.dram_tensor("dbg8", [128, 2048], F32, kind="ExternalOutput").ap()
        dma_s(d0, ar0[:], waits=waits, ctr=sST)
        dma_s(d1, ar1[:], waits=waits, ctr=sST)
        dma_s(d2, aat[:, 0:14336], waits=waits, ctr=sST)
        dma_s(d3, xn[:], waits=waits, ctr=sST)
        dma_s(d4, scf[:], waits=waits, ctr=sST)
        dma_s(d5, aat_f32[:, 0:7168], waits=waits, ctr=sST)
        dma_s(d6, uni[:], waits=waits, ctr=sST)
        dma_s(d7, gts[:], waits=waits, ctr=sST)
        dma_s(d8, gcopy[:], waits=waits, ctr=sST)

    stopped = False
    w_xn0, w_logits, _ = emit_ln(0, [w_params], w_scol_done)
    w_gates = emit_router(w_logits)
    if stop_after == "ln0":
        dbg_dump([w_xn0, w_gates]); stopped = True
    if not stopped:
        w_kd0, w_km0, w_vd0 = emit_kv(0, w_xn0, w_logits)
        gA.read(w_vd0); gB.read(w_vd0); gC0.read(w_vd0); gC1.read(w_vd0); gCB.read(w_vd0)
        gD.read(w_km0); gE.read(w_km0)
        if stop_after == "kv0":
            dbg_dump([w_kd0, w_km0, w_vd0, w_gates]); stopped = True
    if not stopped:
        moe_pump, moe_until, moe_finish, w_attn_ref, fst, moe_state = \
            make_moe_filler(w_gates, w_xn0, w_xn0)
        w_at0 = emit_attention(0, w_kd0, w_km0, w_vd0, w_xn0, NONE, NONE,
                               filler=None)
        if stop_after == "attn0":
            dbg_dump([w_at0]); stopped = True
    if not stopped:
        w_attn_ref[0] = w_at0
        moe_until(("w1", 15, 15))
        w_cc, w_ss = emit_rope_tables([fst["w_gelu_read"][0],
                                       fst["w_gelu_read"][1],
                                       fst["state_g"]["gtr"]])
        w_moe = moe_finish()
        if stop_after == "moe0":
            dbg_dump([w_moe]); stopped = True
    if not stopped:
        w_masks = w_masks0
        w_xn1, _, _ = emit_ln(1, [w_moe], w_moe)
        w_kd1, w_km1, w_vd1 = emit_kv(1, w_xn1, w_xn1)
        gA.read(w_vd1); gB.read(w_vd1); gC0.read(w_kd1); gC1.read(w_vd1); gCB.read(w_kd1)
        gD.read(w_km1); gE.read(w_km1)
        if stop_after == "kv1":
            dbg_dump([w_kd1, w_km1, w_vd1, w_masks]); stopped = True
    if not stopped:
        w_at1 = emit_attention(1, w_kd1, w_km1, w_vd1, w_xn1, w_masks, w_kd1,
                               filler=None)
        if stop_after == "attn1":
            dbg_dump([w_at1]); stopped = True
    if not stopped:
        w_ffn = emit_ffn1(w_xn1, w_at1)

    sST_final = sST.n
    sch.op("sync", lambda e: e.wait_ge(sST.sem, sST_final), waits=())

    with es:
        with nc.Block() as block:
            sch.emit(block)
    return nc


def _host_layout(inputs):
    def cp(a):
        return np.ascontiguousarray(np.asarray(a, dtype=np.float32))

    def oc_tiles(wT):
        return np.ascontiguousarray(
            wT.reshape(8, 128, 8, 128).transpose(2, 1, 0, 3).reshape(8, 128, 1024))

    perm = np.concatenate([np.concatenate([h * 64 + np.arange(0, 64, 2),
                                           h * 64 + np.arange(1, 64, 2)])
                           for h in range(16)])
    m = {}
    m["wq0"] = oc_tiles(cp(inputs["L0_attn_w_q"]).T)
    m["wk0"] = oc_tiles(cp(inputs["L0_attn_w_k"]).T)
    m["wv0"] = np.ascontiguousarray(
        cp(inputs["L0_attn_w_v"]).T.reshape(8, 128, 1024).transpose(1, 0, 2)
        .reshape(128, 8192))
    m["wo0t"] = np.ascontiguousarray(
        cp(inputs["L0_attn_w_o"]).T.reshape(16, 64, 8, 128).transpose(2, 1, 0, 3)
        .reshape(8, 64, 2048))
    for nm, key in (("bq0", "L0_attn_b_q"), ("bk0", "L0_attn_b_k"),
                    ("bv0", "L0_attn_b_v"), ("bo0", "L0_attn_b_o")):
        m[nm] = cp(inputs[key])
    for nm, key in (("sq0", "L0_attn_s_q"), ("sk0", "L0_attn_s_k"),
                    ("sv0", "L0_attn_s_v"), ("so0", "L0_attn_s_o")):
        m[nm] = cp(inputs[key]).reshape(1, 1)
    m["rw0"] = np.ascontiguousarray(
        cp(inputs["L0_ffn_router_w"]).reshape(8, 128, 8).transpose(1, 0, 2)
        .reshape(128, 64))
    w1 = cp(inputs["L0_ffn_w1"])
    m["w1m"] = np.ascontiguousarray(
        w1.reshape(8, 128, 128, 128).transpose(2, 1, 0, 3).reshape(128, 128, 1024))
    w2 = cp(inputs["L0_ffn_w2"])
    m["w2m"] = np.ascontiguousarray(
        w2.reshape(8, 16, 128, 8, 128).transpose(0, 3, 2, 1, 4)
        .reshape(8, 8, 128, 2048))
    m["wq1"] = oc_tiles(cp(inputs["L1_attn_wq"]).T[:, perm])
    m["wk1"] = oc_tiles(cp(inputs["L1_attn_wk"]).T[:, perm])
    m["wv1"] = np.ascontiguousarray(
        cp(inputs["L1_attn_wv"]).T.reshape(8, 128, 1024).transpose(1, 0, 2)
        .reshape(128, 8192))
    m["wo1t"] = np.ascontiguousarray(
        cp(inputs["L1_attn_wo"]).T.reshape(16, 64, 8, 128).transpose(2, 1, 0, 3)
        .reshape(8, 64, 2048))
    w1f = cp(inputs["L1_ffn_w1"]).T
    m["w1f"] = np.ascontiguousarray(
        w1f.reshape(8, 128, 32, 128).transpose(2, 1, 0, 3).reshape(32, 128, 1024))
    w2f = cp(inputs["L1_ffn_w2"]).T
    m["w2f"] = np.ascontiguousarray(
        w2f.reshape(32, 128, 8, 128).transpose(2, 1, 0, 3).reshape(8, 128, 4096))
    m["b1f"] = cp(inputs["L1_ffn_b1"])
    m["b2f"] = cp(inputs["L1_ffn_b2"])
    return m


def kernel(**inputs):
    nc = build_program()
    shared = _host_layout(inputs)
    x = np.asarray(inputs["L0_attn_x"], dtype=np.float32)
    in_maps = []
    for b in range(8):
        mm = dict(shared)
        mm["x_t"] = np.ascontiguousarray(
            x[b].T.reshape(8, 128, S).transpose(1, 0, 2))
        in_maps.append(mm)
    res = run_bass_kernel_spmd(nc, in_maps, core_ids=list(range(8)))
    out = np.stack([np.asarray(r["xout"]).transpose(1, 0, 2).reshape(D, S).T
                    for r in res.results], axis=0)
    return out


# version 19
# speedup vs baseline: 1.0756x; 1.0011x over previous
import math
from contextlib import ExitStack

import numpy as np

import concourse.bass as bass
import concourse.mybir as mybir
from concourse.bass_utils import run_bass_kernel_spmd

F32 = mybir.dt.float32
BF16 = mybir.dt.bfloat16
FP16 = mybir.dt.float16
FP8 = mybir.dt.float8e4
I32 = mybir.dt.int32
AF = mybir.ActivationFunctionType
ALU = mybir.AluOpType
AX = mybir.AxisListType

S = 2048
D = 1024
NH = 16
TQ = 512
NQT = S // TQ
E = 8
H1 = 4096

ENGINES = ("sync", "gpsimd", "vector", "scalar", "tensor")


class Ctr:
    __slots__ = ("sem", "n", "name")

    def __init__(self, sem, name):
        self.sem = sem
        self.n = 0
        self.name = name


class Sched:
    def __init__(self):
        self.items = {e: [] for e in ENGINES}

    def op(self, eng, fn, waits=(), inc=None, amt=1):
        w = [(c, v) for (c, v) in waits if c is not None and v > 0]
        self.items[eng].append((tuple(w), fn, inc, amt))
        if inc is not None:
            inc.n += amt
            return inc.n
        return None

    def emit(self, block):
        for ename in ENGINES:
            items = self.items[ename]

            def mk(items=items):
                waited = {}

                def body(eng):
                    for waits, fn, inc, amt in items:
                        for c, v in waits:
                            if waited.get(c, 0) >= v:
                                continue
                            eng.wait_ge(c.sem, v)
                            waited[c] = v
                        inst = fn(eng)
                        if inc is not None:
                            inst.then_inc(inc.sem, amt)
                return body

            getattr(block, ename)(mk())


class Guard:

    def __init__(self, init=None):
        self.w = init
        self.readers = []

    def for_write(self):
        w = list(self.readers)
        if self.w is not None:
            w.append(self.w)
        return w

    def wrote(self, m):
        self.w = m
        self.readers = []

    def for_read(self):
        return [self.w] if self.w is not None else []

    def read(self, m):
        self.readers.append(m)


def build_program(stop_after=None, probe_attn=False):
    nc = bass.Bass("TRN2", target_bir_lowering=False, detect_race_conditions=False)
    dt = {}

    def din(name, shape):
        dt[name] = nc.dram_tensor(name, list(shape), F32, kind="ExternalInput").ap()
        return dt[name]

    x_in = din("x_t", [128, 8, S])
    din("wq0", [8, 128, 1024]); din("wk0", [8, 128, 1024]); din("wv0", [128, 8192])
    din("wo0t", [8, 64, 2048])
    din("bq0", [D]); din("bk0", [D]); din("bv0", [D]); din("bo0", [D])
    din("sq0", [1, 1]); din("sk0", [1, 1]); din("sv0", [1, 1]); din("so0", [1, 1])
    din("rw0", [128, 64])
    din("w1m", [128, 128, 1024])
    din("w2m", [E, 8, 128, 2048])
    din("wq1", [8, 128, 1024]); din("wk1", [8, 128, 1024]); din("wv1", [128, 8192])
    din("wo1t", [8, 64, 2048])
    din("w1f", [32, 128, 1024]); din("w2f", [8, 128, 4096])
    din("b1f", [H1]); din("b2f", [D])

    x_out = nc.dram_tensor("xout", [128, 8, S], F32, kind="ExternalOutput").ap()

    es = ExitStack()

    def sbuf(name, shape, dtype):
        return es.enter_context(nc.sbuf_tensor(name, list(shape), dtype))

    def psmk(name, shape):
        return es.enter_context(nc.psum_tensor(name, list(shape), F32))

    xn = sbuf("xn", [128, 8 * S], BF16)
    ar0 = sbuf("ar0", [128, 16384], BF16)
    ar1 = sbuf("ar1", [128, 16640], BF16)
    aat = sbuf("aat", [128, 16384], BF16)
    xw = sbuf("xw", [128, 4096], F32)
    uni = sbuf("uni", [128, 6144], BF16)
    wsa = sbuf("wsa", [128, 3072], BF16)
    wsb = sbuf("wsb", [128, 6144], BF16)
    scf = sbuf("scf", [128, 2048], F32)
    tbf = sbuf("tbf", [128, 1024], BF16)
    gts = sbuf("gts", [128, 512], F32)
    sbias = sbuf("sbias", [128, 64], F32)
    b1s = sbuf("b1s", [128, 32], F32)
    cbf = sbuf("cbf", [128, 256], BF16)
    cf = sbuf("cf", [128, 64], F32)
    onesf = sbuf("onesf", [128, 128], F32)
    ident = sbuf("ident", [128, 128], F32)
    pswp = sbuf("pswp", [128, 128], BF16)
    rwsb = sbuf("rwsb", [128, 64], F32)
    gcopy = sbuf("gcopy", [128, 2048], F32)
    itr = sbuf("itr", [128, 2048], F32)

    psA = psmk("psA", [128, 1024])
    psB = psmk("psB", [128, 1024])
    psC = psmk("psC", [128, 1024])
    psD = psmk("psD", [128, 512])
    psE = psmk("psE", [128, 512])

    aat_f32 = aat[:].bitcast(F32)
    uni_f32 = uni[:].bitcast(F32)
    scf_bf = scf[:].bitcast(BF16)
    scf_i32 = scf[:].bitcast(I32)
    scf_f8 = scf[:].bitcast(FP8)
    scf_f16 = scf[:].bitcast(FP16)

    CFC = dict(iota=0, theta=1, sign=2, eps=3, cbias=4, kmax=5, qmax=6, c2=7,
               sq=8, sk=9, sv=10, so=11, t0=12, t1=13, eps5=15, npi=16)
    SBC = dict(bq0=0, bk0=8, bv0=16, bo0=24, b2f=32, bqf=40)

    def mkctr(name):
        h = es.enter_context(nc.semaphore(name))
        return Ctr(h, name)

    sT = mkctr("sT"); sV = mkctr("sV"); sS = mkctr("sS"); sG = mkctr("sG")
    sLD = mkctr("sLD"); sXW = mkctr("sXW"); sST = mkctr("sST")
    sWA = [mkctr("sWA0"), mkctr("sWA1"), mkctr("sWA2")]
    sWB = [mkctr("sWB0"), mkctr("sWB1"), mkctr("sWB2")]
    sQW = mkctr("sQW")
    sFL = mkctr("sFL")

    sch = Sched()
    NONE = (sT, 0)

    def dma_s(out, in_, waits=(), ctr=None):
        c = ctr or sLD
        return (c, sch.op("sync", lambda e, o=out, i=in_: e.dma_start(out=o, in_=i),
                          waits=waits, inc=c, amt=16))

    def dma_g(out, in_, waits=(), ctr=None):
        c = ctr or sQW
        return (c, sch.op("gpsimd", lambda e, o=out, i=in_: e.dma_start(out=o, in_=i),
                          waits=waits, inc=c, amt=16))

    def top(fn, waits=()):
        return (sT, sch.op("tensor", fn, waits=waits, inc=sT))

    def vop(fn, waits=()):
        return (sV, sch.op("vector", fn, waits=waits, inc=sV))

    def sop(fn, waits=()):
        return (sS, sch.op("scalar", fn, waits=waits, inc=sS))

    def gop(fn, waits=()):
        return (sG, sch.op("gpsimd", fn, waits=waits, inc=sG))

    w_iota1 = gop(lambda e: e.iota(cf[:, 0:1], pattern=[[0, 1]], base=0,
                                   channel_multiplier=1,
                                   allow_small_or_imprecise_dtypes=True))
    w_iota2 = gop(lambda e: e.iota(scf[:, 0:512], pattern=[[-1, 512]], base=0,
                                   channel_multiplier=1,
                                   allow_small_or_imprecise_dtypes=True))
    w_itr = gop(lambda e: e.iota(itr[0:1, 0:2048], pattern=[[1, 2048]], base=0,
                                 channel_multiplier=0,
                                 allow_small_or_imprecise_dtypes=True),
                waits=[w_iota2])
    w_masks0 = NONE
    for off in range(4):
        w_masks0 = vop(lambda e, off=off: e.tensor_single_scalar(
            out=uni[:, 4096 + off * 512: 4096 + (off + 1) * 512],
            in_=scf[:, 0:512], scalar=-128.0 * off, op=ALU.is_le),
            waits=[w_iota2, w_masks0])

    vop(lambda e: e.memset(cbf[:, 0:1], 1.0))
    vop(lambda e: e.memset(cbf[:, 3:131], 1.0))
    vop(lambda e: e.memset(onesf[:], 1.0))
    vop(lambda e: e.memset(cf[:, CFC["eps"]:CFC["eps"] + 1], 1e-30))
    vop(lambda e: e.memset(cf[:, CFC["eps5"]:CFC["eps5"] + 1], 1e-5))
    vop(lambda e: e.memset(cf[:, CFC["npi"]:CFC["npi"] + 1], -math.pi))
    w_id = vop(lambda e: e.tensor_single_scalar(out=ident[:], in_=scf[:, 0:128],
                                                scalar=0.0, op=ALU.is_equal),
               waits=[w_iota2])
    P1 = uni_f32[:, 0:128]
    P2 = uni_f32[:, 128:256]
    selc = cf[:, CFC["t0"]:CFC["t0"] + 1]
    selc2 = cf[:, CFC["t1"]:CFC["t1"] + 1]
    wl = vop(lambda e: e.tensor_single_scalar(out=P1, in_=scf[:, 0:128], scalar=-32.0,
                                              op=ALU.is_equal), waits=[w_iota2])
    wl = vop(lambda e: e.tensor_single_scalar(out=P2, in_=scf[:, 0:128], scalar=32.0,
                                              op=ALU.is_equal), waits=[wl])
    wl = vop(lambda e: e.tensor_single_scalar(out=selc2, in_=cf[:, 0:1], scalar=64.0,
                                              op=ALU.is_ge), waits=[w_iota1, wl])
    wl = vop(lambda e: e.scalar_tensor_tensor(out=selc, in0=selc2, scalar=-64.0,
                                              in1=cf[:, 0:1], op0=ALU.mult,
                                              op1=ALU.add), waits=[wl])
    w_pm64 = wl = vop(lambda e: e.tensor_copy(out=cf[:, CFC["t1"] + 1:CFC["t1"] + 2],
                                              in_=selc), waits=[wl])
    wl = vop(lambda e: e.tensor_single_scalar(out=selc, in_=selc, scalar=32.0,
                                              op=ALU.is_lt), waits=[wl])
    wl = vop(lambda e: e.tensor_scalar(out=P1, in0=P1, scalar1=selc, scalar2=None,
                                       op0=ALU.mult), waits=[wl])
    wl = vop(lambda e: e.tensor_scalar(out=selc2, in0=selc, scalar1=-1.0, scalar2=1.0,
                                       op0=ALU.mult, op1=ALU.add), waits=[wl])
    wl = vop(lambda e: e.tensor_scalar(out=P2, in0=P2, scalar1=selc2, scalar2=None,
                                       op0=ALU.mult), waits=[wl])
    wl = vop(lambda e: e.tensor_tensor(out=P1, in0=P1, in1=P2, op=ALU.add), waits=[wl])
    w_pswap = wl = vop(lambda e: e.tensor_copy(out=pswp[:], in_=P1), waits=[wl])
    thc = cf[:, CFC["theta"]:CFC["theta"] + 1]
    sgc = cf[:, CFC["sign"]:CFC["sign"] + 1]
    pm64c = cf[:, CFC["t1"] + 1:CFC["t1"] + 2]
    wl = vop(lambda e: e.tensor_single_scalar(out=thc, in_=pm64c, scalar=32.0,
                                              op=ALU.is_ge), waits=[w_pm64, wl])
    wl = vop(lambda e: e.scalar_tensor_tensor(out=thc, in0=thc, scalar=-32.0,
                                              in1=pm64c, op0=ALU.mult, op1=ALU.add),
             waits=[wl])
    w_thv = wl = vop(lambda e: e.tensor_single_scalar(
        out=thc, in_=thc, scalar=-math.log(10000.0) / 32.0, op=ALU.mult), waits=[wl])
    w_thc = sop(lambda e: e.activation(out=thc, in_=thc, func=AF.Exp), waits=[w_thv])
    wl = vop(lambda e: e.tensor_single_scalar(out=sgc, in_=pm64c, scalar=32.0,
                                              op=ALU.is_ge),
             waits=[w_pm64, wl, w_thc])
    w_sgc = vop(lambda e: e.tensor_scalar(out=sgc, in0=sgc, scalar1=2.0, scalar2=-1.0,
                                          op0=ALU.mult, op1=ALU.add), waits=[wl])
    vop(lambda e: e.memset(cbf[0:64, 1:2], 1.0))
    vop(lambda e: e.memset(cbf[64:128, 1:2], 0.0))
    vop(lambda e: e.memset(cbf[0:64, 2:3], 0.0))
    w_cbf = vop(lambda e: e.memset(cbf[64:128, 2:3], 1.0))

    for nm, col in (("bq0", SBC["bq0"]), ("bk0", SBC["bk0"]), ("bv0", SBC["bv0"]),
                    ("bo0", SBC["bo0"]), ("b2f", SBC["b2f"])):
        bh = dt[nm].rearrange("(c p) -> c p", p=128)
        for c in range(8):
            dma_s(sbias[:, col + c:col + c + 1], bh[c:c + 1, :])
    b1h = dt["b1f"].rearrange("(c p) -> c p", p=128)
    for c in range(32):
        dma_s(b1s[:, c:c + 1], b1h[c:c + 1, :])
    for nm, key in (("sq0", "sq"), ("sk0", "sk"), ("sv0", "sv"), ("so0", "so")):
        dma_s(cf[0:1, CFC[key]:CFC[key] + 1], dt[nm])
    w_rw = dma_s(rwsb[:], dt["rw0"])
    w_params = (sLD, sLD.n)

    w_onesf = (sV, sV.n)
    prev_read = NONE
    w_scol = {}
    for key in ("sq", "sk", "sv", "so"):
        wt = top(lambda e, k=key: e.matmul(psD[:, 0:1], onesf[0:1, :],
                                           cf[0:1, CFC[k]:CFC[k] + 1],
                                           start=True, stop=True),
                 waits=[w_params, w_onesf, prev_read])
        wsc = sop(lambda e, k=key: e.copy(out=cf[:, CFC[k]:CFC[k] + 1],
                                          in_=psD[:, 0:1]), waits=[wt])
        w_scol[key] = wsc
        prev_read = wsc
    w_scol_done = prev_read

    w_b1 = vop(lambda e: e.tensor_copy(out=scf_f16[:, 2048:2056], in_=sbias[:, 32:40]),
               waits=[w_params])
    w_bqf = vop(lambda e: e.tensor_copy(out=sbias[:, 40:48], in_=scf_f16[:, 2048:2056]),
                waits=[w_b1])

    wa_free = [NONE, NONE, NONE]
    wb_free = [NONE, NONE, NONE]

    def emit_ln(layer, deps, psd_free, deps_tiles=None):
        xsrc = x_in if layer == 0 else x_out
        state = dict(last_xn=NONE, w_le=NONE, w_rmm_last=NONE,
                     w_mmm=NONE, w_vmm=NONE)
        arwin = ar0[:, 0:8192].bitcast(F32)
        bc_free = {0: psd_free, 1: psd_free}
        win_free = {0: NONE, 1: NONE}
        if deps_tiles is None:
            deps_tiles = [list(deps)] * 4
        p2 = {}

        def phase1(tt):
            t0 = tt * 512
            win = xw[:] if tt % 2 == 0 else arwin
            bc = psA if tt % 2 == 0 else psC
            wr = win.rearrange("p (dc t) -> p dc t", t=512)
            w_lo = dma_s(wr[:, 0:4, :], xsrc[:, 0:4, t0:t0 + 512],
                         waits=list(deps_tiles[tt]) + [win_free[tt % 2]],
                         ctr=sXW)
            w_hi = dma_s(wr[:, 4:8, :], xsrc[:, 4:8, t0:t0 + 512],
                         waits=list(deps_tiles[tt]) + [win_free[tt % 2]],
                         ctr=sLD)
            for dc in range(8):
                w_ld = w_lo if dc < 4 else w_hi
                state["w_mmm"] = top(lambda e, dc=dc, win=win: e.matmul(
                    psD[0:1, :], onesf[:, 0:1], win[:, dc * 512:(dc + 1) * 512],
                    start=(dc == 0), stop=(dc == 7)),
                    waits=[w_ld, psd_free, w_onesf])
                w_sq = sop(lambda e, dc=dc, win=win: e.activation(
                    out=tbf[:, 512:1024], in_=win[:, dc * 512:(dc + 1) * 512],
                    func=AF.Square), waits=[w_ld, state["w_vmm"]])
                state["w_vmm"] = top(lambda e, dc=dc: e.matmul(
                    psE[0:1, :], cbf[:, 0:1], tbf[:, 512:1024],
                    start=(dc == 0), stop=(dc == 7)),
                    waits=[w_sq, psd_free, w_cbf])
            m_row = scf[0:1, 0:512]
            v_row = scf[0:1, 512:1024]
            mm_row = scf[0:1, 1024:1536]
            w_m = vop(lambda e: e.tensor_single_scalar(out=m_row, in_=psD[0:1, :],
                                                       scalar=1.0 / D, op=ALU.mult),
                      waits=[state["w_mmm"]])
            w_v1 = vop(lambda e: e.tensor_single_scalar(out=v_row, in_=psE[0:1, :],
                                                        scalar=1.0 / D, op=ALU.mult),
                       waits=[state["w_vmm"]])
            w_v2 = vop(lambda e: e.tensor_tensor(out=mm_row, in0=m_row, in1=m_row,
                                                 op=ALU.mult), waits=[w_m])
            w_v3 = vop(lambda e: e.tensor_tensor(out=v_row, in0=v_row, in1=mm_row,
                                                 op=ALU.subtract), waits=[w_v1, w_v2])
            w_v4 = sop(lambda e: e.activation(out=v_row, in_=v_row, func=AF.Ln,
                                              bias=cf[0:1, CFC["eps5"]:CFC["eps5"] + 1]),
                       waits=[w_v3])
            w_v5 = sop(lambda e: e.activation(out=v_row, in_=v_row, func=AF.Exp,
                                              scale=-0.5), waits=[w_v4])
            w_bm = top(lambda e, bc=bc: e.matmul(
                bc[:, 0:512], onesf[0:1, :], m_row,
                start=True, stop=True),
                waits=[w_m, w_v5, psd_free, bc_free[tt % 2]])
            w_br = top(lambda e, bc=bc: e.matmul(
                bc[:, 512:1024], onesf[0:1, :], v_row,
                start=True, stop=True),
                waits=[w_v5, psd_free, bc_free[tt % 2]])
            p2[tt] = (t0, win, bc, w_bm, w_br)

        def phase2(tt):
            t0, win, bc, w_bm, w_br = p2[tt]
            w_xnf = []
            for dc in range(8):
                tmpf = scf[:, 1536:2048]
                w_t1 = vop(lambda e, dc=dc, win=win, bc=bc: e.tensor_tensor(
                    out=tmpf, in0=win[:, dc * 512:(dc + 1) * 512], in1=bc[:, 0:512],
                    op=ALU.subtract),
                    waits=[w_bm, w_xnf[-1] if w_xnf else NONE, state["w_le"]])
                w_t2 = vop(lambda e, dc=dc, t0=t0, bc=bc: e.tensor_tensor(
                    out=xn[:, dc * S + t0: dc * S + t0 + 512], in0=tmpf,
                    in1=bc[:, 512:1024], op=ALU.mult), waits=[w_br, w_t1])
                state["last_xn"] = w_t2
                if layer == 0:
                    w_t3 = vop(lambda e, dc=dc, bc=bc: e.tensor_tensor(
                        out=aat_f32[:, dc * 512:(dc + 1) * 512], in0=tmpf,
                        in1=bc[:, 512:1024], op=ALU.mult), waits=[w_t1])
                    w_xnf.append(w_t3)
                    state["last_xn"] = w_t3
                else:
                    w_xnf.append(w_t2)
            bc_free[tt % 2] = state["last_xn"]
            win_free[tt % 2] = state["last_xn"]
            if layer == 0:
                w_rmm = NONE
                for qs in range(4):
                    for dc in range(8):
                        w_rmm = top(lambda e, qs=qs, dc=dc: e.matmul(
                            psB[:, qs * 8:(qs + 1) * 8],
                            aat_f32[:, dc * 512 + qs * 128: dc * 512 + qs * 128 + 128],
                            rwsb[:, dc * 8:(dc + 1) * 8],
                            start=(dc == 0), stop=(dc == 7)),
                            waits=[w_xnf[dc], w_rw, state["w_le"], psd_free])
                state["w_le"] = sop(lambda e, tt=tt: e.copy(
                    out=gts[:, tt * 32:(tt + 1) * 32],
                    in_=psB[:, 0:32]), waits=[w_rmm])
                state["w_rmm_last"] = w_rmm

        for tt in range(4):
            phase1(tt)
            if tt > 0:
                phase2(tt - 1)
        phase2(3)
        return state["last_xn"], state["w_le"], state["w_rmm_last"]

    def emit_router(w_logits):
        L = gts[:, 0:128].rearrange("p (c e) -> p c e", e=8)
        m1 = gts[:, 128:144]; m2 = gts[:, 144:160]
        i1 = gts[:, 160:176]; i2 = gts[:, 176:192]
        tmp = gts[:, 192:208]; tmp2 = gts[:, 208:224]
        g1c = gts[:, 224:240]; t3 = gts[:, 240:256]
        wl = vop(lambda e: e.tensor_copy(out=m1, in_=L[:, :, 0]), waits=[w_logits])
        wl = vop(lambda e: e.memset(m2, -1e30), waits=[wl])
        wl = vop(lambda e: e.memset(i1, 0.0), waits=[wl])
        wl = vop(lambda e: e.memset(i2, 0.0), waits=[wl])
        for ei in range(1, 8):
            pe = L[:, :, ei]
            wl = vop(lambda e, pe=pe: e.tensor_tensor(out=tmp, in0=pe, in1=m1,
                                                      op=ALU.is_gt), waits=[wl])
            wl = vop(lambda e, pe=pe: e.tensor_tensor(out=tmp2, in0=pe, in1=m2,
                                                      op=ALU.is_gt), waits=[wl])
            wl = vop(lambda e, ei=ei: e.tensor_scalar(out=t3, in0=i2, scalar1=-1.0,
                                                      scalar2=float(ei), op0=ALU.mult,
                                                      op1=ALU.add), waits=[wl])
            wl = vop(lambda e: e.tensor_tensor(out=t3, in0=t3, in1=tmp2, op=ALU.mult),
                     waits=[wl])
            wl = vop(lambda e: e.tensor_tensor(out=i2, in0=i2, in1=t3, op=ALU.add),
                     waits=[wl])
            wl = vop(lambda e: e.tensor_tensor(out=t3, in0=i1, in1=i2,
                                               op=ALU.subtract), waits=[wl])
            wl = vop(lambda e: e.tensor_tensor(out=t3, in0=t3, in1=tmp, op=ALU.mult),
                     waits=[wl])
            wl = vop(lambda e: e.tensor_tensor(out=i2, in0=i2, in1=t3, op=ALU.add),
                     waits=[wl])
            wl = vop(lambda e, pe=pe: e.tensor_tensor(out=t3, in0=m1, in1=pe,
                                                      op=ALU.min), waits=[wl])
            wl = vop(lambda e: e.tensor_tensor(out=m2, in0=m2, in1=t3, op=ALU.max),
                     waits=[wl])
            wl = vop(lambda e, ei=ei: e.tensor_scalar(out=t3, in0=i1, scalar1=-1.0,
                                                      scalar2=float(ei), op0=ALU.mult,
                                                      op1=ALU.add), waits=[wl])
            wl = vop(lambda e: e.tensor_tensor(out=t3, in0=t3, in1=tmp, op=ALU.mult),
                     waits=[wl])
            wl = vop(lambda e: e.tensor_tensor(out=i1, in0=i1, in1=t3, op=ALU.add),
                     waits=[wl])
            wl = vop(lambda e, pe=pe: e.tensor_tensor(out=m1, in0=m1, in1=pe,
                                                      op=ALU.max), waits=[wl])
        wl = vop(lambda e: e.tensor_tensor(out=tmp, in0=m2, in1=m1, op=ALU.subtract),
                 waits=[wl])
        w_e = sop(lambda e: e.activation(out=tmp, in_=tmp, func=AF.Exp), waits=[wl])
        wl = vop(lambda e: e.tensor_scalar(out=tmp, in0=tmp, scalar1=1.0, scalar2=None,
                                           op0=ALU.add), waits=[w_e, wl])
        wl = vop(lambda e: e.reciprocal(out=g1c, in_=tmp), waits=[wl])
        for ei in range(8):
            ge = gts[:, 256 + ei * 16: 256 + (ei + 1) * 16]
            wl = vop(lambda e, ei=ei: e.tensor_single_scalar(out=tmp2, in_=i1,
                                                             scalar=float(ei),
                                                             op=ALU.is_equal),
                     waits=[wl])
            wl = vop(lambda e, ge=ge: e.tensor_tensor(out=ge, in0=g1c, in1=tmp2,
                                                      op=ALU.mult), waits=[wl])
            wl = vop(lambda e, ei=ei: e.tensor_single_scalar(out=tmp2, in_=i2,
                                                             scalar=float(ei),
                                                             op=ALU.is_equal),
                     waits=[wl])
            wl = vop(lambda e: e.tensor_scalar(out=t3, in0=g1c, scalar1=-1.0,
                                               scalar2=1.0, op0=ALU.mult,
                                               op1=ALU.add), waits=[wl])
            wl = vop(lambda e: e.tensor_tensor(out=tmp2, in0=tmp2, in1=t3,
                                               op=ALU.mult), waits=[wl])
            wl = vop(lambda e, ge=ge: e.tensor_tensor(out=ge, in0=ge, in1=tmp2,
                                                      op=ALU.add), waits=[wl])
        return wl

    wa_rot = [0]
    wb_rot = [0]

    def wb_load(src_ap, rows=128):
        slot = wb_rot[0]
        wb_rot[0] = (slot + 1) % 3
        out = wsb[0:rows, slot * 2048:(slot + 1) * 2048]
        return slot, dma_g(out, src_ap, waits=[wb_free[slot]], ctr=sWB[slot])

    def wa_load(src_ap, extra=()):
        slot = wa_rot[0]
        wa_rot[0] = (slot + 1) % 3
        out = wsa[:, slot * 1024:(slot + 1) * 1024]
        return slot, dma_g(out, src_ap, waits=[wa_free[slot]] + list(extra),
                           ctr=sWA[slot])

    def emit_kv(layer, w_xn, psfree):
        sfx = str(layer)
        krope_last = NONE
        prev_ev = [psfree, psfree]
        kev = {}
        for oc in range(8):
            wslot, w_w = wa_load(dt["wk" + sfx][oc])
            for tt in range(4):
                half = (oc * 4 + tt) % 2
                pr = psB[:, half * 512:(half + 1) * 512]
                w_mm = NONE
                for dc in range(8):
                    w_mm = top(lambda e, dc=dc, tt=tt, pr=pr, slot=wslot: e.matmul(
                        pr, wsa[:, slot * 1024 + dc * 128: slot * 1024 + (dc + 1) * 128],
                        xn[:, dc * S + tt * 512: dc * S + tt * 512 + 512],
                        start=(dc == 0), stop=(dc == 7)),
                        waits=[w_w, w_xn, prev_ev[half]])
                if layer == 0:
                    ev = vop(lambda e, oc=oc, tt=tt, pr=pr: e.tensor_scalar(
                        out=ar0[:, oc * S + tt * 512: oc * S + tt * 512 + 512],
                        in0=pr, scalar1=cf[:, CFC["sk"]:CFC["sk"] + 1],
                        scalar2=sbias[:, SBC["bk0"] + oc:SBC["bk0"] + oc + 1],
                        op0=ALU.mult, op1=ALU.add),
                        waits=[w_mm, w_scol_done, w_params])
                else:
                    ev = sop(lambda e, oc=oc, tt=tt, pr=pr: e.copy(
                        out=ar0[:, oc * S + tt * 512: oc * S + tt * 512 + 512],
                        in_=pr), waits=[w_mm])
                prev_ev[half] = ev
                kev[(oc, tt)] = ev
            wa_free[wslot] = w_mm
        klast = kev[(7, 3)]
        if layer == 1:
            for oc in range(8):
                for tt in range(4):
                    ksl = ar0[:, oc * S + tt * 512: oc * S + tt * 512 + 512]
                    w_sw = top(lambda e, ksl=ksl: e.matmul(
                        psC[:, 0:512], pswp[:], ksl, start=True, stop=True),
                        waits=[kev[(oc, tt)], w_pswap, krope_last, psfree])
                    w_r1 = vop(lambda e, ksl=ksl, tt=tt: e.tensor_tensor(
                        out=tbf[:, 0:512], in0=ksl,
                        in1=uni[:, tt * 512:(tt + 1) * 512], op=ALU.mult),
                        waits=[w_cc_g[0], kev[(oc, tt)], krope_last])
                    w_r2 = vop(lambda e, tt=tt: e.tensor_tensor(
                        out=tbf[:, 512:1024], in0=psC[:, 0:512],
                        in1=uni[:, 2048 + tt * 512: 2048 + (tt + 1) * 512],
                        op=ALU.mult), waits=[w_ss_g[0], w_sw])
                    krope_last = vop(lambda e, ksl=ksl: e.tensor_tensor(
                        out=ksl, in0=tbf[:, 0:512], in1=tbf[:, 512:1024],
                        op=ALU.add), waits=[w_r1, w_r2])
            klast = krope_last
        w_k0 = vop(lambda e: e.memset(cf[0:1, CFC["kmax"]:CFC["kmax"] + 1], 0.0))
        wprev = w_k0
        rd_g = [NONE, NONE]
        sq_g = [NONE, NONE]
        i = 0
        for oc in range(8):
            for tt in range(4):
                th = i % 2
                w_sq = sop(lambda e, oc=oc, tt=tt, th=th: e.activation(
                    out=tbf[:, th * 512:(th + 1) * 512],
                    in_=ar0[:, oc * S + tt * 512: oc * S + tt * 512 + 512],
                    func=AF.Square), waits=[klast, sq_g[th]])
                for par in range(2):
                    bank = (i * 2 + par) % 2
                    pr = psD[0:1, :] if bank == 0 else psE[0:1, :]
                    w_mm = top(lambda e, par=par, th=th, pr=pr: e.matmul(
                        pr, cbf[par * 64:(par + 1) * 64, 1 + par:2 + par],
                        tbf[par * 64:(par + 1) * 64, th * 512:(th + 1) * 512],
                        start=True, stop=True), waits=[w_sq, rd_g[bank], psfree])
                    sq_g[th] = w_mm
                    w_rm = vop(lambda e, pr=pr: e.reduce_max(
                        out=cf[0:1, CFC["t0"]:CFC["t0"] + 1], in_=pr,
                        axis=AX.X), waits=[w_mm, wprev])
                    rd_g[bank] = w_rm
                    wprev = vop(lambda e: e.tensor_tensor(
                        out=cf[0:1, CFC["kmax"]:CFC["kmax"] + 1],
                        in0=cf[0:1, CFC["kmax"]:CFC["kmax"] + 1],
                        in1=cf[0:1, CFC["t0"]:CFC["t0"] + 1], op=ALU.max),
                        waits=[w_rm])
                i += 1
        w_kmax = wprev
        w_o65 = vop(lambda e: e.memset(ar1[:, 64:16640:65], 1.0), waits=[psfree])
        w_wv = dma_g(aat[:, 0:8192], dt["wv" + sfx], waits=[psfree])
        w_bv = [NONE, NONE]
        if layer == 0:
            w_bvr = dma_s(scf[0:1, 0:1024],
                          dt["bv0"].rearrange("(o d) -> o d", o=1), waits=[w_xn])
            prev = NONE
            for ot in range(2):
                wt = top(lambda e, ot=ot: e.matmul(
                    psC[:, 0:512], onesf[0:1, :], scf[0:1, ot * 512:(ot + 1) * 512],
                    start=True, stop=True), waits=[w_bvr, prev, psfree])
                wc = sop(lambda e, ot=ot: e.copy(
                    out=scf[:, 1024 + ot * 512: 1024 + (ot + 1) * 512],
                    in_=psC[:, 0:512]), waits=[wt])
                w_bv[ot] = wc
                prev = wc
        prev_ev = [psfree, psfree]
        vev = NONE
        for ot in range(2):
            for tc in range(16):
                half = (ot * 16 + tc) % 2
                pr = psA[:, half * 512:(half + 1) * 512]
                w_mm = NONE
                for dc in range(8):
                    w_mm = top(lambda e, dc=dc, tc=tc, ot=ot, pr=pr: e.matmul(
                        pr, xn[:, dc * S + tc * 128: dc * S + (tc + 1) * 128],
                        aat[:, dc * 1024 + ot * 512: dc * 1024 + ot * 512 + 512],
                        start=(dc == 0), stop=(dc == 7)),
                        waits=[w_wv, w_xn, prev_ev[half]])
                vout = ar1[:, tc * 1040: (tc + 1) * 1040].rearrange(
                    "p (h c) -> p h c", c=65)[:, ot * 8:(ot + 1) * 8, 0:64]
                vin = pr.rearrange("p (h c) -> p h c", c=64)
                if layer == 0:
                    vev = vop(lambda e, vout=vout, vin=vin, ot=ot: e.scalar_tensor_tensor(
                        out=vout, in0=vin, scalar=cf[:, CFC["sv"]:CFC["sv"] + 1],
                        in1=scf[:, 1024 + ot * 512: 1024 + (ot + 1) * 512].rearrange(
                            "p (h c) -> p h c", c=64),
                        op0=ALU.mult, op1=ALU.add),
                        waits=[w_mm, w_scol_done, w_bv[ot], w_o65])
                else:
                    vev = vop(lambda e, vout=vout, vin=vin: e.tensor_copy(
                        out=vout, in_=vin), waits=[w_mm, w_o65])
                prev_ev[half] = vev
        return klast, w_kmax, vev

    w_cc_g = [NONE]
    w_ss_g = [NONE]

    def emit_rope_tables(deps):
        wlast = NONE
        bank = [psA[:, 0:512], psA[:, 512:1024], psD[:, 0:512], psE[:, 0:512]]
        bguard = [gA, gA, gD, gE]
        w_bs = []
        for tt in range(4):
            w_b = top(lambda e, tt=tt: e.matmul(
                bank[tt], onesf[0:1, :], itr[0:1, tt * 512:(tt + 1) * 512],
                start=True, stop=True),
                waits=[w_itr] + list(deps) + bguard[tt].for_write())
            bguard[tt].wrote(w_b)
            w_bs.append(w_b)
        for tt in range(4):
            w_b = w_bs[tt]
            w_a = vop(lambda e, tt=tt: e.tensor_scalar(
                out=scf[:, 512:1024], in0=bank[tt],
                scalar1=thc, scalar2=None,
                op0=ALU.mult), waits=[w_b, w_thc, wlast])
            bguard[tt].read(w_a)
            w_s_prev = NONE
            for which in range(2):
                shift = 0.25 if which == 1 else 0.0
                dst = uni[:, (2048 if which == 0 else 0) + tt * 512:][:, 0:512]
                w_n = vop(lambda e, shift=shift: e.tensor_scalar(
                    out=scf[:, 1024:1536], in0=scf[:, 512:1024],
                    scalar1=1.0 / (2 * math.pi), scalar2=shift,
                    op0=ALU.mult, op1=ALU.add), waits=[w_a, wlast, w_s_prev])
                w_k = vop(lambda e: e.tensor_copy(out=scf_i32[:, 1536:2048],
                                                  in_=scf[:, 1024:1536]),
                          waits=[w_n])
                w_kf = vop(lambda e: e.tensor_copy(out=gts[:, 0:512],
                                                   in_=scf_i32[:, 1536:2048]),
                           waits=[w_k])
                w_f = vop(lambda e: e.tensor_tensor(out=scf[:, 1024:1536],
                                                    in0=scf[:, 1024:1536],
                                                    in1=gts[:, 0:512],
                                                    op=ALU.subtract), waits=[w_kf])
                w_w1 = vop(lambda e: e.tensor_single_scalar(
                    out=gts[:, 0:512], in_=scf[:, 1024:1536], scalar=0.5,
                    op=ALU.is_ge), waits=[w_f])
                w_w2 = vop(lambda e: e.tensor_tensor(
                    out=scf[:, 1024:1536], in0=scf[:, 1024:1536],
                    in1=gts[:, 0:512], op=ALU.subtract), waits=[w_w1])
                w_s = sop(lambda e, dst=dst: e.activation(
                    out=dst, in_=scf[:, 1024:1536], func=AF.Sin,
                    scale=2 * math.pi), waits=[w_w2])
                w_s_prev = w_s
                if which == 0:
                    wlast = vop(lambda e, dst=dst: e.tensor_scalar(
                        out=dst, in0=dst, scalar1=sgc, scalar2=None, op0=ALU.mult),
                        waits=[w_s, w_sgc])
                    w_ss_g[0] = wlast
                else:
                    wlast = w_s
                    w_cc_g[0] = wlast
        return w_cc_g[0], w_ss_g[0]

    def emit_attention(layer, w_kdone, w_kmax, w_vdone, w_xn, w_masks, qrope_init,
                       filler=None, probe_attn=probe_attn):
        sfx = str(layer)
        xsrc = x_in if layer == 0 else x_out
        ctx0 = 0
        q0 = 8192
        p0 = 12288
        w_pv_slot = [NONE, NONE, NONE, NONE]
        qrope_last = qrope_init
        w_xw_store = {}
        w_attn_done = NONE

        def pump(n=1):
            if filler is not None:
                for _ in range(n):
                    if not filler():
                        break

        for qt in range(NQT):
            qev = {}
            for oc in range(8):
                wslot, w_w = wa_load(dt["wq" + sfx][oc])
                w_mm = NONE
                for dc in range(8):
                    w_mm = top(lambda e, dc=dc, qt=qt, slot=wslot: e.matmul(
                        psE[:, 0:512],
                        wsa[:, slot * 1024 + dc * 128: slot * 1024 + (dc + 1) * 128],
                        xn[:, dc * S + qt * 512: dc * S + qt * 512 + 512],
                        start=(dc == 0), stop=(dc == 7)),
                        waits=[w_w, w_xn] + (gE.for_write() if dc == 0 else []))
                wa_free[wslot] = w_mm
                gE.wrote(w_mm)
                if layer == 0:
                    ev = vop(lambda e, oc=oc: e.tensor_scalar(
                        out=aat[:, q0 + oc * 512: q0 + (oc + 1) * 512],
                        in0=psE[:, 0:512], scalar1=cf[:, CFC["sq"]:CFC["sq"] + 1],
                        scalar2=sbias[:, SBC["bq0"] + oc:SBC["bq0"] + oc + 1],
                        op0=ALU.mult, op1=ALU.add),
                        waits=[w_mm, w_scol_done, w_params])
                else:
                    ev = sop(lambda e, oc=oc: e.copy(
                        out=aat[:, q0 + oc * 512: q0 + (oc + 1) * 512],
                        in_=psE[:, 0:512]), waits=[w_mm])
                gE.read(ev)
                qev[oc] = ev
                pump(1)
            if layer == 1:
                for oc in range(8):
                    qsl = aat[:, q0 + oc * 512: q0 + (oc + 1) * 512]
                    w_sw = top(lambda e, qsl=qsl: e.matmul(
                        psC[:, 0:512], pswp[:], qsl, start=True, stop=True),
                        waits=[qev[oc], w_pswap, qrope_last] + gCB.for_write())
                    gCB.wrote(w_sw)
                    w_r1 = vop(lambda e, qsl=qsl, qt=qt: e.tensor_tensor(
                        out=tbf[:, 0:512], in0=qsl,
                        in1=uni[:, qt * 512:(qt + 1) * 512], op=ALU.mult),
                        waits=[w_cc_g[0], qev[oc], qrope_last])
                    w_r2 = vop(lambda e, qt=qt: e.tensor_tensor(
                        out=tbf[:, 512:1024], in0=psC[:, 0:512],
                        in1=uni[:, 2048 + qt * 512: 2048 + (qt + 1) * 512],
                        op=ALU.mult), waits=[w_ss_g[0], w_sw])
                    gCB.read(w_r2)
                    qrope_last = vop(lambda e, qsl=qsl: e.tensor_tensor(
                        out=qsl, in0=tbf[:, 0:512], in1=tbf[:, 512:1024],
                        op=ALU.add), waits=[w_r1, w_r2])
                    qev[oc] = qrope_last
            qall = qev[7] if layer == 0 else qrope_last
            w_qm = vop(lambda e: e.memset(cf[0:1, CFC["qmax"]:CFC["qmax"] + 1], 0.0),
                       waits=[qall])
            wprev = w_qm
            rd_g = {0: NONE, 1: NONE}
            sq_g = [NONE, NONE]
            for oc in range(8):
                th = oc % 2
                w_sq = sop(lambda e, oc=oc, th=th: e.activation(
                    out=tbf[:, th * 512:(th + 1) * 512],
                    in_=aat[:, q0 + oc * 512: q0 + (oc + 1) * 512],
                    func=AF.Square), waits=[qall, sq_g[th]])
                for par in range(2):
                    bank = par
                    pr = psD[0:1, :] if bank == 0 else psE[0:1, :]
                    w_mm = top(lambda e, par=par, th=th, pr=pr: e.matmul(
                        pr, cbf[par * 64:(par + 1) * 64, 1 + par:2 + par],
                        tbf[par * 64:(par + 1) * 64, th * 512:(th + 1) * 512],
                        start=True, stop=True),
                        waits=[w_sq, rd_g[bank]] +
                              (gD.for_write() if (oc == 0 and par == 0) else []) +
                              (gE.for_write() if (oc == 0 and par == 1) else []))
                    sq_g[th] = w_mm
                    w_rm = vop(lambda e, pr=pr: e.reduce_max(
                        out=cf[0:1, CFC["t0"]:CFC["t0"] + 1], in_=pr,
                        axis=AX.X), waits=[w_mm, wprev])
                    rd_g[bank] = w_rm
                    wprev = vop(lambda e: e.tensor_tensor(
                        out=cf[0:1, CFC["qmax"]:CFC["qmax"] + 1],
                        in0=cf[0:1, CFC["qmax"]:CFC["qmax"] + 1],
                        in1=cf[0:1, CFC["t0"]:CFC["t0"] + 1], op=ALU.max),
                        waits=[w_rm])
                pump(1)
            w_c1 = vop(lambda e: e.tensor_tensor(
                out=cf[0:1, CFC["c2"]:CFC["c2"] + 1],
                in0=cf[0:1, CFC["qmax"]:CFC["qmax"] + 1],
                in1=cf[0:1, CFC["kmax"]:CFC["kmax"] + 1], op=ALU.mult),
                waits=[wprev, w_kmax])
            w_c2 = sop(lambda e: e.activation(out=cf[0:1, CFC["c2"]:CFC["c2"] + 1],
                                              in_=cf[0:1, CFC["c2"]:CFC["c2"] + 1],
                                              func=AF.Ln,
                                              bias=cf[0:1, CFC["eps"]:CFC["eps"] + 1]),
                       waits=[w_c1])
            w_c3 = sop(lambda e: e.activation(out=cf[0:1, CFC["c2"]:CFC["c2"] + 1],
                                              in_=cf[0:1, CFC["c2"]:CFC["c2"] + 1],
                                              func=AF.Exp, scale=0.5), waits=[w_c2])
            w_c4a = vop(lambda e: e.tensor_scalar(
                out=cf[0:1, CFC["c2"]:CFC["c2"] + 1],
                in0=cf[0:1, CFC["c2"]:CFC["c2"] + 1], scalar1=0.125, scalar2=-40.0,
                op0=ALU.mult, op1=ALU.add), waits=[w_c3])
            w_c4b = vop(lambda e: e.tensor_scalar_max(
                out=cf[0:1, CFC["c2"]:CFC["c2"] + 1],
                in0=cf[0:1, CFC["c2"]:CFC["c2"] + 1], scalar1=0.0), waits=[w_c4a])
            w_c4 = vop(lambda e: e.tensor_single_scalar(
                out=cf[0:1, CFC["c2"]:CFC["c2"] + 1],
                in_=cf[0:1, CFC["c2"]:CFC["c2"] + 1], scalar=-1.0, op=ALU.mult),
                waits=[w_c4b])
            w_c5 = top(lambda e: e.matmul(psD[:, 0:1], onesf[0:1, :],
                                          cf[0:1, CFC["c2"]:CFC["c2"] + 1],
                                          start=True, stop=True),
                       waits=[w_c4, rd_g[0]] + gD.for_write())
            gD.wrote(w_c5)
            w_cb = sop(lambda e: e.copy(out=cf[:, CFC["cbias"]:CFC["cbias"] + 1],
                                        in_=psD[:, 0:1]), waits=[w_c5])
            gD.read(w_cb)
            nkc = 16 if layer == 0 else (qt + 1) * 4
            nbat = nkc // 2
            pending_den = [None]
            psCB = psC[:, 0:512]
            psDEN = psC[:, 512:1024]

            def emit_den(h, ctxps, gX):
                w_pv = gX.w
                w_d1 = sop(lambda e, ctxps=ctxps: e.activation(
                    out=scf[64:65, 0:512], in_=ctxps[64:65, :], func=AF.Ln,
                    bias=cf[64:65, CFC["eps"]:CFC["eps"] + 1]), waits=[w_pv])
                w_d2 = sop(lambda e: e.activation(
                    out=scf[64:65, 0:512], in_=scf[64:65, 0:512], func=AF.Exp,
                    scale=-1.0), waits=[w_d1])
                w_d3 = top(lambda e: e.matmul(psDEN, onesf[64:65, :],
                                              scf[64:65, 0:512],
                                              start=True, stop=True),
                           waits=[w_d2] + gC1.for_write())
                gC1.wrote(w_d3)
                w_d4 = sop(lambda e: e.copy(out=scf[:, 512:1024], in_=psDEN),
                           waits=[w_d3])
                gC1.read(w_d4)
                w_ce = vop(lambda e, h=h, ctxps=ctxps: e.tensor_tensor(
                    out=aat[0:64, ctx0 + h * 512: ctx0 + (h + 1) * 512],
                    in0=ctxps[0:64, :], in1=scf[0:64, 512:1024], op=ALU.mult),
                    waits=[w_d4, w_pv])
                gX.read(w_d1)
                gX.read(w_ce)
                gC1.read(w_ce)
                return w_ce

            w_ctx_last = NONE
            pending_pv = [None]

            def flush_pv():
                if pending_pv[0] is not None:
                    pending_pv[0]()
                    pending_pv[0] = None

            for hp in range(NH // 2):
                hA = 2 * hp
                hB = 2 * hp + 1
                oc = hp
                for b in range(nbat):
                    w_smm = {}
                    for par, g in ((0, gA), (1, gB)):
                        rows = slice(par * 64, (par + 1) * 64)
                        w_m = NONE
                        for j in range(2):
                            kc = b * 2 + j
                            w_m = top(lambda e, kc=kc, j=j, oc=oc, rows=rows,
                                      par=par: e.matmul(
                                (psA if par == 0 else psB)[:, j * 512:(j + 1) * 512],
                                ar0[rows, oc * S + kc * 128: oc * S + (kc + 1) * 128],
                                aat[rows, q0 + oc * 512: q0 + (oc + 1) * 512],
                                start=True, stop=True),
                                waits=[w_kdone, qall] +
                                      (g.for_write() if j == 0 else []))
                        g.wrote(w_m)
                        w_smm[par] = w_m
                    flush_pv()
                    if b == 0 and pending_den[0] is not None:
                        w_ctx_last = pending_den[0]()
                        pending_den[0] = None
                    pump(1)
                    w_ready = {}
                    for par, g in ((0, gA), (1, gB)):
                        slot = par * 2 + (b % 2)
                        pbank = psA if par == 0 else psB
                        w_exp = sop(lambda e, pbank=pbank, slot=slot: e.activation(
                            out=aat[:, p0 + slot * 1024: p0 + slot * 1024 + 1024],
                            in_=pbank[:, 0:1024], func=AF.Exp,
                            bias=cf[:, CFC["cbias"]:CFC["cbias"] + 1], scale=0.125),
                            waits=[w_smm[par], w_cb, w_pv_slot[slot]])
                        g.read(w_exp)
                        w_r = w_exp
                        if layer == 1:
                            for j in range(2):
                                kc = b * 2 + j
                                off = kc - (nkc - 4)
                                if off >= 0:
                                    w_r = vop(lambda e, slot=slot, j=j, off=off:
                                              e.tensor_tensor(
                                        out=aat[:, p0 + slot * 1024 + j * 512:
                                                p0 + slot * 1024 + (j + 1) * 512],
                                        in0=aat[:, p0 + slot * 1024 + j * 512:
                                                p0 + slot * 1024 + (j + 1) * 512],
                                        in1=uni[:, 4096 + off * 512:
                                                4096 + (off + 1) * 512],
                                        op=ALU.mult), waits=[w_exp, w_masks])
                        w_ready[par] = w_r

                    def emit_pv(b=b, w_ready=w_ready, nkc=nkc, hA=hA, hB=hB):
                        for par, gX, ctxps, h in ((0, gD, psD[0:65, :], hA),
                                                  (1, gCB, psCB[0:65, :]
                                                   if False else psC[0:65, 0:512],
                                                   hB)):
                            slot = par * 2 + (b % 2)
                            w_pv = NONE
                            for j in range(2):
                                kc = b * 2 + j
                                w_pv = top(lambda e, kc=kc, j=j, h=h, slot=slot,
                                           nkc=nkc, ctxps=ctxps: e.matmul(
                                    ctxps,
                                    ar1[:, kc * 1040 + h * 65:
                                        kc * 1040 + (h + 1) * 65],
                                    aat[:, p0 + slot * 1024 + j * 512:
                                        p0 + slot * 1024 + (j + 1) * 512],
                                    start=(kc == 0), stop=(kc == nkc - 1)),
                                    waits=[w_ready[par], w_vdone] +
                                          (gX.for_write() if kc == 0 else []))
                            w_pv_slot[slot] = w_pv
                            if b == nbat - 1:
                                gX.wrote(w_pv)

                    pending_pv[0] = emit_pv
                flush_pv()

                def den_pair(hA=hA, hB=hB):
                    emit_den(hA, psD[0:65, :], gD)
                    return emit_den(hB, psC[0:65, 0:512], gCB)

                pending_den[0] = den_pair
                pump(1)
            if pending_den[0] is not None:
                w_ctx_last = pending_den[0]()
                pending_den[0] = None
            for oc in range(8):
                wslot, w_w = wb_load(dt["wo" + sfx + "t"][oc], rows=64)
                w_mm = NONE
                for h in range(NH):
                    w_mm = top(lambda e, h=h, slot=wslot: e.matmul(
                        psE[:, 0:512],
                        wsb[0:64, slot * 2048 + h * 128: slot * 2048 + (h + 1) * 128],
                        aat[0:64, ctx0 + h * 512: ctx0 + (h + 1) * 512],
                        start=(h == 0), stop=(h == 15)),
                        waits=[w_w, w_ctx_last] +
                              (gE.for_write() if h == 0 else []))
                wb_free[wslot] = w_mm
                gE.wrote(w_mm)
                xslot = oc % 4
                xsl = xw[:, xslot * 512:(xslot + 1) * 512]
                w_xl = dma_s(xsl, xsrc[:, oc, qt * 512:(qt + 1) * 512],
                             waits=[w_xn, w_xw_store.get((qt - 1, oc), NONE)],
                             ctr=sXW)
                if layer == 0:
                    w_a1 = vop(lambda e, xsl=xsl, oc=oc: e.tensor_scalar(
                        out=xsl, in0=xsl,
                        scalar1=sbias[:, SBC["bo0"] + oc:SBC["bo0"] + oc + 1],
                        scalar2=None, op0=ALU.add), waits=[w_xl, w_params])
                    w_a2 = vop(lambda e, xsl=xsl: e.scalar_tensor_tensor(
                        out=xsl, in0=psE[:, 0:512],
                        scalar=cf[:, CFC["so"]:CFC["so"] + 1], in1=xsl,
                        op0=ALU.mult, op1=ALU.add),
                        waits=[w_mm, w_a1, w_scol_done])
                else:
                    w_a2 = vop(lambda e, xsl=xsl: e.tensor_tensor(
                        out=xsl, in0=xsl, in1=psE[:, 0:512], op=ALU.add),
                        waits=[w_xl, w_mm])
                gE.read(w_a2)
                w_st = dma_s(x_out[:, oc, qt * 512:(qt + 1) * 512], xsl,
                             waits=[w_a2], ctr=sST)
                w_xw_store[(qt, oc)] = w_st
                w_attn_done = w_st
                pump(1)
        return w_attn_done

    def make_moe_filler(w_gates, w_xn, w_ln_done):
        acc = aat_f32
        w1t = dt["w1m"]
        w2t = dt["w2m"]
        state = dict(done=NONE, x07=NONE)
        w_h_unit = {}
        w_attn_ref = [NONE]
        state_g = dict(gtr=NONE, fl=NONE)
        w_gelu_read = [NONE, NONE]
        fst = dict(state_g=state_g, w_gelu_read=w_gelu_read)

        def units():
            w_gelu_read[0] = w_attn_ref[0]
            w_gelu_read[1] = w_attn_ref[0]
            prev_psC = [w_attn_ref[0]]
            w_xupd = {}
            w_accev = {}
            w_gate_done = {}
            seq = [(tt, ei) for tt in range(2) for ei in range(E)]

            def gates_A(tt, ei):
                ge = gts[:, 256 + ei * 16: 256 + (ei + 1) * 16]
                w_tr = top(lambda e, ge=ge: e.transpose(
                    psD[0:16, 0:128], ge, ident[:]),
                    waits=[w_gates, w_id, state_g["gtr"], w_attn_ref[0]] +
                          gD.for_write())
                gD.wrote(w_tr)
                w_gc = sop(lambda e: e.copy(out=scf[0:16, 1536:1664],
                                            in_=psD[0:16, 0:128]),
                           waits=[w_tr])
                gD.read(w_gc)
                state_g["fl"] = dma_g(scf_bf[0:1, 0:2048],
                                      scf[0:16, 1536:1664],
                                      waits=[w_gc], ctr=sFL)

            def gates_B(tt, ei):
                gcol = (ei % 2) * 1024
                w_fl = state_g["fl"]
                w_gb = NONE
                for hf in range(2):
                    pr = psD if hf == 0 else psE
                    gX = gD if hf == 0 else gE
                    wt = top(lambda e, hf=hf, tt=tt, pr=pr: e.matmul(
                        pr[:, 0:512], cbf[0:1, 3:131],
                        scf_bf[0:1, tt * 1024 + hf * 512:
                               tt * 1024 + (hf + 1) * 512],
                        start=True, stop=True),
                        waits=[w_fl, state_g["gtr"]] + gX.for_write())
                    gX.wrote(wt)
                    w_gb = sop(lambda e, hf=hf, pr=pr, gcol=gcol: e.copy(
                        out=uni[:, gcol + hf * 512: gcol + (hf + 1) * 512],
                        in_=pr[:, 0:512]),
                        waits=[wt, w_pswap, w_gelu_read[0], w_gelu_read[1]])
                    gX.read(w_gb)
                state_g["gtr"] = w_gb
                w_gate_done[(tt, ei)] = w_gb

            def x_update(tt, dc):
                xslot = dc % 2
                xsl = gcopy[:, xslot * 1024: (xslot + 1) * 1024]
                w_xl = dma_s(xsl, x_out[:, dc, tt * 1024:(tt + 1) * 1024],
                             waits=[w_attn_ref[0],
                                    w_xupd.get((tt, dc - 2), NONE)],
                             ctr=sXW)
                w_ad = vop(lambda e, xsl=xsl, dc=dc: e.tensor_tensor(
                    out=xsl, in0=xsl, in1=acc[:, dc * 1024:(dc + 1) * 1024],
                    op=ALU.add), waits=[w_xl, w_accev[dc]])
                w_stx = dma_s(x_out[:, dc, tt * 1024:(tt + 1) * 1024], xsl,
                              waits=[w_ad], ctr=sST)
                w_xupd[(tt, dc)] = w_stx
                state["done"] = w_stx
                if (tt, dc) == (0, 7):
                    state["x07"] = w_stx

            gates_A(*seq[0])
            gates_B(*seq[0])
            for k, (tt, ei) in enumerate(seq):
                gcol = (ei % 2) * 1024
                yield ("start", k)
                for hc in range(16):
                    slot = hc % 2
                    wslot, w_w = wa_load(w1t[ei * 16 + hc])
                    pbank = psA if slot == 0 else psB
                    g = gA if slot == 0 else gB
                    harena = ar0 if hc % 2 == 0 else ar1
                    hcol = (hc // 2) * 2048
                    w_mm = NONE
                    for dc in range(8):
                        for half in range(2):
                            w_mm = top(lambda e, dc=dc, half=half, tt=tt,
                                       slot=wslot, pbank=pbank: e.matmul(
                                pbank[:, half * 512:(half + 1) * 512],
                                wsa[:, slot * 1024 + dc * 128:
                                    slot * 1024 + (dc + 1) * 128],
                                xn[:, dc * S + tt * 1024 + half * 512:
                                   dc * S + tt * 1024 + half * 512 + 512],
                                start=(dc == 0), stop=(dc == 7)),
                                waits=[w_w, w_xn] +
                                      (g.for_write()
                                       if (dc == 0 and half == 0) else []))
                    wa_free[wslot] = w_mm
                    g.wrote(w_mm)
                    w_gl = sop(lambda e, pbank=pbank: e.activation(
                        out=uni_f32[:, 1024:2048], in_=pbank[:, 0:1024],
                        func=AF.Gelu),
                        waits=[w_mm, w_gate_done[(tt, ei)], w_gelu_read[0],
                               w_gelu_read[1]])
                    g.read(w_gl)
                    w_hm = vop(lambda e, harena=harena, hcol=hcol, gcol=gcol:
                               e.tensor_tensor(
                        out=harena[:, hcol:hcol + 1024],
                        in0=uni_f32[:, 1024:2048],
                        in1=uni[:, gcol:gcol + 1024], op=ALU.mult),
                        waits=[w_gl])
                    w_gelu_read[slot] = w_hm
                    w_h_unit[(ei, hc)] = w_hm
                    if hc % 2 == 1:
                        yield ("w1", k, hc)
                w2_loads = {}
                for dc in range(8):
                    pr = psC if dc % 2 == 0 else psB
                    if dc == 0:
                        w2_loads[0] = wb_load(w2t[ei, 0])
                        w2_loads[1] = wb_load(w2t[ei, 1])
                    if dc + 2 < 8:
                        w2_loads[dc + 2] = wb_load(w2t[ei, dc + 2])
                    wslot, w_w = w2_loads[dc]
                    w_mm = NONE
                    if dc % 2 == 0:
                        bank_w = [prev_psC[0]]
                    else:
                        bank_w = gB.for_write()
                    for hc in range(16):
                        harena = ar0 if hc % 2 == 0 else ar1
                        hcol = (hc // 2) * 2048
                        for half in range(2):
                            w_mm = top(lambda e, hc=hc, half=half, slot=wslot,
                                       harena=harena, hcol=hcol, pr=pr:
                                       e.matmul(
                                pr[:, half * 512:(half + 1) * 512],
                                wsb[:, slot * 2048 + hc * 128:
                                    slot * 2048 + (hc + 1) * 128],
                                harena[:, hcol + half * 512:
                                       hcol + half * 512 + 512],
                                start=(hc == 0), stop=(hc == 15)),
                                waits=[w_w, w_h_unit[(ei, hc)]] +
                                      (bank_w if hc == 0 and half == 0
                                       else []))
                    wb_free[wslot] = w_mm
                    if dc % 2 == 1:
                        gB.wrote(w_mm)
                    if ei == 0:
                        w_accev[dc] = vop(lambda e, dc=dc, pr=pr:
                                          e.tensor_copy(
                            out=acc[:, dc * 1024:(dc + 1) * 1024],
                            in_=pr[:, 0:1024]),
                            waits=[w_mm, w_gates, w_attn_ref[0],
                                   w_xupd.get((tt - 1, dc), NONE)])
                    else:
                        w_accev[dc] = vop(lambda e, dc=dc, pr=pr:
                                          e.tensor_tensor(
                            out=acc[:, dc * 1024:(dc + 1) * 1024],
                            in0=acc[:, dc * 1024:(dc + 1) * 1024],
                            in1=pr[:, 0:1024], op=ALU.add), waits=[w_mm])
                    if dc % 2 == 0:
                        prev_psC[0] = w_accev[dc]
                    else:
                        gB.read(w_accev[dc])
                    if ei == E - 1:
                        x_update(tt, dc)
                    if dc == 3 and k + 1 < len(seq):
                        gates_A(*seq[k + 1])
                    if dc == 6 and k + 1 < len(seq):
                        gates_B(*seq[k + 1])
                    yield ("w2", k, dc)

        gen = units()

        def pump_fn():
            try:
                next(gen)
                return True
            except StopIteration:
                return False

        def pump_until(label):
            for lab in gen:
                if lab == label:
                    return

        def finish_fn():
            while pump_fn():
                pass
            return state["done"]

        return pump_fn, pump_until, finish_fn, w_attn_ref, fst, state

    def emit_ffn1(w_xn, w_attn_done):
        w1t = dt["w1f"]
        w2t = dt["w2f"]
        w_done = NONE
        w_h8_read = [w_attn_done, w_attn_done]
        w_h8s = [w_attn_done, w_attn_done]
        w_fupd = {}
        psC_rd = [w_attn_done, w_attn_done]
        w_fadd_l = NONE
        for tt in range(2):
            w_h = {}
            for hc in range(32):
                slot = hc % 2
                wslot, w_w = wa_load(w1t[hc])
                pbank = psA if slot == 0 else psB
                g = gA if slot == 0 else gB
                w_mm = NONE
                for dc in range(8):
                    for half in range(2):
                        w_mm = top(lambda e, dc=dc, half=half, tt=tt, slot=wslot,
                                   pbank=pbank: e.matmul(
                            pbank[:, half * 512:(half + 1) * 512],
                            wsa[:, slot * 1024 + dc * 128:
                                slot * 1024 + (dc + 1) * 128],
                            xn[:, dc * S + tt * 1024 + half * 512:
                               dc * S + tt * 1024 + half * 512 + 512],
                            start=(dc == 0), stop=(dc == 7)),
                            waits=[w_w, w_xn] +
                                  (g.for_write() if (dc == 0 and half == 0)
                                   else []))
                wa_free[wslot] = w_mm
                g.wrote(w_mm)
                w_rl = sop(lambda e, pbank=pbank, hc=hc, slot=slot: e.activation(
                    out=scf_f8[:, slot * 1024:(slot + 1) * 1024],
                    in_=pbank[:, 0:1024], func=AF.Relu,
                    bias=b1s[:, hc:hc + 1]),
                    waits=[w_mm, w_params, w_h8s[slot]])
                g.read(w_rl)
                harena = ar0 if hc < 16 else ar1
                hcol = (hc % 16) * 1024
                w_cv = vop(lambda e, harena=harena, hcol=hcol, slot=slot:
                           e.tensor_copy(
                    out=harena[:, hcol:hcol + 1024],
                    in_=scf_f8[:, slot * 1024:(slot + 1) * 1024]),
                    waits=[w_rl])
                w_h8s[slot] = w_cv
                w_h[hc] = w_cv
            for dc in range(8):
                w_q = {}
                for hhalf in range(2):
                    wslot, w_w = wb_load(w2t[dc, :, hhalf * 2048:
                                              (hhalf + 1) * 2048])
                    for half in range(2):
                        w_mm = NONE
                        for hc16 in range(16):
                            hc = hhalf * 16 + hc16
                            harena = ar0 if hc < 16 else ar1
                            hcol = (hc % 16) * 1024
                            w_mm = top(lambda e, hc16=hc16, half=half, slot=wslot,
                                       harena=harena, hcol=hcol, hc=hc,
                                       hhalf=hhalf: e.matmul(
                                psC[:, half * 512:(half + 1) * 512],
                                wsb[:, slot * 2048 + hc16 * 128:
                                    slot * 2048 + (hc16 + 1) * 128],
                                harena[:, hcol + half * 512:
                                       hcol + half * 512 + 512],
                                start=(hc == 0), stop=(hc == 31)),
                                waits=[w_w, w_h[hc]] +
                                      ([psC_rd[half]] if hc16 == 0 and
                                       hhalf == 0 else []))
                        if hhalf == 1:
                            w_q[half] = vop(lambda e, dc=dc, half=half:
                                            e.tensor_scalar(
                                out=scf_f16[:, 2048 + half * 512:
                                            2048 + (half + 1) * 512],
                                in0=psC[:, half * 512:(half + 1) * 512],
                                scalar1=sbias[:, SBC["bqf"] + dc:
                                              SBC["bqf"] + dc + 1],
                                scalar2=None, op0=ALU.add),
                                waits=[w_mm, w_bqf, w_fadd_l])
                            psC_rd[half] = w_q[half]
                    wb_free[wslot] = w_mm
                xslot = dc % 2
                xsl = xw[:, 2048 + xslot * 1024: 2048 + (xslot + 1) * 1024]
                w_xl = dma_s(xsl, x_out[:, dc, tt * 1024:(tt + 1) * 1024],
                             waits=[w_attn_done, w_fupd.get((tt, dc - 2), NONE),
                                    w_fupd.get((tt - 1, dc + 6), NONE)],
                             ctr=sXW)
                w_ad = vop(lambda e, xsl=xsl: e.tensor_tensor(
                    out=xsl, in0=xsl, in1=scf_f16[:, 2048:3072], op=ALU.add),
                    waits=[w_xl, w_q[0], w_q[1]])
                w_fadd_l = w_ad
                w_stx = dma_s(x_out[:, dc, tt * 1024:(tt + 1) * 1024], xsl,
                              waits=[w_ad], ctr=sST)
                w_fupd[(tt, dc)] = w_stx
                w_done = w_stx
        return w_done

    w_gelu_read2 = [NONE]
    w_h8_read2 = [NONE]
    w_fadd = {}
    gA = Guard(); gB = Guard(); gC0 = Guard(); gC1 = Guard()
    gD = Guard(); gE = Guard(); gCB = Guard()

    def dbg_dump(waits):
        d0 = nc.dram_tensor("dbg0", [128, 16384], BF16, kind="ExternalOutput").ap()
        d1 = nc.dram_tensor("dbg1", [128, 16640], BF16, kind="ExternalOutput").ap()
        d2 = nc.dram_tensor("dbg2", [128, 14336], BF16, kind="ExternalOutput").ap()
        d3 = nc.dram_tensor("dbg3", [128, 16384], BF16, kind="ExternalOutput").ap()
        d4 = nc.dram_tensor("dbg4", [128, 2048], F32, kind="ExternalOutput").ap()
        d5 = nc.dram_tensor("dbg5", [128, 7168], F32, kind="ExternalOutput").ap()
        d6 = nc.dram_tensor("dbg6", [128, 6144], BF16, kind="ExternalOutput").ap()
        d7 = nc.dram_tensor("dbg7", [128, 512], F32, kind="ExternalOutput").ap()
        d8 = nc.dram_tensor("dbg8", [128, 2048], F32, kind="ExternalOutput").ap()
        dma_s(d0, ar0[:], waits=waits, ctr=sST)
        dma_s(d1, ar1[:], waits=waits, ctr=sST)
        dma_s(d2, aat[:, 0:14336], waits=waits, ctr=sST)
        dma_s(d3, xn[:], waits=waits, ctr=sST)
        dma_s(d4, scf[:], waits=waits, ctr=sST)
        dma_s(d5, aat_f32[:, 0:7168], waits=waits, ctr=sST)
        dma_s(d6, uni[:], waits=waits, ctr=sST)
        dma_s(d7, gts[:], waits=waits, ctr=sST)
        dma_s(d8, gcopy[:], waits=waits, ctr=sST)

    stopped = False
    w_xn0, w_logits, _ = emit_ln(0, [w_params], w_scol_done)
    w_gates = emit_router(w_logits)
    if stop_after == "ln0":
        dbg_dump([w_xn0, w_gates]); stopped = True
    if not stopped:
        w_kd0, w_km0, w_vd0 = emit_kv(0, w_xn0, w_logits)
        gA.read(w_vd0); gB.read(w_vd0); gC0.read(w_vd0); gC1.read(w_vd0); gCB.read(w_vd0)
        gD.read(w_km0); gE.read(w_km0)
        if stop_after == "kv0":
            dbg_dump([w_kd0, w_km0, w_vd0, w_gates]); stopped = True
    if not stopped:
        moe_pump, moe_until, moe_finish, w_attn_ref, fst, moe_state = \
            make_moe_filler(w_gates, w_xn0, w_xn0)
        w_at0 = emit_attention(0, w_kd0, w_km0, w_vd0, w_xn0, NONE, NONE,
                               filler=None)
        if stop_after == "attn0":
            dbg_dump([w_at0]); stopped = True
    if not stopped:
        w_attn_ref[0] = w_at0
        moe_until(("w1", 15, 15))
        w_cc, w_ss = emit_rope_tables([fst["w_gelu_read"][0],
                                       fst["w_gelu_read"][1],
                                       fst["state_g"]["gtr"]])
        w_moe = moe_finish()
        if stop_after == "moe0":
            dbg_dump([w_moe]); stopped = True
    if not stopped:
        w_masks = w_masks0
        w_xn1, _, _ = emit_ln(1, [w_moe], w_moe)
        w_kd1, w_km1, w_vd1 = emit_kv(1, w_xn1, w_xn1)
        gA.read(w_vd1); gB.read(w_vd1); gC0.read(w_kd1); gC1.read(w_vd1); gCB.read(w_kd1)
        gD.read(w_km1); gE.read(w_km1)
        if stop_after == "kv1":
            dbg_dump([w_kd1, w_km1, w_vd1, w_masks]); stopped = True
    if not stopped:
        w_at1 = emit_attention(1, w_kd1, w_km1, w_vd1, w_xn1, w_masks, w_kd1,
                               filler=None)
        if stop_after == "attn1":
            dbg_dump([w_at1]); stopped = True
    if not stopped:
        w_ffn = emit_ffn1(w_xn1, w_at1)

    sST_final = sST.n
    sch.op("sync", lambda e: e.wait_ge(sST.sem, sST_final), waits=())

    with es:
        with nc.Block() as block:
            sch.emit(block)
    return nc


def _host_layout(inputs):
    def cp(a):
        return np.ascontiguousarray(np.asarray(a, dtype=np.float32))

    def oc_tiles(wT):
        return np.ascontiguousarray(
            wT.reshape(8, 128, 8, 128).transpose(2, 1, 0, 3).reshape(8, 128, 1024))

    perm = np.concatenate([np.concatenate([h * 64 + np.arange(0, 64, 2),
                                           h * 64 + np.arange(1, 64, 2)])
                           for h in range(16)])
    m = {}
    m["wq0"] = oc_tiles(cp(inputs["L0_attn_w_q"]).T)
    m["wk0"] = oc_tiles(cp(inputs["L0_attn_w_k"]).T)
    m["wv0"] = np.ascontiguousarray(
        cp(inputs["L0_attn_w_v"]).T.reshape(8, 128, 1024).transpose(1, 0, 2)
        .reshape(128, 8192))
    m["wo0t"] = np.ascontiguousarray(
        cp(inputs["L0_attn_w_o"]).T.reshape(16, 64, 8, 128).transpose(2, 1, 0, 3)
        .reshape(8, 64, 2048))
    for nm, key in (("bq0", "L0_attn_b_q"), ("bk0", "L0_attn_b_k"),
                    ("bv0", "L0_attn_b_v"), ("bo0", "L0_attn_b_o")):
        m[nm] = cp(inputs[key])
    for nm, key in (("sq0", "L0_attn_s_q"), ("sk0", "L0_attn_s_k"),
                    ("sv0", "L0_attn_s_v"), ("so0", "L0_attn_s_o")):
        m[nm] = cp(inputs[key]).reshape(1, 1)
    m["rw0"] = np.ascontiguousarray(
        cp(inputs["L0_ffn_router_w"]).reshape(8, 128, 8).transpose(1, 0, 2)
        .reshape(128, 64))
    w1 = cp(inputs["L0_ffn_w1"])
    m["w1m"] = np.ascontiguousarray(
        w1.reshape(8, 128, 128, 128).transpose(2, 1, 0, 3).reshape(128, 128, 1024))
    w2 = cp(inputs["L0_ffn_w2"])
    m["w2m"] = np.ascontiguousarray(
        w2.reshape(8, 16, 128, 8, 128).transpose(0, 3, 2, 1, 4)
        .reshape(8, 8, 128, 2048))
    m["wq1"] = oc_tiles(cp(inputs["L1_attn_wq"]).T[:, perm])
    m["wk1"] = oc_tiles(cp(inputs["L1_attn_wk"]).T[:, perm])
    m["wv1"] = np.ascontiguousarray(
        cp(inputs["L1_attn_wv"]).T.reshape(8, 128, 1024).transpose(1, 0, 2)
        .reshape(128, 8192))
    m["wo1t"] = np.ascontiguousarray(
        cp(inputs["L1_attn_wo"]).T.reshape(16, 64, 8, 128).transpose(2, 1, 0, 3)
        .reshape(8, 64, 2048))
    w1f = cp(inputs["L1_ffn_w1"]).T
    m["w1f"] = np.ascontiguousarray(
        w1f.reshape(8, 128, 32, 128).transpose(2, 1, 0, 3).reshape(32, 128, 1024))
    w2f = cp(inputs["L1_ffn_w2"]).T
    m["w2f"] = np.ascontiguousarray(
        w2f.reshape(32, 128, 8, 128).transpose(2, 1, 0, 3).reshape(8, 128, 4096))
    m["b1f"] = cp(inputs["L1_ffn_b1"])
    m["b2f"] = cp(inputs["L1_ffn_b2"])
    return m


def kernel(**inputs):
    nc = build_program()
    shared = _host_layout(inputs)
    x = np.asarray(inputs["L0_attn_x"], dtype=np.float32)
    in_maps = []
    for b in range(8):
        mm = dict(shared)
        mm["x_t"] = np.ascontiguousarray(
            x[b].T.reshape(8, 128, S).transpose(1, 0, 2))
        in_maps.append(mm)
    res = run_bass_kernel_spmd(nc, in_maps, core_ids=list(range(8)))
    out = np.stack([np.asarray(r["xout"]).transpose(1, 0, 2).reshape(D, S).T
                    for r in res.results], axis=0)
    return out


# version 20
# speedup vs baseline: 1.0801x; 1.0041x over previous
import math
from contextlib import ExitStack

import numpy as np

import concourse.bass as bass
import concourse.mybir as mybir
from concourse.bass_utils import run_bass_kernel_spmd

F32 = mybir.dt.float32
BF16 = mybir.dt.bfloat16
FP16 = mybir.dt.float16
FP8 = mybir.dt.float8e4
I32 = mybir.dt.int32
AF = mybir.ActivationFunctionType
ALU = mybir.AluOpType
AX = mybir.AxisListType

S = 2048
D = 1024
NH = 16
TQ = 512
NQT = S // TQ
E = 8
H1 = 4096

ENGINES = ("sync", "gpsimd", "vector", "scalar", "tensor")


class Ctr:
    __slots__ = ("sem", "n", "name")

    def __init__(self, sem, name):
        self.sem = sem
        self.n = 0
        self.name = name


class Sched:
    def __init__(self):
        self.items = {e: [] for e in ENGINES}

    def op(self, eng, fn, waits=(), inc=None, amt=1):
        w = [(c, v) for (c, v) in waits if c is not None and v > 0]
        self.items[eng].append((tuple(w), fn, inc, amt))
        if inc is not None:
            inc.n += amt
            return inc.n
        return None

    def emit(self, block):
        for ename in ENGINES:
            items = self.items[ename]

            def mk(items=items):
                waited = {}

                def body(eng):
                    for waits, fn, inc, amt in items:
                        for c, v in waits:
                            if waited.get(c, 0) >= v:
                                continue
                            eng.wait_ge(c.sem, v)
                            waited[c] = v
                        inst = fn(eng)
                        if inc is not None:
                            inst.then_inc(inc.sem, amt)
                return body

            getattr(block, ename)(mk())


class Guard:

    def __init__(self, init=None):
        self.w = init
        self.readers = []

    def for_write(self):
        w = list(self.readers)
        if self.w is not None:
            w.append(self.w)
        return w

    def wrote(self, m):
        self.w = m
        self.readers = []

    def for_read(self):
        return [self.w] if self.w is not None else []

    def read(self, m):
        self.readers.append(m)


def build_program(stop_after=None, probe_attn=False):
    nc = bass.Bass("TRN2", target_bir_lowering=False, detect_race_conditions=False)
    dt = {}

    def din(name, shape):
        dt[name] = nc.dram_tensor(name, list(shape), F32, kind="ExternalInput").ap()
        return dt[name]

    x_in = din("x_t", [128, 8, S])
    din("wq0", [8, 128, 1024]); din("wk0", [8, 128, 1024]); din("wv0", [128, 8192])
    din("wo0t", [8, 64, 2048])
    din("bq0", [D]); din("bk0", [D]); din("bv0", [D]); din("bo0", [D])
    din("sq0", [1, 1]); din("sk0", [1, 1]); din("sv0", [1, 1]); din("so0", [1, 1])
    din("rw0", [128, 64])
    din("w1m", [128, 128, 1024])
    din("w2m", [E, 8, 128, 2048])
    din("wq1", [8, 128, 1024]); din("wk1", [8, 128, 1024]); din("wv1", [128, 8192])
    din("wo1t", [8, 64, 2048])
    din("w1f", [32, 128, 1024]); din("w2f", [8, 128, 4096])
    din("b1f", [H1]); din("b2f", [D])

    x_out = nc.dram_tensor("xout", [128, 8, S], F32, kind="ExternalOutput").ap()

    es = ExitStack()

    def sbuf(name, shape, dtype):
        return es.enter_context(nc.sbuf_tensor(name, list(shape), dtype))

    def psmk(name, shape):
        return es.enter_context(nc.psum_tensor(name, list(shape), F32))

    xn = sbuf("xn", [128, 8 * S], BF16)
    ar0 = sbuf("ar0", [128, 16384], BF16)
    ar1 = sbuf("ar1", [128, 16640], BF16)
    aat = sbuf("aat", [128, 16384], BF16)
    xw = sbuf("xw", [128, 4096], F32)
    uni = sbuf("uni", [128, 6144], BF16)
    wsa = sbuf("wsa", [128, 3072], BF16)
    wsb = sbuf("wsb", [128, 6144], BF16)
    scf = sbuf("scf", [128, 2048], F32)
    tbf = sbuf("tbf", [128, 1024], BF16)
    gts = sbuf("gts", [128, 512], F32)
    sbias = sbuf("sbias", [128, 64], F32)
    b1s = sbuf("b1s", [128, 32], F32)
    cbf = sbuf("cbf", [128, 256], BF16)
    cf = sbuf("cf", [128, 64], F32)
    onesf = sbuf("onesf", [128, 128], F32)
    ident = sbuf("ident", [128, 128], F32)
    pswp = sbuf("pswp", [128, 128], BF16)
    rwsb = sbuf("rwsb", [128, 64], F32)
    gcopy = sbuf("gcopy", [128, 2048], F32)
    itr = sbuf("itr", [128, 2048], F32)

    psA = psmk("psA", [128, 1024])
    psB = psmk("psB", [128, 1024])
    psC = psmk("psC", [128, 1024])
    psD = psmk("psD", [128, 512])
    psE = psmk("psE", [128, 512])

    aat_f32 = aat[:].bitcast(F32)
    uni_f32 = uni[:].bitcast(F32)
    scf_bf = scf[:].bitcast(BF16)
    scf_i32 = scf[:].bitcast(I32)
    scf_f8 = scf[:].bitcast(FP8)
    scf_f16 = scf[:].bitcast(FP16)

    CFC = dict(iota=0, theta=1, sign=2, eps=3, cbias=4, kmax=5, qmax=6, c2=7,
               sq=8, sk=9, sv=10, so=11, t0=12, t1=13, eps5=15, npi=16)
    SBC = dict(bq0=0, bk0=8, bv0=16, bo0=24, b2f=32, bqf=40)

    def mkctr(name):
        h = es.enter_context(nc.semaphore(name))
        return Ctr(h, name)

    sT = mkctr("sT"); sV = mkctr("sV"); sS = mkctr("sS"); sG = mkctr("sG")
    sLD = mkctr("sLD"); sXW = mkctr("sXW"); sST = mkctr("sST")
    sWA = [mkctr("sWA0"), mkctr("sWA1"), mkctr("sWA2")]
    sWB = [mkctr("sWB0"), mkctr("sWB1"), mkctr("sWB2")]
    sQW = mkctr("sQW")
    sFL = mkctr("sFL")
    sVW = [mkctr("sVW0"), mkctr("sVW1"), mkctr("sVW2"), mkctr("sVW3")]

    sch = Sched()
    NONE = (sT, 0)

    def dma_s(out, in_, waits=(), ctr=None):
        c = ctr or sLD
        return (c, sch.op("sync", lambda e, o=out, i=in_: e.dma_start(out=o, in_=i),
                          waits=waits, inc=c, amt=16))

    def dma_g(out, in_, waits=(), ctr=None):
        c = ctr or sQW
        return (c, sch.op("gpsimd", lambda e, o=out, i=in_: e.dma_start(out=o, in_=i),
                          waits=waits, inc=c, amt=16))

    def top(fn, waits=()):
        return (sT, sch.op("tensor", fn, waits=waits, inc=sT))

    def vop(fn, waits=()):
        return (sV, sch.op("vector", fn, waits=waits, inc=sV))

    def sop(fn, waits=()):
        return (sS, sch.op("scalar", fn, waits=waits, inc=sS))

    def gop(fn, waits=()):
        return (sG, sch.op("gpsimd", fn, waits=waits, inc=sG))

    w_iota1 = gop(lambda e: e.iota(cf[:, 0:1], pattern=[[0, 1]], base=0,
                                   channel_multiplier=1,
                                   allow_small_or_imprecise_dtypes=True))
    w_iota2 = gop(lambda e: e.iota(scf[:, 0:512], pattern=[[-1, 512]], base=0,
                                   channel_multiplier=1,
                                   allow_small_or_imprecise_dtypes=True))
    w_itr = gop(lambda e: e.iota(itr[0:1, 0:2048], pattern=[[1, 2048]], base=0,
                                 channel_multiplier=0,
                                 allow_small_or_imprecise_dtypes=True),
                waits=[w_iota2])
    w_masks0 = NONE
    for off in range(4):
        w_masks0 = vop(lambda e, off=off: e.tensor_single_scalar(
            out=uni[:, 4096 + off * 512: 4096 + (off + 1) * 512],
            in_=scf[:, 0:512], scalar=-128.0 * off, op=ALU.is_le),
            waits=[w_iota2, w_masks0])

    vop(lambda e: e.memset(cbf[:, 0:1], 1.0))
    vop(lambda e: e.memset(cbf[:, 3:131], 1.0))
    vop(lambda e: e.memset(onesf[:], 1.0))
    vop(lambda e: e.memset(cf[:, CFC["eps"]:CFC["eps"] + 1], 1e-30))
    vop(lambda e: e.memset(cf[:, CFC["eps5"]:CFC["eps5"] + 1], 1e-5))
    vop(lambda e: e.memset(cf[:, CFC["npi"]:CFC["npi"] + 1], -math.pi))
    w_id = vop(lambda e: e.tensor_single_scalar(out=ident[:], in_=scf[:, 0:128],
                                                scalar=0.0, op=ALU.is_equal),
               waits=[w_iota2])
    P1 = uni_f32[:, 0:128]
    P2 = uni_f32[:, 128:256]
    selc = cf[:, CFC["t0"]:CFC["t0"] + 1]
    selc2 = cf[:, CFC["t1"]:CFC["t1"] + 1]
    wl = vop(lambda e: e.tensor_single_scalar(out=P1, in_=scf[:, 0:128], scalar=-32.0,
                                              op=ALU.is_equal), waits=[w_iota2])
    wl = vop(lambda e: e.tensor_single_scalar(out=P2, in_=scf[:, 0:128], scalar=32.0,
                                              op=ALU.is_equal), waits=[wl])
    wl = vop(lambda e: e.tensor_single_scalar(out=selc2, in_=cf[:, 0:1], scalar=64.0,
                                              op=ALU.is_ge), waits=[w_iota1, wl])
    wl = vop(lambda e: e.scalar_tensor_tensor(out=selc, in0=selc2, scalar=-64.0,
                                              in1=cf[:, 0:1], op0=ALU.mult,
                                              op1=ALU.add), waits=[wl])
    w_pm64 = wl = vop(lambda e: e.tensor_copy(out=cf[:, CFC["t1"] + 1:CFC["t1"] + 2],
                                              in_=selc), waits=[wl])
    wl = vop(lambda e: e.tensor_single_scalar(out=selc, in_=selc, scalar=32.0,
                                              op=ALU.is_lt), waits=[wl])
    wl = vop(lambda e: e.tensor_scalar(out=P1, in0=P1, scalar1=selc, scalar2=None,
                                       op0=ALU.mult), waits=[wl])
    wl = vop(lambda e: e.tensor_scalar(out=selc2, in0=selc, scalar1=-1.0, scalar2=1.0,
                                       op0=ALU.mult, op1=ALU.add), waits=[wl])
    wl = vop(lambda e: e.tensor_scalar(out=P2, in0=P2, scalar1=selc2, scalar2=None,
                                       op0=ALU.mult), waits=[wl])
    wl = vop(lambda e: e.tensor_tensor(out=P1, in0=P1, in1=P2, op=ALU.add), waits=[wl])
    w_pswap = wl = vop(lambda e: e.tensor_copy(out=pswp[:], in_=P1), waits=[wl])
    thc = cf[:, CFC["theta"]:CFC["theta"] + 1]
    sgc = cf[:, CFC["sign"]:CFC["sign"] + 1]
    pm64c = cf[:, CFC["t1"] + 1:CFC["t1"] + 2]
    wl = vop(lambda e: e.tensor_single_scalar(out=thc, in_=pm64c, scalar=32.0,
                                              op=ALU.is_ge), waits=[w_pm64, wl])
    wl = vop(lambda e: e.scalar_tensor_tensor(out=thc, in0=thc, scalar=-32.0,
                                              in1=pm64c, op0=ALU.mult, op1=ALU.add),
             waits=[wl])
    w_thv = wl = vop(lambda e: e.tensor_single_scalar(
        out=thc, in_=thc, scalar=-math.log(10000.0) / 32.0, op=ALU.mult), waits=[wl])
    w_thc = sop(lambda e: e.activation(out=thc, in_=thc, func=AF.Exp), waits=[w_thv])
    wl = vop(lambda e: e.tensor_single_scalar(out=sgc, in_=pm64c, scalar=32.0,
                                              op=ALU.is_ge),
             waits=[w_pm64, wl, w_thc])
    w_sgc = vop(lambda e: e.tensor_scalar(out=sgc, in0=sgc, scalar1=2.0, scalar2=-1.0,
                                          op0=ALU.mult, op1=ALU.add), waits=[wl])
    vop(lambda e: e.memset(cbf[0:64, 1:2], 1.0))
    vop(lambda e: e.memset(cbf[64:128, 1:2], 0.0))
    vop(lambda e: e.memset(cbf[0:64, 2:3], 0.0))
    w_cbf = vop(lambda e: e.memset(cbf[64:128, 2:3], 1.0))

    for nm, col in (("bq0", SBC["bq0"]), ("bk0", SBC["bk0"]), ("bv0", SBC["bv0"]),
                    ("bo0", SBC["bo0"]), ("b2f", SBC["b2f"])):
        bh = dt[nm].rearrange("(c p) -> c p", p=128)
        for c in range(8):
            dma_s(sbias[:, col + c:col + c + 1], bh[c:c + 1, :])
    b1h = dt["b1f"].rearrange("(c p) -> c p", p=128)
    for c in range(32):
        dma_s(b1s[:, c:c + 1], b1h[c:c + 1, :])
    for nm, key in (("sq0", "sq"), ("sk0", "sk"), ("sv0", "sv"), ("so0", "so")):
        dma_s(cf[0:1, CFC[key]:CFC[key] + 1], dt[nm])
    w_rw = dma_s(rwsb[:], dt["rw0"])
    w_params = (sLD, sLD.n)

    w_onesf = (sV, sV.n)
    prev_read = NONE
    w_scol = {}
    for key in ("sq", "sk", "sv", "so"):
        wt = top(lambda e, k=key: e.matmul(psD[:, 0:1], onesf[0:1, :],
                                           cf[0:1, CFC[k]:CFC[k] + 1],
                                           start=True, stop=True),
                 waits=[w_params, w_onesf, prev_read])
        wsc = sop(lambda e, k=key: e.copy(out=cf[:, CFC[k]:CFC[k] + 1],
                                          in_=psD[:, 0:1]), waits=[wt])
        w_scol[key] = wsc
        prev_read = wsc
    w_scol_done = prev_read

    w_b1 = vop(lambda e: e.tensor_copy(out=scf_f16[:, 2048:2056], in_=sbias[:, 32:40]),
               waits=[w_params])
    w_bqf = vop(lambda e: e.tensor_copy(out=sbias[:, 40:48], in_=scf_f16[:, 2048:2056]),
                waits=[w_b1])

    wa_free = [NONE, NONE, NONE]
    wb_free = [NONE, NONE, NONE]

    def emit_ln(layer, deps, psd_free, deps_tiles=None):
        xsrc = x_in if layer == 0 else x_out
        state = dict(last_xn=NONE, w_le=NONE, w_rmm_last=NONE,
                     w_mmm=NONE, w_vmm=NONE)
        arwin = ar0[:, 0:8192].bitcast(F32)
        bc_free = {0: psd_free, 1: psd_free}
        win_free = {0: NONE, 1: NONE}
        if deps_tiles is None:
            deps_tiles = [list(deps)] * 4
        p2 = {}

        def phase1(tt):
            t0 = tt * 512
            win = xw[:] if tt % 2 == 0 else arwin
            bc = psA if tt % 2 == 0 else psC
            wr = win.rearrange("p (dc t) -> p dc t", t=512)
            w_lo = dma_s(wr[:, 0:4, :], xsrc[:, 0:4, t0:t0 + 512],
                         waits=list(deps_tiles[tt]) + [win_free[tt % 2]],
                         ctr=sXW)
            w_hi = dma_s(wr[:, 4:8, :], xsrc[:, 4:8, t0:t0 + 512],
                         waits=list(deps_tiles[tt]) + [win_free[tt % 2]],
                         ctr=sLD)
            for dc in range(8):
                w_ld = w_lo if dc < 4 else w_hi
                state["w_mmm"] = top(lambda e, dc=dc, win=win: e.matmul(
                    psD[0:1, :], onesf[:, 0:1], win[:, dc * 512:(dc + 1) * 512],
                    start=(dc == 0), stop=(dc == 7)),
                    waits=[w_ld, psd_free, w_onesf])
                w_sq = sop(lambda e, dc=dc, win=win: e.activation(
                    out=tbf[:, 512:1024], in_=win[:, dc * 512:(dc + 1) * 512],
                    func=AF.Square), waits=[w_ld, state["w_vmm"]])
                state["w_vmm"] = top(lambda e, dc=dc: e.matmul(
                    psE[0:1, :], cbf[:, 0:1], tbf[:, 512:1024],
                    start=(dc == 0), stop=(dc == 7)),
                    waits=[w_sq, psd_free, w_cbf])
            m_row = scf[0:1, 0:512]
            v_row = scf[0:1, 512:1024]
            mm_row = scf[0:1, 1024:1536]
            w_m = vop(lambda e: e.tensor_single_scalar(out=m_row, in_=psD[0:1, :],
                                                       scalar=1.0 / D, op=ALU.mult),
                      waits=[state["w_mmm"]])
            w_v1 = vop(lambda e: e.tensor_single_scalar(out=v_row, in_=psE[0:1, :],
                                                        scalar=1.0 / D, op=ALU.mult),
                       waits=[state["w_vmm"]])
            w_v2 = vop(lambda e: e.tensor_tensor(out=mm_row, in0=m_row, in1=m_row,
                                                 op=ALU.mult), waits=[w_m])
            w_v3 = vop(lambda e: e.tensor_tensor(out=v_row, in0=v_row, in1=mm_row,
                                                 op=ALU.subtract), waits=[w_v1, w_v2])
            w_v4 = sop(lambda e: e.activation(out=v_row, in_=v_row, func=AF.Ln,
                                              bias=cf[0:1, CFC["eps5"]:CFC["eps5"] + 1]),
                       waits=[w_v3])
            w_v5 = sop(lambda e: e.activation(out=v_row, in_=v_row, func=AF.Exp,
                                              scale=-0.5), waits=[w_v4])
            w_bm = top(lambda e, bc=bc: e.matmul(
                bc[:, 0:512], onesf[0:1, :], m_row,
                start=True, stop=True),
                waits=[w_m, w_v5, psd_free, bc_free[tt % 2]])
            w_br = top(lambda e, bc=bc: e.matmul(
                bc[:, 512:1024], onesf[0:1, :], v_row,
                start=True, stop=True),
                waits=[w_v5, psd_free, bc_free[tt % 2]])
            p2[tt] = (t0, win, bc, w_bm, w_br)

        def phase2(tt):
            t0, win, bc, w_bm, w_br = p2[tt]
            w_xnf = []
            for dc in range(8):
                tmpf = scf[:, 1536:2048]
                w_t1 = vop(lambda e, dc=dc, win=win, bc=bc: e.tensor_tensor(
                    out=tmpf, in0=win[:, dc * 512:(dc + 1) * 512], in1=bc[:, 0:512],
                    op=ALU.subtract),
                    waits=[w_bm, w_xnf[-1] if w_xnf else NONE, state["w_le"]])
                w_t2 = vop(lambda e, dc=dc, t0=t0, bc=bc: e.tensor_tensor(
                    out=xn[:, dc * S + t0: dc * S + t0 + 512], in0=tmpf,
                    in1=bc[:, 512:1024], op=ALU.mult), waits=[w_br, w_t1])
                state["last_xn"] = w_t2
                if layer == 0:
                    w_t3 = vop(lambda e, dc=dc, bc=bc: e.tensor_tensor(
                        out=aat_f32[:, dc * 512:(dc + 1) * 512], in0=tmpf,
                        in1=bc[:, 512:1024], op=ALU.mult), waits=[w_t1])
                    w_xnf.append(w_t3)
                    state["last_xn"] = w_t3
                else:
                    w_xnf.append(w_t2)
            bc_free[tt % 2] = state["last_xn"]
            win_free[tt % 2] = state["last_xn"]
            if layer == 0:
                w_rmm = NONE
                for qs in range(4):
                    for dc in range(8):
                        w_rmm = top(lambda e, qs=qs, dc=dc: e.matmul(
                            psB[:, qs * 8:(qs + 1) * 8],
                            aat_f32[:, dc * 512 + qs * 128: dc * 512 + qs * 128 + 128],
                            rwsb[:, dc * 8:(dc + 1) * 8],
                            start=(dc == 0), stop=(dc == 7)),
                            waits=[w_xnf[dc], w_rw, state["w_le"], psd_free])
                state["w_le"] = sop(lambda e, tt=tt: e.copy(
                    out=gts[:, tt * 32:(tt + 1) * 32],
                    in_=psB[:, 0:32]), waits=[w_rmm])
                state["w_rmm_last"] = w_rmm

        for tt in range(4):
            phase1(tt)
            if tt > 0:
                phase2(tt - 1)
        phase2(3)
        return state["last_xn"], state["w_le"], state["w_rmm_last"]

    def emit_router(w_logits):
        L = gts[:, 0:128].rearrange("p (c e) -> p c e", e=8)
        m1 = gts[:, 128:144]; m2 = gts[:, 144:160]
        i1 = gts[:, 160:176]; i2 = gts[:, 176:192]
        tmp = gts[:, 192:208]; tmp2 = gts[:, 208:224]
        g1c = gts[:, 224:240]; t3 = gts[:, 240:256]
        wl = vop(lambda e: e.tensor_copy(out=m1, in_=L[:, :, 0]), waits=[w_logits])
        wl = vop(lambda e: e.memset(m2, -1e30), waits=[wl])
        wl = vop(lambda e: e.memset(i1, 0.0), waits=[wl])
        wl = vop(lambda e: e.memset(i2, 0.0), waits=[wl])
        for ei in range(1, 8):
            pe = L[:, :, ei]
            wl = vop(lambda e, pe=pe: e.tensor_tensor(out=tmp, in0=pe, in1=m1,
                                                      op=ALU.is_gt), waits=[wl])
            wl = vop(lambda e, pe=pe: e.tensor_tensor(out=tmp2, in0=pe, in1=m2,
                                                      op=ALU.is_gt), waits=[wl])
            wl = vop(lambda e, ei=ei: e.tensor_scalar(out=t3, in0=i2, scalar1=-1.0,
                                                      scalar2=float(ei), op0=ALU.mult,
                                                      op1=ALU.add), waits=[wl])
            wl = vop(lambda e: e.tensor_tensor(out=t3, in0=t3, in1=tmp2, op=ALU.mult),
                     waits=[wl])
            wl = vop(lambda e: e.tensor_tensor(out=i2, in0=i2, in1=t3, op=ALU.add),
                     waits=[wl])
            wl = vop(lambda e: e.tensor_tensor(out=t3, in0=i1, in1=i2,
                                               op=ALU.subtract), waits=[wl])
            wl = vop(lambda e: e.tensor_tensor(out=t3, in0=t3, in1=tmp, op=ALU.mult),
                     waits=[wl])
            wl = vop(lambda e: e.tensor_tensor(out=i2, in0=i2, in1=t3, op=ALU.add),
                     waits=[wl])
            wl = vop(lambda e, pe=pe: e.tensor_tensor(out=t3, in0=m1, in1=pe,
                                                      op=ALU.min), waits=[wl])
            wl = vop(lambda e: e.tensor_tensor(out=m2, in0=m2, in1=t3, op=ALU.max),
                     waits=[wl])
            wl = vop(lambda e, ei=ei: e.tensor_scalar(out=t3, in0=i1, scalar1=-1.0,
                                                      scalar2=float(ei), op0=ALU.mult,
                                                      op1=ALU.add), waits=[wl])
            wl = vop(lambda e: e.tensor_tensor(out=t3, in0=t3, in1=tmp, op=ALU.mult),
                     waits=[wl])
            wl = vop(lambda e: e.tensor_tensor(out=i1, in0=i1, in1=t3, op=ALU.add),
                     waits=[wl])
            wl = vop(lambda e, pe=pe: e.tensor_tensor(out=m1, in0=m1, in1=pe,
                                                      op=ALU.max), waits=[wl])
        wl = vop(lambda e: e.tensor_tensor(out=tmp, in0=m2, in1=m1, op=ALU.subtract),
                 waits=[wl])
        w_e = sop(lambda e: e.activation(out=tmp, in_=tmp, func=AF.Exp), waits=[wl])
        wl = vop(lambda e: e.tensor_scalar(out=tmp, in0=tmp, scalar1=1.0, scalar2=None,
                                           op0=ALU.add), waits=[w_e, wl])
        wl = vop(lambda e: e.reciprocal(out=g1c, in_=tmp), waits=[wl])
        for ei in range(8):
            ge = gts[:, 256 + ei * 16: 256 + (ei + 1) * 16]
            wl = vop(lambda e, ei=ei: e.tensor_single_scalar(out=tmp2, in_=i1,
                                                             scalar=float(ei),
                                                             op=ALU.is_equal),
                     waits=[wl])
            wl = vop(lambda e, ge=ge: e.tensor_tensor(out=ge, in0=g1c, in1=tmp2,
                                                      op=ALU.mult), waits=[wl])
            wl = vop(lambda e, ei=ei: e.tensor_single_scalar(out=tmp2, in_=i2,
                                                             scalar=float(ei),
                                                             op=ALU.is_equal),
                     waits=[wl])
            wl = vop(lambda e: e.tensor_scalar(out=t3, in0=g1c, scalar1=-1.0,
                                               scalar2=1.0, op0=ALU.mult,
                                               op1=ALU.add), waits=[wl])
            wl = vop(lambda e: e.tensor_tensor(out=tmp2, in0=tmp2, in1=t3,
                                               op=ALU.mult), waits=[wl])
            wl = vop(lambda e, ge=ge: e.tensor_tensor(out=ge, in0=ge, in1=tmp2,
                                                      op=ALU.add), waits=[wl])
        return wl

    wa_rot = [0]
    wb_rot = [0]

    def wb_load(src_ap, rows=128):
        slot = wb_rot[0]
        wb_rot[0] = (slot + 1) % 3
        out = wsb[0:rows, slot * 2048:(slot + 1) * 2048]
        return slot, dma_g(out, src_ap, waits=[wb_free[slot]], ctr=sWB[slot])

    def wa_load(src_ap, extra=()):
        slot = wa_rot[0]
        wa_rot[0] = (slot + 1) % 3
        out = wsa[:, slot * 1024:(slot + 1) * 1024]
        return slot, dma_g(out, src_ap, waits=[wa_free[slot]] + list(extra),
                           ctr=sWA[slot])

    def emit_kv(layer, w_xn, psfree):
        sfx = str(layer)
        krope_last = NONE
        prev_ev = [psfree, psfree]
        kev = {}
        for oc in range(8):
            wslot, w_w = wa_load(dt["wk" + sfx][oc])
            for tt in range(4):
                half = (oc * 4 + tt) % 2
                pr = psB[:, half * 512:(half + 1) * 512]
                w_mm = NONE
                for dc in range(8):
                    w_mm = top(lambda e, dc=dc, tt=tt, pr=pr, slot=wslot: e.matmul(
                        pr, wsa[:, slot * 1024 + dc * 128: slot * 1024 + (dc + 1) * 128],
                        xn[:, dc * S + tt * 512: dc * S + tt * 512 + 512],
                        start=(dc == 0), stop=(dc == 7)),
                        waits=[w_w, w_xn, prev_ev[half]])
                if layer == 0:
                    ev = vop(lambda e, oc=oc, tt=tt, pr=pr: e.tensor_scalar(
                        out=ar0[:, oc * S + tt * 512: oc * S + tt * 512 + 512],
                        in0=pr, scalar1=cf[:, CFC["sk"]:CFC["sk"] + 1],
                        scalar2=sbias[:, SBC["bk0"] + oc:SBC["bk0"] + oc + 1],
                        op0=ALU.mult, op1=ALU.add),
                        waits=[w_mm, w_scol_done, w_params])
                else:
                    ev = sop(lambda e, oc=oc, tt=tt, pr=pr: e.copy(
                        out=ar0[:, oc * S + tt * 512: oc * S + tt * 512 + 512],
                        in_=pr), waits=[w_mm])
                prev_ev[half] = ev
                kev[(oc, tt)] = ev
            wa_free[wslot] = w_mm
        klast = kev[(7, 3)]
        if layer == 1:
            for oc in range(8):
                for tt in range(4):
                    ksl = ar0[:, oc * S + tt * 512: oc * S + tt * 512 + 512]
                    w_sw = top(lambda e, ksl=ksl: e.matmul(
                        psC[:, 0:512], pswp[:], ksl, start=True, stop=True),
                        waits=[kev[(oc, tt)], w_pswap, krope_last, psfree])
                    w_r1 = vop(lambda e, ksl=ksl, tt=tt: e.tensor_tensor(
                        out=tbf[:, 0:512], in0=ksl,
                        in1=uni[:, tt * 512:(tt + 1) * 512], op=ALU.mult),
                        waits=[w_cc_g[0], kev[(oc, tt)], krope_last])
                    w_r2 = vop(lambda e, tt=tt: e.tensor_tensor(
                        out=tbf[:, 512:1024], in0=psC[:, 0:512],
                        in1=uni[:, 2048 + tt * 512: 2048 + (tt + 1) * 512],
                        op=ALU.mult), waits=[w_ss_g[0], w_sw])
                    krope_last = vop(lambda e, ksl=ksl: e.tensor_tensor(
                        out=ksl, in0=tbf[:, 0:512], in1=tbf[:, 512:1024],
                        op=ALU.add), waits=[w_r1, w_r2])
            klast = krope_last
        w_k0 = vop(lambda e: e.memset(cf[0:1, CFC["kmax"]:CFC["kmax"] + 1], 0.0))
        wprev = w_k0
        rd_g = [NONE, NONE]
        sq_g = [NONE, NONE]
        i = 0
        for oc in range(8):
            for tt in range(4):
                th = i % 2
                w_sq = sop(lambda e, oc=oc, tt=tt, th=th: e.activation(
                    out=tbf[:, th * 512:(th + 1) * 512],
                    in_=ar0[:, oc * S + tt * 512: oc * S + tt * 512 + 512],
                    func=AF.Square), waits=[klast, sq_g[th]])
                for par in range(2):
                    bank = (i * 2 + par) % 2
                    pr = psD[0:1, :] if bank == 0 else psE[0:1, :]
                    w_mm = top(lambda e, par=par, th=th, pr=pr: e.matmul(
                        pr, cbf[par * 64:(par + 1) * 64, 1 + par:2 + par],
                        tbf[par * 64:(par + 1) * 64, th * 512:(th + 1) * 512],
                        start=True, stop=True), waits=[w_sq, rd_g[bank], psfree])
                    sq_g[th] = w_mm
                    w_rm = vop(lambda e, pr=pr: e.reduce_max(
                        out=cf[0:1, CFC["t0"]:CFC["t0"] + 1], in_=pr,
                        axis=AX.X), waits=[w_mm, wprev])
                    rd_g[bank] = w_rm
                    wprev = vop(lambda e: e.tensor_tensor(
                        out=cf[0:1, CFC["kmax"]:CFC["kmax"] + 1],
                        in0=cf[0:1, CFC["kmax"]:CFC["kmax"] + 1],
                        in1=cf[0:1, CFC["t0"]:CFC["t0"] + 1], op=ALU.max),
                        waits=[w_rm])
                i += 1
        w_kmax = wprev
        w_o65 = vop(lambda e: e.memset(ar1[:, 64:16640:65], 1.0), waits=[psfree])
        w_wvs = [dma_g(aat[:, i * 2048:(i + 1) * 2048],
                       dt["wv" + sfx][:, i * 2048:(i + 1) * 2048],
                       waits=[psfree], ctr=sVW[i]) for i in range(4)]
        w_bv = [NONE, NONE]
        if layer == 0:
            w_bvr = dma_s(scf[0:1, 0:1024],
                          dt["bv0"].rearrange("(o d) -> o d", o=1), waits=[w_xn])
            prev = NONE
            for ot in range(2):
                wt = top(lambda e, ot=ot: e.matmul(
                    psC[:, 0:512], onesf[0:1, :], scf[0:1, ot * 512:(ot + 1) * 512],
                    start=True, stop=True), waits=[w_bvr, prev, psfree])
                wc = sop(lambda e, ot=ot: e.copy(
                    out=scf[:, 1024 + ot * 512: 1024 + (ot + 1) * 512],
                    in_=psC[:, 0:512]), waits=[wt])
                w_bv[ot] = wc
                prev = wc
        prev_ev = [psfree, psfree]
        vev = NONE
        for ot in range(2):
            for tc in range(16):
                half = (ot * 16 + tc) % 2
                pr = psA[:, half * 512:(half + 1) * 512]
                w_mm = NONE
                for dc in range(8):
                    w_mm = top(lambda e, dc=dc, tc=tc, ot=ot, pr=pr: e.matmul(
                        pr, xn[:, dc * S + tc * 128: dc * S + (tc + 1) * 128],
                        aat[:, dc * 1024 + ot * 512: dc * 1024 + ot * 512 + 512],
                        start=(dc == 0), stop=(dc == 7)),
                        waits=[w_wvs[dc // 2], w_xn, prev_ev[half]])
                vout = ar1[:, tc * 1040: (tc + 1) * 1040].rearrange(
                    "p (h c) -> p h c", c=65)[:, ot * 8:(ot + 1) * 8, 0:64]
                vin = pr.rearrange("p (h c) -> p h c", c=64)
                if layer == 0:
                    vev = vop(lambda e, vout=vout, vin=vin, ot=ot: e.scalar_tensor_tensor(
                        out=vout, in0=vin, scalar=cf[:, CFC["sv"]:CFC["sv"] + 1],
                        in1=scf[:, 1024 + ot * 512: 1024 + (ot + 1) * 512].rearrange(
                            "p (h c) -> p h c", c=64),
                        op0=ALU.mult, op1=ALU.add),
                        waits=[w_mm, w_scol_done, w_bv[ot], w_o65])
                else:
                    vev = vop(lambda e, vout=vout, vin=vin: e.tensor_copy(
                        out=vout, in_=vin), waits=[w_mm, w_o65])
                prev_ev[half] = vev
        return klast, w_kmax, vev

    w_cc_g = [NONE]
    w_ss_g = [NONE]

    def emit_rope_tables(deps):
        wlast = NONE
        bank = [psA[:, 0:512], psA[:, 512:1024], psD[:, 0:512], psE[:, 0:512]]
        bguard = [gA, gA, gD, gE]
        w_bs = []
        for tt in range(4):
            w_b = top(lambda e, tt=tt: e.matmul(
                bank[tt], onesf[0:1, :], itr[0:1, tt * 512:(tt + 1) * 512],
                start=True, stop=True),
                waits=[w_itr] + list(deps) + bguard[tt].for_write())
            bguard[tt].wrote(w_b)
            w_bs.append(w_b)
        for tt in range(4):
            w_b = w_bs[tt]
            w_a = vop(lambda e, tt=tt: e.tensor_scalar(
                out=scf[:, 512:1024], in0=bank[tt],
                scalar1=thc, scalar2=None,
                op0=ALU.mult), waits=[w_b, w_thc, wlast])
            bguard[tt].read(w_a)
            w_s_prev = NONE
            for which in range(2):
                shift = 0.25 if which == 1 else 0.0
                dst = uni[:, (2048 if which == 0 else 0) + tt * 512:][:, 0:512]
                w_n = vop(lambda e, shift=shift: e.tensor_scalar(
                    out=scf[:, 1024:1536], in0=scf[:, 512:1024],
                    scalar1=1.0 / (2 * math.pi), scalar2=shift,
                    op0=ALU.mult, op1=ALU.add), waits=[w_a, wlast, w_s_prev])
                w_k = vop(lambda e: e.tensor_copy(out=scf_i32[:, 1536:2048],
                                                  in_=scf[:, 1024:1536]),
                          waits=[w_n])
                w_kf = vop(lambda e: e.tensor_copy(out=gts[:, 0:512],
                                                   in_=scf_i32[:, 1536:2048]),
                           waits=[w_k])
                w_f = vop(lambda e: e.tensor_tensor(out=scf[:, 1024:1536],
                                                    in0=scf[:, 1024:1536],
                                                    in1=gts[:, 0:512],
                                                    op=ALU.subtract), waits=[w_kf])
                w_w1 = vop(lambda e: e.tensor_single_scalar(
                    out=gts[:, 0:512], in_=scf[:, 1024:1536], scalar=0.5,
                    op=ALU.is_ge), waits=[w_f])
                w_w2 = vop(lambda e: e.tensor_tensor(
                    out=scf[:, 1024:1536], in0=scf[:, 1024:1536],
                    in1=gts[:, 0:512], op=ALU.subtract), waits=[w_w1])
                w_s = sop(lambda e, dst=dst: e.activation(
                    out=dst, in_=scf[:, 1024:1536], func=AF.Sin,
                    scale=2 * math.pi), waits=[w_w2])
                w_s_prev = w_s
                if which == 0:
                    wlast = vop(lambda e, dst=dst: e.tensor_scalar(
                        out=dst, in0=dst, scalar1=sgc, scalar2=None, op0=ALU.mult),
                        waits=[w_s, w_sgc])
                    w_ss_g[0] = wlast
                else:
                    wlast = w_s
                    w_cc_g[0] = wlast
        return w_cc_g[0], w_ss_g[0]

    def emit_attention(layer, w_kdone, w_kmax, w_vdone, w_xn, w_masks, qrope_init,
                       filler=None, probe_attn=probe_attn):
        sfx = str(layer)
        xsrc = x_in if layer == 0 else x_out
        ctx0 = 0
        q0 = 8192
        p0 = 12288
        w_pv_slot = [NONE, NONE, NONE, NONE]
        qrope_last = qrope_init
        w_xw_store = {}
        w_attn_done = NONE

        def pump(n=1):
            if filler is not None:
                for _ in range(n):
                    if not filler():
                        break

        for qt in range(NQT):
            qev = {}
            for oc in range(8):
                wslot, w_w = wa_load(dt["wq" + sfx][oc])
                w_mm = NONE
                for dc in range(8):
                    w_mm = top(lambda e, dc=dc, qt=qt, slot=wslot: e.matmul(
                        psE[:, 0:512],
                        wsa[:, slot * 1024 + dc * 128: slot * 1024 + (dc + 1) * 128],
                        xn[:, dc * S + qt * 512: dc * S + qt * 512 + 512],
                        start=(dc == 0), stop=(dc == 7)),
                        waits=[w_w, w_xn] + (gE.for_write() if dc == 0 else []))
                wa_free[wslot] = w_mm
                gE.wrote(w_mm)
                if layer == 0:
                    ev = vop(lambda e, oc=oc: e.tensor_scalar(
                        out=aat[:, q0 + oc * 512: q0 + (oc + 1) * 512],
                        in0=psE[:, 0:512], scalar1=cf[:, CFC["sq"]:CFC["sq"] + 1],
                        scalar2=sbias[:, SBC["bq0"] + oc:SBC["bq0"] + oc + 1],
                        op0=ALU.mult, op1=ALU.add),
                        waits=[w_mm, w_scol_done, w_params])
                else:
                    ev = sop(lambda e, oc=oc: e.copy(
                        out=aat[:, q0 + oc * 512: q0 + (oc + 1) * 512],
                        in_=psE[:, 0:512]), waits=[w_mm])
                gE.read(ev)
                qev[oc] = ev
                pump(1)
            if layer == 1:
                for oc in range(8):
                    qsl = aat[:, q0 + oc * 512: q0 + (oc + 1) * 512]
                    w_sw = top(lambda e, qsl=qsl: e.matmul(
                        psC[:, 0:512], pswp[:], qsl, start=True, stop=True),
                        waits=[qev[oc], w_pswap, qrope_last] + gCB.for_write())
                    gCB.wrote(w_sw)
                    w_r1 = vop(lambda e, qsl=qsl, qt=qt: e.tensor_tensor(
                        out=tbf[:, 0:512], in0=qsl,
                        in1=uni[:, qt * 512:(qt + 1) * 512], op=ALU.mult),
                        waits=[w_cc_g[0], qev[oc], qrope_last])
                    w_r2 = vop(lambda e, qt=qt: e.tensor_tensor(
                        out=tbf[:, 512:1024], in0=psC[:, 0:512],
                        in1=uni[:, 2048 + qt * 512: 2048 + (qt + 1) * 512],
                        op=ALU.mult), waits=[w_ss_g[0], w_sw])
                    gCB.read(w_r2)
                    qrope_last = vop(lambda e, qsl=qsl: e.tensor_tensor(
                        out=qsl, in0=tbf[:, 0:512], in1=tbf[:, 512:1024],
                        op=ALU.add), waits=[w_r1, w_r2])
                    qev[oc] = qrope_last
            qall = qev[7] if layer == 0 else qrope_last
            w_qm = vop(lambda e: e.memset(cf[0:1, CFC["qmax"]:CFC["qmax"] + 1], 0.0),
                       waits=[qall])
            wprev = w_qm
            rd_g = {0: NONE, 1: NONE}
            sq_g = [NONE, NONE]
            for oc in range(8):
                th = oc % 2
                w_sq = sop(lambda e, oc=oc, th=th: e.activation(
                    out=tbf[:, th * 512:(th + 1) * 512],
                    in_=aat[:, q0 + oc * 512: q0 + (oc + 1) * 512],
                    func=AF.Square), waits=[qall, sq_g[th]])
                for par in range(2):
                    bank = par
                    pr = psD[0:1, :] if bank == 0 else psE[0:1, :]
                    w_mm = top(lambda e, par=par, th=th, pr=pr: e.matmul(
                        pr, cbf[par * 64:(par + 1) * 64, 1 + par:2 + par],
                        tbf[par * 64:(par + 1) * 64, th * 512:(th + 1) * 512],
                        start=True, stop=True),
                        waits=[w_sq, rd_g[bank]] +
                              (gD.for_write() if (oc == 0 and par == 0) else []) +
                              (gE.for_write() if (oc == 0 and par == 1) else []))
                    sq_g[th] = w_mm
                    w_rm = vop(lambda e, pr=pr: e.reduce_max(
                        out=cf[0:1, CFC["t0"]:CFC["t0"] + 1], in_=pr,
                        axis=AX.X), waits=[w_mm, wprev])
                    rd_g[bank] = w_rm
                    wprev = vop(lambda e: e.tensor_tensor(
                        out=cf[0:1, CFC["qmax"]:CFC["qmax"] + 1],
                        in0=cf[0:1, CFC["qmax"]:CFC["qmax"] + 1],
                        in1=cf[0:1, CFC["t0"]:CFC["t0"] + 1], op=ALU.max),
                        waits=[w_rm])
                pump(1)
            w_c1 = vop(lambda e: e.tensor_tensor(
                out=cf[0:1, CFC["c2"]:CFC["c2"] + 1],
                in0=cf[0:1, CFC["qmax"]:CFC["qmax"] + 1],
                in1=cf[0:1, CFC["kmax"]:CFC["kmax"] + 1], op=ALU.mult),
                waits=[wprev, w_kmax])
            w_c2 = sop(lambda e: e.activation(out=cf[0:1, CFC["c2"]:CFC["c2"] + 1],
                                              in_=cf[0:1, CFC["c2"]:CFC["c2"] + 1],
                                              func=AF.Ln,
                                              bias=cf[0:1, CFC["eps"]:CFC["eps"] + 1]),
                       waits=[w_c1])
            w_c3 = sop(lambda e: e.activation(out=cf[0:1, CFC["c2"]:CFC["c2"] + 1],
                                              in_=cf[0:1, CFC["c2"]:CFC["c2"] + 1],
                                              func=AF.Exp, scale=0.5), waits=[w_c2])
            w_c4a = vop(lambda e: e.tensor_scalar(
                out=cf[0:1, CFC["c2"]:CFC["c2"] + 1],
                in0=cf[0:1, CFC["c2"]:CFC["c2"] + 1], scalar1=0.125, scalar2=-40.0,
                op0=ALU.mult, op1=ALU.add), waits=[w_c3])
            w_c4b = vop(lambda e: e.tensor_scalar_max(
                out=cf[0:1, CFC["c2"]:CFC["c2"] + 1],
                in0=cf[0:1, CFC["c2"]:CFC["c2"] + 1], scalar1=0.0), waits=[w_c4a])
            w_c4 = vop(lambda e: e.tensor_single_scalar(
                out=cf[0:1, CFC["c2"]:CFC["c2"] + 1],
                in_=cf[0:1, CFC["c2"]:CFC["c2"] + 1], scalar=-1.0, op=ALU.mult),
                waits=[w_c4b])
            w_c5 = top(lambda e: e.matmul(psD[:, 0:1], onesf[0:1, :],
                                          cf[0:1, CFC["c2"]:CFC["c2"] + 1],
                                          start=True, stop=True),
                       waits=[w_c4, rd_g[0]] + gD.for_write())
            gD.wrote(w_c5)
            w_cb = sop(lambda e: e.copy(out=cf[:, CFC["cbias"]:CFC["cbias"] + 1],
                                        in_=psD[:, 0:1]), waits=[w_c5])
            gD.read(w_cb)
            nkc = 16 if layer == 0 else (qt + 1) * 4
            nbat = nkc // 2
            pending_den = [None]
            psCB = psC[:, 0:512]
            psDEN = psC[:, 512:1024]

            def emit_den(h, ctxps, gX):
                w_pv = gX.w
                w_d1 = sop(lambda e, ctxps=ctxps: e.activation(
                    out=scf[64:65, 0:512], in_=ctxps[64:65, :], func=AF.Ln,
                    bias=cf[64:65, CFC["eps"]:CFC["eps"] + 1]), waits=[w_pv])
                w_d2 = sop(lambda e: e.activation(
                    out=scf[64:65, 0:512], in_=scf[64:65, 0:512], func=AF.Exp,
                    scale=-1.0), waits=[w_d1])
                w_d3 = top(lambda e: e.matmul(psDEN, onesf[64:65, :],
                                              scf[64:65, 0:512],
                                              start=True, stop=True),
                           waits=[w_d2] + gC1.for_write())
                gC1.wrote(w_d3)
                w_d4 = sop(lambda e: e.copy(out=scf[:, 512:1024], in_=psDEN),
                           waits=[w_d3])
                gC1.read(w_d4)
                w_ce = vop(lambda e, h=h, ctxps=ctxps: e.tensor_tensor(
                    out=aat[0:64, ctx0 + h * 512: ctx0 + (h + 1) * 512],
                    in0=ctxps[0:64, :], in1=scf[0:64, 512:1024], op=ALU.mult),
                    waits=[w_d4, w_pv])
                gX.read(w_d1)
                gX.read(w_ce)
                gC1.read(w_ce)
                return w_ce

            w_ctx_last = NONE
            pending_pv = [None]

            def flush_pv():
                if pending_pv[0] is not None:
                    pending_pv[0]()
                    pending_pv[0] = None

            for hp in range(NH // 2):
                hA = 2 * hp
                hB = 2 * hp + 1
                oc = hp
                for b in range(nbat):
                    w_smm = {}
                    for par, g in ((0, gA), (1, gB)):
                        rows = slice(par * 64, (par + 1) * 64)
                        w_m = NONE
                        for j in range(2):
                            kc = b * 2 + j
                            w_m = top(lambda e, kc=kc, j=j, oc=oc, rows=rows,
                                      par=par: e.matmul(
                                (psA if par == 0 else psB)[:, j * 512:(j + 1) * 512],
                                ar0[rows, oc * S + kc * 128: oc * S + (kc + 1) * 128],
                                aat[rows, q0 + oc * 512: q0 + (oc + 1) * 512],
                                start=True, stop=True),
                                waits=[w_kdone, qall] +
                                      (g.for_write() if j == 0 else []))
                        g.wrote(w_m)
                        w_smm[par] = w_m
                    flush_pv()
                    if b == 0 and pending_den[0] is not None:
                        w_ctx_last = pending_den[0]()
                        pending_den[0] = None
                    pump(1)
                    w_ready = {}
                    for par, g in ((0, gA), (1, gB)):
                        slot = par * 2 + (b % 2)
                        pbank = psA if par == 0 else psB
                        w_exp = sop(lambda e, pbank=pbank, slot=slot: e.activation(
                            out=aat[:, p0 + slot * 1024: p0 + slot * 1024 + 1024],
                            in_=pbank[:, 0:1024], func=AF.Exp,
                            bias=cf[:, CFC["cbias"]:CFC["cbias"] + 1], scale=0.125),
                            waits=[w_smm[par], w_cb, w_pv_slot[slot]])
                        g.read(w_exp)
                        w_r = w_exp
                        if layer == 1:
                            for j in range(2):
                                kc = b * 2 + j
                                off = kc - (nkc - 4)
                                if off >= 0:
                                    w_r = vop(lambda e, slot=slot, j=j, off=off:
                                              e.tensor_tensor(
                                        out=aat[:, p0 + slot * 1024 + j * 512:
                                                p0 + slot * 1024 + (j + 1) * 512],
                                        in0=aat[:, p0 + slot * 1024 + j * 512:
                                                p0 + slot * 1024 + (j + 1) * 512],
                                        in1=uni[:, 4096 + off * 512:
                                                4096 + (off + 1) * 512],
                                        op=ALU.mult), waits=[w_exp, w_masks])
                        w_ready[par] = w_r

                    def emit_pv(b=b, w_ready=w_ready, nkc=nkc, hA=hA, hB=hB):
                        for par, gX, ctxps, h in ((0, gD, psD[0:65, :], hA),
                                                  (1, gCB, psCB[0:65, :]
                                                   if False else psC[0:65, 0:512],
                                                   hB)):
                            slot = par * 2 + (b % 2)
                            w_pv = NONE
                            for j in range(2):
                                kc = b * 2 + j
                                w_pv = top(lambda e, kc=kc, j=j, h=h, slot=slot,
                                           nkc=nkc, ctxps=ctxps: e.matmul(
                                    ctxps,
                                    ar1[:, kc * 1040 + h * 65:
                                        kc * 1040 + (h + 1) * 65],
                                    aat[:, p0 + slot * 1024 + j * 512:
                                        p0 + slot * 1024 + (j + 1) * 512],
                                    start=(kc == 0), stop=(kc == nkc - 1)),
                                    waits=[w_ready[par], w_vdone] +
                                          (gX.for_write() if kc == 0 else []))
                            w_pv_slot[slot] = w_pv
                            if b == nbat - 1:
                                gX.wrote(w_pv)

                    pending_pv[0] = emit_pv
                flush_pv()

                def den_pair(hA=hA, hB=hB):
                    emit_den(hA, psD[0:65, :], gD)
                    return emit_den(hB, psC[0:65, 0:512], gCB)

                pending_den[0] = den_pair
                pump(1)
            if pending_den[0] is not None:
                w_ctx_last = pending_den[0]()
                pending_den[0] = None
            for oc in range(8):
                wslot, w_w = wb_load(dt["wo" + sfx + "t"][oc], rows=64)
                w_mm = NONE
                for h in range(NH):
                    w_mm = top(lambda e, h=h, slot=wslot: e.matmul(
                        psE[:, 0:512],
                        wsb[0:64, slot * 2048 + h * 128: slot * 2048 + (h + 1) * 128],
                        aat[0:64, ctx0 + h * 512: ctx0 + (h + 1) * 512],
                        start=(h == 0), stop=(h == 15)),
                        waits=[w_w, w_ctx_last] +
                              (gE.for_write() if h == 0 else []))
                wb_free[wslot] = w_mm
                gE.wrote(w_mm)
                xslot = oc % 4
                xsl = xw[:, xslot * 512:(xslot + 1) * 512]
                w_xl = dma_s(xsl, xsrc[:, oc, qt * 512:(qt + 1) * 512],
                             waits=[w_xn, w_xw_store.get((qt - 1, oc), NONE)],
                             ctr=sXW)
                if layer == 0:
                    w_a1 = vop(lambda e, xsl=xsl, oc=oc: e.tensor_scalar(
                        out=xsl, in0=xsl,
                        scalar1=sbias[:, SBC["bo0"] + oc:SBC["bo0"] + oc + 1],
                        scalar2=None, op0=ALU.add), waits=[w_xl, w_params])
                    w_a2 = vop(lambda e, xsl=xsl: e.scalar_tensor_tensor(
                        out=xsl, in0=psE[:, 0:512],
                        scalar=cf[:, CFC["so"]:CFC["so"] + 1], in1=xsl,
                        op0=ALU.mult, op1=ALU.add),
                        waits=[w_mm, w_a1, w_scol_done])
                else:
                    w_a2 = vop(lambda e, xsl=xsl: e.tensor_tensor(
                        out=xsl, in0=xsl, in1=psE[:, 0:512], op=ALU.add),
                        waits=[w_xl, w_mm])
                gE.read(w_a2)
                w_st = dma_s(x_out[:, oc, qt * 512:(qt + 1) * 512], xsl,
                             waits=[w_a2], ctr=sST)
                w_xw_store[(qt, oc)] = w_st
                w_attn_done = w_st
                pump(1)
        return w_attn_done

    def make_moe_filler(w_gates, w_xn, w_ln_done):
        acc = aat_f32
        w1t = dt["w1m"]
        w2t = dt["w2m"]
        state = dict(done=NONE, x07=NONE)
        w_h_unit = {}
        w_attn_ref = [NONE]
        state_g = dict(gtr=NONE, fl=NONE)
        w_gelu_read = [NONE, NONE]
        fst = dict(state_g=state_g, w_gelu_read=w_gelu_read)

        def units():
            w_gelu_read[0] = w_attn_ref[0]
            w_gelu_read[1] = w_attn_ref[0]
            prev_psC = [w_attn_ref[0]]
            w_xupd = {}
            w_accev = {}
            w_gate_done = {}
            seq = [(tt, ei) for tt in range(2) for ei in range(E)]

            def gates_A(tt, ei):
                ge = gts[:, 256 + ei * 16: 256 + (ei + 1) * 16]
                w_tr = top(lambda e, ge=ge: e.transpose(
                    psD[0:16, 0:128], ge, ident[:]),
                    waits=[w_gates, w_id, state_g["gtr"], w_attn_ref[0]] +
                          gD.for_write())
                gD.wrote(w_tr)
                w_gc = sop(lambda e: e.copy(out=scf[0:16, 1536:1664],
                                            in_=psD[0:16, 0:128]),
                           waits=[w_tr])
                gD.read(w_gc)
                state_g["fl"] = dma_g(scf_bf[0:1, 0:2048],
                                      scf[0:16, 1536:1664],
                                      waits=[w_gc], ctr=sFL)

            def gates_B(tt, ei):
                gcol = (ei % 2) * 1024
                w_fl = state_g["fl"]
                w_gb = NONE
                for hf in range(2):
                    pr = psD if hf == 0 else psE
                    gX = gD if hf == 0 else gE
                    wt = top(lambda e, hf=hf, tt=tt, pr=pr: e.matmul(
                        pr[:, 0:512], cbf[0:1, 3:131],
                        scf_bf[0:1, tt * 1024 + hf * 512:
                               tt * 1024 + (hf + 1) * 512],
                        start=True, stop=True),
                        waits=[w_fl, state_g["gtr"]] + gX.for_write())
                    gX.wrote(wt)
                    w_gb = sop(lambda e, hf=hf, pr=pr, gcol=gcol: e.copy(
                        out=uni[:, gcol + hf * 512: gcol + (hf + 1) * 512],
                        in_=pr[:, 0:512]),
                        waits=[wt, w_pswap, w_gelu_read[0], w_gelu_read[1]])
                    gX.read(w_gb)
                state_g["gtr"] = w_gb
                w_gate_done[(tt, ei)] = w_gb

            def x_update(tt, dc):
                xslot = dc % 2
                xsl = gcopy[:, xslot * 1024: (xslot + 1) * 1024]
                w_xl = dma_s(xsl, x_out[:, dc, tt * 1024:(tt + 1) * 1024],
                             waits=[w_attn_ref[0],
                                    w_xupd.get((tt, dc - 2), NONE)],
                             ctr=sXW)
                w_ad = vop(lambda e, xsl=xsl, dc=dc: e.tensor_tensor(
                    out=xsl, in0=xsl, in1=acc[:, dc * 1024:(dc + 1) * 1024],
                    op=ALU.add), waits=[w_xl, w_accev[dc]])
                w_stx = dma_s(x_out[:, dc, tt * 1024:(tt + 1) * 1024], xsl,
                              waits=[w_ad], ctr=sST)
                w_xupd[(tt, dc)] = w_stx
                state["done"] = w_stx
                if (tt, dc) == (0, 7):
                    state["x07"] = w_stx

            gates_A(*seq[0])
            gates_B(*seq[0])
            for k, (tt, ei) in enumerate(seq):
                gcol = (ei % 2) * 1024
                yield ("start", k)
                for hc in range(16):
                    slot = hc % 2
                    wslot, w_w = wa_load(w1t[ei * 16 + hc])
                    pbank = psA if slot == 0 else psB
                    g = gA if slot == 0 else gB
                    harena = ar0 if hc % 2 == 0 else ar1
                    hcol = (hc // 2) * 2048
                    w_mm = NONE
                    for dc in range(8):
                        for half in range(2):
                            w_mm = top(lambda e, dc=dc, half=half, tt=tt,
                                       slot=wslot, pbank=pbank: e.matmul(
                                pbank[:, half * 512:(half + 1) * 512],
                                wsa[:, slot * 1024 + dc * 128:
                                    slot * 1024 + (dc + 1) * 128],
                                xn[:, dc * S + tt * 1024 + half * 512:
                                   dc * S + tt * 1024 + half * 512 + 512],
                                start=(dc == 0), stop=(dc == 7)),
                                waits=[w_w, w_xn] +
                                      (g.for_write()
                                       if (dc == 0 and half == 0) else []))
                    wa_free[wslot] = w_mm
                    g.wrote(w_mm)
                    w_gl = sop(lambda e, pbank=pbank: e.activation(
                        out=uni_f32[:, 1024:2048], in_=pbank[:, 0:1024],
                        func=AF.Gelu),
                        waits=[w_mm, w_gate_done[(tt, ei)], w_gelu_read[0],
                               w_gelu_read[1]])
                    g.read(w_gl)
                    w_hm = vop(lambda e, harena=harena, hcol=hcol, gcol=gcol:
                               e.tensor_tensor(
                        out=harena[:, hcol:hcol + 1024],
                        in0=uni_f32[:, 1024:2048],
                        in1=uni[:, gcol:gcol + 1024], op=ALU.mult),
                        waits=[w_gl])
                    w_gelu_read[slot] = w_hm
                    w_h_unit[(ei, hc)] = w_hm
                    if hc % 2 == 1:
                        yield ("w1", k, hc)
                w2_loads = {}
                for dc in range(8):
                    pr = psC if dc % 2 == 0 else psB
                    if dc == 0:
                        w2_loads[0] = wb_load(w2t[ei, 0])
                        w2_loads[1] = wb_load(w2t[ei, 1])
                    if dc + 2 < 8:
                        w2_loads[dc + 2] = wb_load(w2t[ei, dc + 2])
                    wslot, w_w = w2_loads[dc]
                    w_mm = NONE
                    if dc % 2 == 0:
                        bank_w = [prev_psC[0]]
                    else:
                        bank_w = gB.for_write()
                    for hc in range(16):
                        harena = ar0 if hc % 2 == 0 else ar1
                        hcol = (hc // 2) * 2048
                        for half in range(2):
                            w_mm = top(lambda e, hc=hc, half=half, slot=wslot,
                                       harena=harena, hcol=hcol, pr=pr:
                                       e.matmul(
                                pr[:, half * 512:(half + 1) * 512],
                                wsb[:, slot * 2048 + hc * 128:
                                    slot * 2048 + (hc + 1) * 128],
                                harena[:, hcol + half * 512:
                                       hcol + half * 512 + 512],
                                start=(hc == 0), stop=(hc == 15)),
                                waits=[w_w, w_h_unit[(ei, hc)]] +
                                      (bank_w if hc == 0 and half == 0
                                       else []))
                    wb_free[wslot] = w_mm
                    if dc % 2 == 1:
                        gB.wrote(w_mm)
                    if ei == 0:
                        w_accev[dc] = vop(lambda e, dc=dc, pr=pr:
                                          e.tensor_copy(
                            out=acc[:, dc * 1024:(dc + 1) * 1024],
                            in_=pr[:, 0:1024]),
                            waits=[w_mm, w_gates, w_attn_ref[0],
                                   w_xupd.get((tt - 1, dc), NONE)])
                    else:
                        w_accev[dc] = vop(lambda e, dc=dc, pr=pr:
                                          e.tensor_tensor(
                            out=acc[:, dc * 1024:(dc + 1) * 1024],
                            in0=acc[:, dc * 1024:(dc + 1) * 1024],
                            in1=pr[:, 0:1024], op=ALU.add), waits=[w_mm])
                    if dc % 2 == 0:
                        prev_psC[0] = w_accev[dc]
                    else:
                        gB.read(w_accev[dc])
                    if ei == E - 1:
                        x_update(tt, dc)
                    if dc == 3 and k + 1 < len(seq):
                        gates_A(*seq[k + 1])
                    if dc == 6 and k + 1 < len(seq):
                        gates_B(*seq[k + 1])
                    yield ("w2", k, dc)

        gen = units()

        def pump_fn():
            try:
                next(gen)
                return True
            except StopIteration:
                return False

        def pump_until(label):
            for lab in gen:
                if lab == label:
                    return

        def finish_fn():
            while pump_fn():
                pass
            return state["done"]

        return pump_fn, pump_until, finish_fn, w_attn_ref, fst, state

    def emit_ffn1(w_xn, w_attn_done):
        w1t = dt["w1f"]
        w2t = dt["w2f"]
        w_done = NONE
        w_h8_read = [w_attn_done, w_attn_done]
        w_h8s = [w_attn_done, w_attn_done]
        w_fupd = {}
        psC_rd = [w_attn_done, w_attn_done]
        w_fadd_l = NONE
        for tt in range(2):
            w_h = {}
            for hc in range(32):
                slot = hc % 2
                wslot, w_w = wa_load(w1t[hc])
                pbank = psA if slot == 0 else psB
                g = gA if slot == 0 else gB
                w_mm = NONE
                for dc in range(8):
                    for half in range(2):
                        w_mm = top(lambda e, dc=dc, half=half, tt=tt, slot=wslot,
                                   pbank=pbank: e.matmul(
                            pbank[:, half * 512:(half + 1) * 512],
                            wsa[:, slot * 1024 + dc * 128:
                                slot * 1024 + (dc + 1) * 128],
                            xn[:, dc * S + tt * 1024 + half * 512:
                               dc * S + tt * 1024 + half * 512 + 512],
                            start=(dc == 0), stop=(dc == 7)),
                            waits=[w_w, w_xn] +
                                  (g.for_write() if (dc == 0 and half == 0)
                                   else []))
                wa_free[wslot] = w_mm
                g.wrote(w_mm)
                w_rl = sop(lambda e, pbank=pbank, hc=hc, slot=slot: e.activation(
                    out=scf_f8[:, slot * 1024:(slot + 1) * 1024],
                    in_=pbank[:, 0:1024], func=AF.Relu,
                    bias=b1s[:, hc:hc + 1]),
                    waits=[w_mm, w_params, w_h8s[slot]])
                g.read(w_rl)
                harena = ar0 if hc < 16 else ar1
                hcol = (hc % 16) * 1024
                w_cv = vop(lambda e, harena=harena, hcol=hcol, slot=slot:
                           e.tensor_copy(
                    out=harena[:, hcol:hcol + 1024],
                    in_=scf_f8[:, slot * 1024:(slot + 1) * 1024]),
                    waits=[w_rl])
                w_h8s[slot] = w_cv
                w_h[hc] = w_cv
            for dc in range(8):
                w_q = {}
                for hhalf in range(2):
                    wslot, w_w = wb_load(w2t[dc, :, hhalf * 2048:
                                              (hhalf + 1) * 2048])
                    for half in range(2):
                        w_mm = NONE
                        for hc16 in range(16):
                            hc = hhalf * 16 + hc16
                            harena = ar0 if hc < 16 else ar1
                            hcol = (hc % 16) * 1024
                            w_mm = top(lambda e, hc16=hc16, half=half, slot=wslot,
                                       harena=harena, hcol=hcol, hc=hc,
                                       hhalf=hhalf: e.matmul(
                                psC[:, half * 512:(half + 1) * 512],
                                wsb[:, slot * 2048 + hc16 * 128:
                                    slot * 2048 + (hc16 + 1) * 128],
                                harena[:, hcol + half * 512:
                                       hcol + half * 512 + 512],
                                start=(hc == 0), stop=(hc == 31)),
                                waits=[w_w, w_h[hc]] +
                                      ([psC_rd[half]] if hc16 == 0 and
                                       hhalf == 0 else []))
                        if hhalf == 1:
                            w_q[half] = vop(lambda e, dc=dc, half=half:
                                            e.tensor_scalar(
                                out=scf_f16[:, 2048 + half * 512:
                                            2048 + (half + 1) * 512],
                                in0=psC[:, half * 512:(half + 1) * 512],
                                scalar1=sbias[:, SBC["bqf"] + dc:
                                              SBC["bqf"] + dc + 1],
                                scalar2=None, op0=ALU.add),
                                waits=[w_mm, w_bqf, w_fadd_l])
                            psC_rd[half] = w_q[half]
                    wb_free[wslot] = w_mm
                xslot = dc % 2
                xsl = xw[:, 2048 + xslot * 1024: 2048 + (xslot + 1) * 1024]
                w_xl = dma_s(xsl, x_out[:, dc, tt * 1024:(tt + 1) * 1024],
                             waits=[w_attn_done, w_fupd.get((tt, dc - 2), NONE),
                                    w_fupd.get((tt - 1, dc + 6), NONE)],
                             ctr=sXW)
                w_ad = vop(lambda e, xsl=xsl: e.tensor_tensor(
                    out=xsl, in0=xsl, in1=scf_f16[:, 2048:3072], op=ALU.add),
                    waits=[w_xl, w_q[0], w_q[1]])
                w_fadd_l = w_ad
                w_stx = dma_s(x_out[:, dc, tt * 1024:(tt + 1) * 1024], xsl,
                              waits=[w_ad], ctr=sST)
                w_fupd[(tt, dc)] = w_stx
                w_done = w_stx
        return w_done

    w_gelu_read2 = [NONE]
    w_h8_read2 = [NONE]
    w_fadd = {}
    gA = Guard(); gB = Guard(); gC0 = Guard(); gC1 = Guard()
    gD = Guard(); gE = Guard(); gCB = Guard()

    def dbg_dump(waits):
        d0 = nc.dram_tensor("dbg0", [128, 16384], BF16, kind="ExternalOutput").ap()
        d1 = nc.dram_tensor("dbg1", [128, 16640], BF16, kind="ExternalOutput").ap()
        d2 = nc.dram_tensor("dbg2", [128, 14336], BF16, kind="ExternalOutput").ap()
        d3 = nc.dram_tensor("dbg3", [128, 16384], BF16, kind="ExternalOutput").ap()
        d4 = nc.dram_tensor("dbg4", [128, 2048], F32, kind="ExternalOutput").ap()
        d5 = nc.dram_tensor("dbg5", [128, 7168], F32, kind="ExternalOutput").ap()
        d6 = nc.dram_tensor("dbg6", [128, 6144], BF16, kind="ExternalOutput").ap()
        d7 = nc.dram_tensor("dbg7", [128, 512], F32, kind="ExternalOutput").ap()
        d8 = nc.dram_tensor("dbg8", [128, 2048], F32, kind="ExternalOutput").ap()
        dma_s(d0, ar0[:], waits=waits, ctr=sST)
        dma_s(d1, ar1[:], waits=waits, ctr=sST)
        dma_s(d2, aat[:, 0:14336], waits=waits, ctr=sST)
        dma_s(d3, xn[:], waits=waits, ctr=sST)
        dma_s(d4, scf[:], waits=waits, ctr=sST)
        dma_s(d5, aat_f32[:, 0:7168], waits=waits, ctr=sST)
        dma_s(d6, uni[:], waits=waits, ctr=sST)
        dma_s(d7, gts[:], waits=waits, ctr=sST)
        dma_s(d8, gcopy[:], waits=waits, ctr=sST)

    stopped = False
    w_xn0, w_logits, _ = emit_ln(0, [w_params], w_scol_done)
    w_gates = emit_router(w_logits)
    if stop_after == "ln0":
        dbg_dump([w_xn0, w_gates]); stopped = True
    if not stopped:
        w_kd0, w_km0, w_vd0 = emit_kv(0, w_xn0, w_logits)
        gA.read(w_vd0); gB.read(w_vd0); gC0.read(w_vd0); gC1.read(w_vd0); gCB.read(w_vd0)
        gD.read(w_km0); gE.read(w_km0)
        if stop_after == "kv0":
            dbg_dump([w_kd0, w_km0, w_vd0, w_gates]); stopped = True
    if not stopped:
        moe_pump, moe_until, moe_finish, w_attn_ref, fst, moe_state = \
            make_moe_filler(w_gates, w_xn0, w_xn0)
        w_at0 = emit_attention(0, w_kd0, w_km0, w_vd0, w_xn0, NONE, NONE,
                               filler=None)
        if stop_after == "attn0":
            dbg_dump([w_at0]); stopped = True
    if not stopped:
        w_attn_ref[0] = w_at0
        moe_until(("w1", 15, 15))
        w_cc, w_ss = emit_rope_tables([fst["w_gelu_read"][0],
                                       fst["w_gelu_read"][1],
                                       fst["state_g"]["gtr"]])
        w_moe = moe_finish()
        if stop_after == "moe0":
            dbg_dump([w_moe]); stopped = True
    if not stopped:
        w_masks = w_masks0
        w_xn1, _, _ = emit_ln(1, [w_moe], w_moe)
        w_kd1, w_km1, w_vd1 = emit_kv(1, w_xn1, w_xn1)
        gA.read(w_vd1); gB.read(w_vd1); gC0.read(w_kd1); gC1.read(w_vd1); gCB.read(w_kd1)
        gD.read(w_km1); gE.read(w_km1)
        if stop_after == "kv1":
            dbg_dump([w_kd1, w_km1, w_vd1, w_masks]); stopped = True
    if not stopped:
        w_at1 = emit_attention(1, w_kd1, w_km1, w_vd1, w_xn1, w_masks, w_kd1,
                               filler=None)
        if stop_after == "attn1":
            dbg_dump([w_at1]); stopped = True
    if not stopped:
        w_ffn = emit_ffn1(w_xn1, w_at1)

    sST_final = sST.n
    sch.op("sync", lambda e: e.wait_ge(sST.sem, sST_final), waits=())

    with es:
        with nc.Block() as block:
            sch.emit(block)
    return nc


def _host_layout(inputs):
    def cp(a):
        return np.ascontiguousarray(np.asarray(a, dtype=np.float32))

    def oc_tiles(wT):
        return np.ascontiguousarray(
            wT.reshape(8, 128, 8, 128).transpose(2, 1, 0, 3).reshape(8, 128, 1024))

    perm = np.concatenate([np.concatenate([h * 64 + np.arange(0, 64, 2),
                                           h * 64 + np.arange(1, 64, 2)])
                           for h in range(16)])
    m = {}
    m["wq0"] = oc_tiles(cp(inputs["L0_attn_w_q"]).T)
    m["wk0"] = oc_tiles(cp(inputs["L0_attn_w_k"]).T)
    m["wv0"] = np.ascontiguousarray(
        cp(inputs["L0_attn_w_v"]).T.reshape(8, 128, 1024).transpose(1, 0, 2)
        .reshape(128, 8192))
    m["wo0t"] = np.ascontiguousarray(
        cp(inputs["L0_attn_w_o"]).T.reshape(16, 64, 8, 128).transpose(2, 1, 0, 3)
        .reshape(8, 64, 2048))
    for nm, key in (("bq0", "L0_attn_b_q"), ("bk0", "L0_attn_b_k"),
                    ("bv0", "L0_attn_b_v"), ("bo0", "L0_attn_b_o")):
        m[nm] = cp(inputs[key])
    for nm, key in (("sq0", "L0_attn_s_q"), ("sk0", "L0_attn_s_k"),
                    ("sv0", "L0_attn_s_v"), ("so0", "L0_attn_s_o")):
        m[nm] = cp(inputs[key]).reshape(1, 1)
    m["rw0"] = np.ascontiguousarray(
        cp(inputs["L0_ffn_router_w"]).reshape(8, 128, 8).transpose(1, 0, 2)
        .reshape(128, 64))
    w1 = cp(inputs["L0_ffn_w1"])
    m["w1m"] = np.ascontiguousarray(
        w1.reshape(8, 128, 128, 128).transpose(2, 1, 0, 3).reshape(128, 128, 1024))
    w2 = cp(inputs["L0_ffn_w2"])
    m["w2m"] = np.ascontiguousarray(
        w2.reshape(8, 16, 128, 8, 128).transpose(0, 3, 2, 1, 4)
        .reshape(8, 8, 128, 2048))
    m["wq1"] = oc_tiles(cp(inputs["L1_attn_wq"]).T[:, perm])
    m["wk1"] = oc_tiles(cp(inputs["L1_attn_wk"]).T[:, perm])
    m["wv1"] = np.ascontiguousarray(
        cp(inputs["L1_attn_wv"]).T.reshape(8, 128, 1024).transpose(1, 0, 2)
        .reshape(128, 8192))
    m["wo1t"] = np.ascontiguousarray(
        cp(inputs["L1_attn_wo"]).T.reshape(16, 64, 8, 128).transpose(2, 1, 0, 3)
        .reshape(8, 64, 2048))
    w1f = cp(inputs["L1_ffn_w1"]).T
    m["w1f"] = np.ascontiguousarray(
        w1f.reshape(8, 128, 32, 128).transpose(2, 1, 0, 3).reshape(32, 128, 1024))
    w2f = cp(inputs["L1_ffn_w2"]).T
    m["w2f"] = np.ascontiguousarray(
        w2f.reshape(32, 128, 8, 128).transpose(2, 1, 0, 3).reshape(8, 128, 4096))
    m["b1f"] = cp(inputs["L1_ffn_b1"])
    m["b2f"] = cp(inputs["L1_ffn_b2"])
    return m


def kernel(**inputs):
    nc = build_program()
    shared = _host_layout(inputs)
    x = np.asarray(inputs["L0_attn_x"], dtype=np.float32)
    in_maps = []
    for b in range(8):
        mm = dict(shared)
        mm["x_t"] = np.ascontiguousarray(
            x[b].T.reshape(8, 128, S).transpose(1, 0, 2))
        in_maps.append(mm)
    res = run_bass_kernel_spmd(nc, in_maps, core_ids=list(range(8)))
    out = np.stack([np.asarray(r["xout"]).transpose(1, 0, 2).reshape(D, S).T
                    for r in res.results], axis=0)
    return out
